# version 06
# baseline (speedup 1.0000x reference)
import numpy as np
import ml_dtypes
from contextlib import ExitStack

import concourse.bass as bass
import concourse.mybir as mybir
from concourse.bass_utils import run_bass_kernel_spmd

F32 = mybir.dt.float32
BF16 = mybir.dt.bfloat16
F16 = mybir.dt.float16
I16 = mybir.dt.int16
I32 = mybir.dt.int32
AF = mybir.ActivationFunctionType
ALU = mybir.AluOpType

L = 2048
H = 2048
E = 4096
NST = 16
TR = 128
IFF = 5632
OGU = 11264
NET = 32
NKT = 16
NLT = 16

DEBUG_TAPS = ()

_c = np.arange(OGU if False else 11264)
_half = _c // 5632
_r = _c % 5632
_SGU_PERM = _half * 5632 + 8 * (_r % 704) + (_r // 704)


class Sem:
    def __init__(self, stack, nc, name):
        self.h = stack.enter_context(nc.semaphore(name))
        self.n = 0

    def inc(self, k=1):
        self.n += k
        return self.n


class Q:
    def __init__(self):
        self.sync = []
        self.tensor = []
        self.vector = []
        self.scalar = []
        self.gpsimd = []


def emit_block(nc, q):
    with nc.Block() as block:
        if q.sync:
            block.sync(lambda e, ops=tuple(q.sync): [f(e) for f in ops])
        if q.tensor:
            block.tensor(lambda e, ops=tuple(q.tensor): [f(e) for f in ops])
        if q.vector:
            block.vector(lambda e, ops=tuple(q.vector): [f(e) for f in ops])
        if q.scalar:
            block.scalar(lambda e, ops=tuple(q.scalar): [f(e) for f in ops])
        if q.gpsimd:
            block.gpsimd(lambda e, ops=tuple(q.gpsimd): [f(e) for f in ops])


def W(sem, v):
    assert v <= sem.n, f"wait beyond planned count on {sem.h}: {v} > {sem.n}"
    return lambda e, h=sem.h, v=v: e.wait_ge(h, v)


def ln_stats(q, sA, sV, src_f, scr, c1, eps):
    q.scalar.append(lambda e: e.activation(out=scr[:, :], in_=src_f(), func=AF.Identity,
                                           accum_out=c1[:, 0:1]))
    q.scalar.append(lambda e: e.activation(out=scr[:, :], in_=src_f(), func=AF.Square,
                                           accum_out=c1[:, 1:2]).then_inc(sA.h, 1))
    va = sA.inc()
    q.vector.append(W(sA, va))
    q.vector.append(lambda e: e.tensor_scalar_mul(c1[:, 2:3], c1[:, 0:1], 1.0 / H))
    q.vector.append(lambda e: e.tensor_scalar_mul(c1[:, 3:4], c1[:, 1:2], 1.0 / H).then_inc(sV.h, 1))
    g1 = sV.inc()
    q.vector.append(W(sV, g1))
    q.vector.append(lambda e: e.tensor_mul(c1[:, 4:5], c1[:, 2:3], c1[:, 2:3]).then_inc(sV.h, 1))
    g2 = sV.inc()
    q.vector.append(W(sV, g2))
    q.vector.append(lambda e: e.tensor_scalar(
        out=c1[:, 5:6], in0=c1[:, 3:4], scalar1=c1[:, 4:5], scalar2=float(eps),
        op0=ALU.subtract, op1=ALU.add).then_inc(sV.h, 1))
    vv = sV.inc()
    q.scalar.append(W(sV, vv))
    q.scalar.append(lambda e: e.activation(out=c1[:, 6:7], in_=c1[:, 5:6], func=AF.Ln).then_inc(sA.h, 1))
    ga = sA.inc()
    q.scalar.append(W(sA, ga))
    q.scalar.append(lambda e: e.activation(out=c1[:, 7:8], in_=c1[:, 6:7], func=AF.Exp,
                                           scale=-0.5).then_inc(sA.h, 1))
    return sA.inc()


def dequant_ops(q, sV, nib, ta, tb, tc, ts_, wpl, sfull_ap, src, kpos, rows, width):
    r = rows
    q.vector.append(lambda e: e.tensor_scalar(
        out=nib[0:r, 0:width], in0=src, scalar1=int(4 * kpos), scalar2=int(7),
        op0=ALU.logical_shift_right, op1=ALU.bitwise_and))
    q.vector.append(lambda e: e.tensor_scalar(
        out=ta[0:r, 0:width], in0=nib[0:r, 0:width], scalar1=int(2), scalar2=int(0x3F00),
        op0=ALU.min, op1=ALU.mult))
    q.vector.append(lambda e: e.tensor_scalar(
        out=tb[0:r, 0:width], in0=nib[0:r, 0:width], scalar1=int(0x80), scalar2=int(0x3E80),
        op0=ALU.mult, op1=ALU.add))
    q.vector.append(lambda e: e.tensor_scalar(
        out=tc[0:r, 0:width], in0=nib[0:r, 0:width], scalar1=int(0x40), scalar2=int(0x3F00),
        op0=ALU.mult, op1=ALU.add))
    q.vector.append(lambda e: e.tensor_tensor(
        out=ta[0:r, 0:width], in0=ta[0:r, 0:width], in1=tb[0:r, 0:width], op=ALU.min))
    q.vector.append(lambda e: e.tensor_tensor(
        out=ta[0:r, 0:width], in0=ta[0:r, 0:width], in1=tc[0:r, 0:width], op=ALU.min))
    q.vector.append(lambda e: e.tensor_scalar(
        out=ts_[0:r, 0:width], in0=src, scalar1=int(12 - 4 * kpos), scalar2=int(-0x8000),
        op0=ALU.logical_shift_left, op1=ALU.bitwise_and))
    q.vector.append(lambda e: e.tensor_tensor(
        out=ta[0:r, 0:width], in0=ta[0:r, 0:width], in1=ts_[0:r, 0:width], op=ALU.bitwise_or))
    q.vector.append(lambda e: e.tensor_mul(
        wpl[0:r, 0:width], ta[0:r, 0:width].bitcast(BF16), sfull_ap).then_inc(sV.h, 1))
    return sV.inc()




def make_deq_gu_feeder(nc, q, ph, sems, d_pgu, d_sgu_bf, d_plgu):
    sL, sGd, sSt, sV = sems
    pk32 = ph.enter_context(nc.sbuf_tensor("t_pk32", [128, OGU], I32))
    nib = ph.enter_context(nc.sbuf_tensor("t_nib", [128, 704], I16))
    ta = ph.enter_context(nc.sbuf_tensor("t_ta6", [128, 704], I16))
    tb = ph.enter_context(nc.sbuf_tensor("t_tb6", [128, 704], I16))
    tc = ph.enter_context(nc.sbuf_tensor("t_tc6", [128, 704], I16))
    ts_ = ph.enter_context(nc.sbuf_tensor("t_ts6", [128, 704], I16))
    sfull = ph.enter_context(nc.sbuf_tensor("t_sf6", [128, 704], BF16))
    wpl = ph.enter_context(nc.sbuf_tensor("t_wpl", [128, 704], BF16))
    st = {"i": 0, "prev_spill": 0, "loaded_jb": -1}
    order = [(jb, half, kp, kk) for jb in range(2) for half in range(2)
             for kp in range(8) for kk in range(8)]

    def feed(n):
        for _ in range(n):
            if st["i"] >= len(order):
                return
            jb, half, kp, kk = order[st["i"]]
            st["i"] += 1
            if jb != st["loaded_jb"]:
                if jb > 0:
                    q.sync.append(W(sSt, sSt.n))
                q.sync.append(lambda e, jb=jb: e.dma_start(
                    out=pk32[:, :], in_=d_pgu[jb * 128:(jb + 1) * 128, :]).then_inc(sL.h, 16))
                sL.n += 16
                st["vl"] = sL.n
                st["loaded_jb"] = jb
            if kk == 0:
                if st["i"] > 1:
                    q.gpsimd.append(W(sV, sV.n))
                for g in range(8):
                    q.gpsimd.append(lambda e, jb=jb, half=half, kp=kp, g=g: e.dma_start(
                        out=sfull[g * 16:(g + 1) * 16, :],
                        in_=bass.AP(d_sgu_bf.tensor, (jb * 8 + g) * OGU + half * IFF + kp * 704,
                                    [[0, 16], [1, 704]])).then_inc(sGd.h, 16))
                    sGd.n += 16
                st["vsg"] = sGd.n
                q.gpsimd.append(W(sGd, st["vsg"]))
            h16 = kk // 4
            kpos = kk % 4
            pk16 = pk32[:, :].bitcast(I16)
            base = 2 * (half * IFF + kp) + h16
            src = pk16[:, base: base + 16 * 703 + 1: 16]
            q.vector.append(W(sL, st["vl"]))
            q.vector.append(W(sGd, st["vsg"]))
            if st["prev_spill"]:
                q.vector.append(W(sSt, st["prev_spill"]))
            dequant_ops(q, sV, nib, ta, tb, tc, ts_, wpl, sfull[:, :], src, kpos, 128, 704)
            vw = sV.n
            pid = kk * 2 + jb
            q.sync.append(W(sV, vw))
            q.sync.append(lambda e, pid=pid, half=half, kp=kp: e.dma_start(
                out=d_plgu[pid, :, half * IFF + kp * 704: half * IFF + (kp + 1) * 704],
                in_=wpl[:, :]).then_inc(sSt.h, 16))
            sSt.n += 16
            st["prev_spill"] = sSt.n

    def remaining():
        return len(order) - st["i"]
    feed.remaining = remaining
    return feed


def make_deq_dn_feeder(nc, q, ph, sems, d_pdn, d_sdn_bf, d_pldn):
    sL, sGd, sSt, sV = sems
    pk32 = ph.enter_context(nc.sbuf_tensor("t_pk32d", [128, 6 * H], I32))
    nib = ph.enter_context(nc.sbuf_tensor("t_nibd", [128, H], I16))
    ta = ph.enter_context(nc.sbuf_tensor("t_tad", [128, H], I16))
    tb = ph.enter_context(nc.sbuf_tensor("t_tbd", [128, H], I16))
    tc = ph.enter_context(nc.sbuf_tensor("t_tcd", [128, H], I16))
    ts_ = ph.enter_context(nc.sbuf_tensor("t_tsd", [128, H], I16))
    sfull = ph.enter_context(nc.sbuf_tensor("t_sfd", [128, 6 * H], BF16))
    wpl = ph.enter_context(nc.sbuf_tensor("t_wpld", [128, H], BF16))
    st = {"i": 0, "prev_spill": 0, "init": False}
    order = [(kp, pt_) for kp in range(8) for pt_ in range(6)]

    def feed(n):
        if not st["init"]:
            st["init"] = True
            q.sync.append(lambda e: e.dma_start(
                out=pk32[:, 0:5 * H].rearrange("p (t h) -> p t h", t=5),
                in_=d_pdn[0:640, :].rearrange("(t p) h -> p t h", p=128)).then_inc(sL.h, 16))
            sL.n += 16
            q.sync.append(lambda e: e.dma_start(
                out=pk32[0:64, 5 * H:6 * H], in_=d_pdn[640:704, :]).then_inc(sL.h, 16))
            sL.n += 16
            st["vl"] = sL.n
            for pt_ in range(6):
                ng = 8 if pt_ < 5 else 4
                for g in range(ng):
                    q.gpsimd.append(lambda e, pt_=pt_, g=g: e.dma_start(
                        out=sfull[g * 16:(g + 1) * 16, pt_ * H:(pt_ + 1) * H],
                        in_=bass.AP(d_sdn_bf.tensor, (pt_ * 8 + g) * H, [[0, 16], [1, H]])).then_inc(sGd.h, 16))
                    sGd.n += 16
            q.gpsimd.append(W(sGd, sGd.n))
            st["vg"] = sGd.n
        for _ in range(n):
            if st["i"] >= len(order):
                return
            kp, pt_ = order[st["i"]]
            st["i"] += 1
            rows = 128 if pt_ < 5 else 64
            pk16 = pk32[:, :].bitcast(I16)
            h16 = kp // 4
            kpos = kp % 4
            src = pk16[0:rows, 2 * pt_ * H + h16: 2 * pt_ * H + h16 + 2 * (H - 1) + 1: 2]
            q.vector.append(W(sL, st["vl"]))
            q.vector.append(W(sGd, st["vg"]))
            if st["prev_spill"]:
                q.vector.append(W(sSt, st["prev_spill"]))
            dequant_ops(q, sV, nib, ta, tb, tc, ts_, wpl,
                        sfull[0:rows, pt_ * H:(pt_ + 1) * H], src, kpos, rows, H)
            vw = sV.n
            q.sync.append(W(sV, vw))
            q.sync.append(lambda e, kp=kp, pt_=pt_, rows=rows: e.dma_start(
                out=d_pldn[kp * 704 + pt_ * 128: kp * 704 + pt_ * 128 + rows, :],
                in_=wpl[0:rows, :]).then_inc(sSt.h, 16))
            sSt.n += 16
            st["prev_spill"] = sSt.n

    def remaining():
        return len(order) - st["i"]
    feed.remaining = remaining
    return feed


PHASE_ORDER = ["ph0", "ph0b", "ph1", "ph2", "ph3", "ph4",
               "ph5", "ph6a", "ph6b", "ph6c", "ph7"]


def dequant_ops_col(q, sV, nib, ta, tb, tc, ts_, wpl, sfull_ap, src, shc, rows, width):
    r = rows
    q.vector.append(lambda e: e.tensor_single_scalar(
        out=nib[0:r, 0:width], in_=src, scalar=shc[0:r, 0:1],
        op=ALU.logical_shift_right))
    q.vector.append(lambda e: e.tensor_single_scalar(
        out=nib[0:r, 0:width], in_=nib[0:r, 0:width], scalar=int(7),
        op=ALU.bitwise_and))
    q.vector.append(lambda e: e.tensor_scalar(
        out=ta[0:r, 0:width], in0=nib[0:r, 0:width], scalar1=int(2), scalar2=int(0x3F00),
        op0=ALU.min, op1=ALU.mult))
    q.vector.append(lambda e: e.tensor_scalar(
        out=tb[0:r, 0:width], in0=nib[0:r, 0:width], scalar1=int(0x80), scalar2=int(0x3E80),
        op0=ALU.mult, op1=ALU.add))
    q.vector.append(lambda e: e.tensor_scalar(
        out=tc[0:r, 0:width], in0=nib[0:r, 0:width], scalar1=int(0x40), scalar2=int(0x3F00),
        op0=ALU.mult, op1=ALU.add))
    q.vector.append(lambda e: e.tensor_tensor(
        out=ta[0:r, 0:width], in0=ta[0:r, 0:width], in1=tb[0:r, 0:width], op=ALU.min))
    q.vector.append(lambda e: e.tensor_tensor(
        out=ta[0:r, 0:width], in0=ta[0:r, 0:width], in1=tc[0:r, 0:width], op=ALU.min))
    q.vector.append(lambda e: e.tensor_single_scalar(
        out=ts_[0:r, 0:width], in_=src, scalar=shc[0:r, 1:2],
        op=ALU.logical_shift_left))
    q.vector.append(lambda e: e.tensor_single_scalar(
        out=ts_[0:r, 0:width], in_=ts_[0:r, 0:width], scalar=int(-0x8000),
        op=ALU.bitwise_and))
    q.vector.append(lambda e: e.tensor_tensor(
        out=ta[0:r, 0:width], in0=ta[0:r, 0:width], in1=ts_[0:r, 0:width], op=ALU.bitwise_or))
    q.vector.append(lambda e: e.tensor_mul(
        wpl[0:r, 0:width], ta[0:r, 0:width].bitcast(BF16), sfull_ap).then_inc(sV.h, 1))
    return sV.inc()


def build_program(taps=(), upto="ph7"):
    _lim = PHASE_ORDER.index(upto)

    def _skip(name):
        return PHASE_ORDER.index(name) > _lim

    nc = bass.Bass("TRN2", target_bir_lowering=False, detect_race_conditions=False)
    open_stacks = []

    def _stop():
        for st_ in reversed(open_stacks):
            st_.close()
        return nc

    d_x = nc.dram_tensor("x", [L, H], F32, kind="ExternalInput").ap()
    d_win = nc.dram_tensor("w_in", [H, 2 * E], F32, kind="ExternalInput").ap()
    d_ck = nc.dram_tensor("conv_k", [E, 4], F32, kind="ExternalInput").ap()
    d_cb = nc.dram_tensor("conv_b", [128, NET], F32, kind="ExternalInput").ap()
    d_wx = nc.dram_tensor("w_x", [E, 160], F32, kind="ExternalInput").ap()
    d_wdt = nc.dram_tensor("w_dt", [TR, E], F32, kind="ExternalInput").ap()
    d_bdt = nc.dram_tensor("b_dt", [128, NET], F32, kind="ExternalInput").ap()
    d_wout = nc.dram_tensor("w_out", [E, H], F32, kind="ExternalInput").ap()
    d_alog = nc.dram_tensor("a_log", [E, NST], F32, kind="ExternalInput").ap()
    d_D = nc.dram_tensor("dvec", [128, NET], F32, kind="ExternalInput").ap()
    d_pguh = nc.dram_tensor("pgu_h", [256, OGU], I16, kind="ExternalInput").ap()
    d_sgu = nc.dram_tensor("s_gu", [16, OGU], F16, kind="ExternalInput").ap()
    d_pdnh = nc.dram_tensor("pdn_h", [704, H], I16, kind="ExternalInput").ap()
    d_dqsh = nc.dram_tensor("dqsh", [128, 2], I32, kind="ExternalInput").ap()
    d_sdn = nc.dram_tensor("s_dn", [44, H], F16, kind="ExternalInput").ap()
    d_id = nc.dram_tensor("ident", [128, 128], F32, kind="ExternalInput").ap()

    d_u = nc.dram_tensor("sc_u", [E, L], BF16, kind="Internal").ap()
    d_gate = nc.dram_tensor("sc_gate", [E, L], BF16, kind="Internal").ap()
    d_bcbf = nc.dram_tensor("sc_bc", [32, L], BF16, kind="Internal").ap()
    d_yh0 = nc.dram_tensor("sc_yh0", [E, L], BF16, kind="Internal").ap()
    d_yg = nc.dram_tensor("sc_yg", [E, L], BF16, kind="Internal").ap()
    d_attn = nc.dram_tensor("sc_attn", [L, H], F32, kind="Internal").ap()
    d_x1 = nc.dram_tensor("sc_x1", [L, H], F32, kind="Internal").ap()
    d_sgu_bf = nc.dram_tensor("sc_sgu", [16, OGU], BF16, kind="Internal").ap()
    d_sdn_bf = nc.dram_tensor("sc_sdn", [44, H], BF16, kind="Internal").ap()
    d_plgu = nc.dram_tensor("sc_plgu", [16, 128, OGU], BF16, kind="Internal").ap()
    d_plgu_loc = nc.dram_tensor("sc_plgu_loc", [256, OGU], BF16, kind="Internal").ap()
    d_pldn_loc = nc.dram_tensor("sc_pldn_loc", [704, H], BF16, kind="Internal").ap()
    d_hact = nc.dram_tensor("sc_hact", [IFF, L], BF16, kind="Internal").ap()
    d_pldn = nc.dram_tensor("sc_pldn", [IFF, H], BF16, kind="Internal").ap()
    d_ffn = nc.dram_tensor("sc_ffn", [L, H], F32, kind="Internal").ap()

    d_out = nc.dram_tensor("out", [L, H], F32, kind="ExternalOutput").ap()
    d_taps = {}
    for t in taps:
        shp = {"xlnT": [128, NKT * L], "u": [E, L], "gate": [E, L], "dtT": [TR, L],
               "bc": [32, L], "yg": [E, L], "attn": [L, H], "x1": [L, H],
               "hact": [IFF, L], "ffn": [L, H], "plgu": [16 * 128, OGU],
               "xtp": [128, 16 * L]}[t]
        dt = F32 if t in ("attn", "x1", "ffn") else BF16
        d_taps[t] = nc.dram_tensor("tap_" + t, shp, dt, kind="ExternalOutput").ap()

    stack = ExitStack()
    with stack:
        sL = Sem(stack, nc, "sL")
        sGd = Sem(stack, nc, "sGd")
        sSt = Sem(stack, nc, "sSt")
        sP = Sem(stack, nc, "sP")
        sA = Sem(stack, nc, "sA")
        sV = Sem(stack, nc, "sV")
        sA2 = Sem(stack, nc, "sA2")
        sV2 = Sem(stack, nc, "sV2")
        sCC = Sem(stack, nc, "sCC")

        sb_id = stack.enter_context(nc.sbuf_tensor("c_id", [128, 128], BF16))
        sb_aneg = stack.enter_context(nc.sbuf_tensor("c_aneg", [128, NET * NST], F32))
        sb_D = stack.enter_context(nc.sbuf_tensor("c_D", [128, NET], F32))
        sb_cb = stack.enter_context(nc.sbuf_tensor("c_cb", [128, NET], F32))
        sb_bdt = stack.enter_context(nc.sbuf_tensor("c_bdt", [128, NET], F32))
        sb_ck = stack.enter_context(nc.sbuf_tensor("c_ck", [128, NET * 4], F32))
        sb_wdt = stack.enter_context(nc.sbuf_tensor("c_wdt", [128, E], BF16))
        sb_wx = stack.enter_context(nc.sbuf_tensor("c_wx", [128, NET * 160], BF16))
        sb_dtT = stack.enter_context(nc.sbuf_tensor("p_dtT", [128, L], BF16))

        q = Q()
        ph0 = ExitStack()
        sb_alog = ph0.enter_context(nc.sbuf_tensor("t_alog", [128, NET * NST], F32))

        def _ph0_sync(e):
            e.dma_start(out=sb_alog[:, :].rearrange("p (t n) -> p t n", n=NST), in_=d_alog.rearrange("(t p) n -> p t n", p=128)).then_inc(sL.h, 16)
            e.dma_start(out=sb_D[:, :], in_=d_D).then_inc(sL.h, 16)
            e.dma_start(out=sb_cb[:, :], in_=d_cb).then_inc(sL.h, 16)
            e.dma_start(out=sb_bdt[:, :], in_=d_bdt).then_inc(sL.h, 16)
            e.dma_start(out=sb_ck[:, :].rearrange("p (t k) -> p t k", k=4), in_=d_ck.rearrange("(t p) k -> p t k", p=128)).then_inc(sL.h, 16)
        sL.n += 5 * 16
        q.sync.append(_ph0_sync)

        def _ph0_gp(e):
            e.dma_start(out=sb_id[:, :], in_=d_id).then_inc(sGd.h, 16)
            e.dma_start(out=sb_wdt[:, :], in_=d_wdt).then_inc(sGd.h, 16)
            e.dma_start(out=sb_wx[:, :].rearrange("p (t d) -> p t d", d=160), in_=d_wx.rearrange("(t p) d -> p t d", p=128)).then_inc(sGd.h, 16)
            e.dma_start(out=d_sgu_bf, in_=d_sgu).then_inc(sGd.h, 16)
            e.dma_start(out=d_sdn_bf, in_=d_sdn).then_inc(sGd.h, 16)
            e.wait_ge(sGd.h, 80)
        sGd.n += 5 * 16
        q.gpsimd.append(_ph0_gp)

        q.scalar.append(W(sL, sL.n))
        q.scalar.append(lambda e: e.activation(out=sb_aneg[:, :], in_=sb_alog[:, :],
                                               func=AF.Exp).then_inc(sA.h, 1))
        va = sA.inc()
        q.vector.append(W(sA, va))
        q.vector.append(lambda e: e.tensor_scalar_mul(sb_aneg[:, :], sb_aneg[:, :], -1.0).then_inc(sV.h, 1))
        sV.inc()
        emit_block(nc, q)
        ph0.close()


        q = Q()
        with ExitStack() as ph:
            pkg = ph.enter_context(nc.sbuf_tensor("t_pkg", [128, OGU], I16))
            pkd = ph.enter_context(nc.sbuf_tensor("t_pkd", [128, 6 * H], I16))
            shc = ph.enter_context(nc.sbuf_tensor("t_shc", [128, 2], I32))
            nib = ph.enter_context(nc.sbuf_tensor("t_nibq", [128, H], I16))
            ta = ph.enter_context(nc.sbuf_tensor("t_taq", [128, H], I16))
            tb = ph.enter_context(nc.sbuf_tensor("t_tbq", [128, H], I16))
            tc = ph.enter_context(nc.sbuf_tensor("t_tcq", [128, H], I16))
            ts_ = ph.enter_context(nc.sbuf_tensor("t_tsq", [128, H], I16))
            sfull = ph.enter_context(nc.sbuf_tensor("t_sfq", [128, 6 * H], BF16))
            wpl = ph.enter_context(nc.sbuf_tensor("t_wplq", [128, H], BF16))

            q.sync.append(lambda e: e.dma_start(out=shc[:, :], in_=d_dqsh).then_inc(sL.h, 16))
            sL.n += 16
            prev_spill = 0
            for jb in range(2):
                if jb > 0:
                    q.sync.append(W(sSt, sSt.n))
                q.sync.append(lambda e, jb=jb: e.dma_start(
                    out=pkg[:, :], in_=d_pguh[jb * 128:(jb + 1) * 128, :]).then_inc(sL.h, 16))
                sL.n += 16
                vl = sL.n
                for half in range(2):
                    for kp in range(8):
                        if not (jb == 0 and half == 0 and kp == 0):
                            q.gpsimd.append(W(sV, sV.n))
                        for g in range(8):
                            q.gpsimd.append(lambda e, jb=jb, half=half, kp=kp, g=g: e.dma_start(
                                out=sfull[g * 16:(g + 1) * 16, 0:704],
                                in_=bass.AP(d_sgu_bf.tensor, (jb * 8 + g) * OGU + half * IFF + kp * 704,
                                            [[0, 16], [1, 704]])).then_inc(sGd.h, 16))
                            sGd.n += 16
                        vsg = sGd.n
                        q.gpsimd.append(W(sGd, vsg))
                        src = pkg[:, half * IFF + kp * 704: half * IFF + (kp + 1) * 704]
                        q.vector.append(W(sL, vl))
                        q.vector.append(W(sGd, vsg))
                        if prev_spill:
                            q.vector.append(W(sSt, prev_spill))
                        dequant_ops_col(q, sV, nib, ta, tb, tc, ts_, wpl,
                                        sfull[:, 0:704], src, shc, 128, 704)
                        vw = sV.n
                        q.sync.append(W(sV, vw))
                        q.sync.append(lambda e, jb=jb, half=half, kp=kp: e.dma_start(
                            out=d_plgu_loc[jb * 128:(jb + 1) * 128,
                                           half * IFF + kp * 704: half * IFF + (kp + 1) * 704],
                            in_=wpl[:, 0:704]).then_inc(sSt.h, 16))
                        sSt.n += 16
                        prev_spill = sSt.n
            q.sync.append(lambda e: e.dma_start(
                out=pkd[:, 0:5 * H].rearrange("p (t h) -> p t h", t=5),
                in_=d_pdnh[0:640, :].rearrange("(t p) h -> p t h", p=128)).then_inc(sL.h, 16))
            sL.n += 16
            q.sync.append(lambda e: e.dma_start(
                out=pkd[0:64, 5 * H:6 * H], in_=d_pdnh[640:704, :]).then_inc(sL.h, 16))
            sL.n += 16
            vld = sL.n
            for pt_ in range(6):
                ng = 8 if pt_ < 5 else 4
                q.gpsimd.append(W(sV, sV.n))
                for g in range(ng):
                    q.gpsimd.append(lambda e, pt_=pt_, g=g: e.dma_start(
                        out=sfull[g * 16:(g + 1) * 16, pt_ * H:(pt_ + 1) * H],
                        in_=bass.AP(d_sdn_bf.tensor, (pt_ * 8 + g) * H, [[0, 16], [1, H]])).then_inc(sGd.h, 16))
                    sGd.n += 16
                vsg = sGd.n
                q.gpsimd.append(W(sGd, vsg))
                rows = 128 if pt_ < 5 else 64
                src = pkd[0:rows, pt_ * H:(pt_ + 1) * H]
                q.vector.append(W(sL, vld))
                q.vector.append(W(sGd, vsg))
                if prev_spill:
                    q.vector.append(W(sSt, prev_spill))
                dequant_ops_col(q, sV, nib, ta, tb, tc, ts_, wpl,
                                sfull[0:rows, pt_ * H:(pt_ + 1) * H], src, shc, rows, H)
                vw = sV.n
                q.sync.append(W(sV, vw))
                q.sync.append(lambda e, pt_=pt_, rows=rows: e.dma_start(
                    out=d_pldn_loc[pt_ * 128: pt_ * 128 + rows, :],
                    in_=wpl[0:rows, :]).then_inc(sSt.h, 16))
                sSt.n += 16
                prev_spill = sSt.n
            q.gpsimd.append(W(sSt, sSt.n))
            q.gpsimd.append(lambda e: e.collective_compute(
                "AllGather", ALU.bypass, replica_groups=[list(range(8))],
                ins=[d_plgu_loc.opt()], outs=[d_plgu.rearrange("pid p c -> (pid p) c").opt()]).then_inc(sCC.h))
            sCC.n += 1
            q.gpsimd.append(lambda e: e.collective_compute(
                "AllGather", ALU.bypass, replica_groups=[list(range(8))],
                ins=[d_pldn_loc.opt()], outs=[d_pldn.opt()]).then_inc(sCC.h))
            sCC.n += 1
            q.gpsimd.append(W(sCC, 2))
            q.sync.append(W(sSt, sSt.n))
            if "plgu" in taps:
                q.sync.append(W(sCC, 1))
                q.sync.append(lambda e: e.dma_start(out=d_taps["plgu"], in_=d_plgu.rearrange("pid p c -> (pid p) c")).then_inc(sSt.h, 16))
                sSt.n += 16
                q.sync.append(W(sSt, sSt.n))
            emit_block(nc, q)

        if _skip("ph0b"):
            return _stop()
        es1 = ExitStack(); open_stacks.append(es1)
        sb_xlnT = es1.enter_context(nc.sbuf_tensor("p_xlnT", [128, NKT, L], BF16))
        q = Q()
        with ExitStack() as ph:
            xt = [ph.enter_context(nc.sbuf_tensor(f"t_x{i}", [128, H], F32)) for i in range(2)]
            xln = [ph.enter_context(nc.sbuf_tensor(f"t_xl{i}", [128, H], BF16)) for i in range(2)]
            scr = ph.enter_context(nc.sbuf_tensor("t_scr", [128, H], F32))
            c1 = ph.enter_context(nc.sbuf_tensor("t_c1", [128, 8], F32))
            pt = [ph.enter_context(nc.psum_tensor(f"t_pt{i}", [128, 1024], BF16)) for i in range(2)]

            norm_done = {}
            evac_done = {}
            gi = 0
            for lt in range(NLT):
                pr = lt % 2
                if lt >= 2:
                    q.sync.append(W(sV, norm_done[lt - 2]))
                q.sync.append(lambda e, lt=lt, pr=pr: e.dma_start(
                    out=xt[pr][:, :], in_=d_x[lt * 128:(lt + 1) * 128, :]).then_inc(sL.h, 16))
                sL.n += 16
                vl = sL.n

                q.scalar.append(W(sL, vl))
                va = ln_stats(q, sA, sV, (lambda pr=pr: xt[pr][:, :]), scr, c1, 1e-5)
                q.vector.append(W(sA, va))
                if lt >= 2:
                    q.vector.append(W(sP, sP.n))
                q.vector.append(lambda e, pr=pr: e.tensor_scalar(
                    out=xln[pr][:, :], in0=xt[pr][:, :], scalar1=c1[:, 2:3], scalar2=c1[:, 7:8],
                    op0=ALU.subtract, op1=ALU.mult).then_inc(sV.h, 1))
                norm_done[lt] = sV.inc()

                import os as _os
                if _os.environ.get("K_PH0B_CUT") == "1":
                    continue
                _noevac = _os.environ.get("K_PH0B_NOEVAC") == "1"
                for g in range(4):
                    gpr = gi % 2
                    q.tensor.append(W(sV, norm_done[lt]))
                    if gi >= 2:
                        q.tensor.append(W(sV2, evac_done[gi - 2]))
                    for hb in range(4):
                        ht = g * 4 + hb
                        q.tensor.append(lambda e, pr=pr, gpr=gpr, ht=ht, hb=hb: (
                            e.transpose(pt[gpr][:, hb * 128:(hb + 1) * 128],
                                        xln[pr][:, ht * 128:(ht + 1) * 128],
                                        sb_id[:, :]).then_inc(sP.h, 1) if hb == 3 else
                            e.transpose(pt[gpr][:, hb * 128:(hb + 1) * 128],
                                        xln[pr][:, ht * 128:(ht + 1) * 128], sb_id[:, :])))
                    vp = sP.inc()
                    q.vector.append(W(sP, vp))
                    if _noevac:
                        q.vector.append(lambda e: e.engine_nop().then_inc(sV2.h, 1))
                    else:
                        for hb in range(4):
                            ht = g * 4 + hb
                            last = hb == 3
                            q.vector.append(lambda e, gpr=gpr, hb=hb, ht=ht, lt=lt, last=last: (
                                e.tensor_copy(out=sb_xlnT[:, ht, lt * 128:(lt + 1) * 128],
                                              in_=pt[gpr][:, hb * 128:(hb + 1) * 128]).then_inc(sV2.h, 1)
                                if last else
                                e.tensor_copy(out=sb_xlnT[:, ht, lt * 128:(lt + 1) * 128],
                                              in_=pt[gpr][:, hb * 128:(hb + 1) * 128])))
                    evac_done[gi] = sV2.inc()
                    gi += 1
            if "xlnT" in taps:
                q.sync.append(W(sV2, sV2.n))
                q.sync.append(lambda e: e.dma_start(
                    out=d_taps["xlnT"], in_=sb_xlnT[:, :, :].rearrange("p a b -> p (a b)")).then_inc(sSt.h, 16))
                sSt.n += 16
                q.sync.append(W(sSt, sSt.n))
            emit_block(nc, q)

        if _skip("ph1"):
            return _stop()
        q = Q()
        with ExitStack() as ph:
            winb = [ph.enter_context(nc.sbuf_tensor(f"t_win{i}", [128, NKT * 128], BF16)) for i in range(2)]
            hpad = ph.enter_context(nc.sbuf_tensor("t_hpad", [128, L + 8], BF16))
            diag = ph.enter_context(nc.sbuf_tensor("t_diag", [128, 4 * 128], BF16))
            ubuf = ph.enter_context(nc.sbuf_tensor("t_ubuf", [128, L], BF16))
            gbuf = ph.enter_context(nc.sbuf_tensor("t_gbuf", [128, L], BF16))
            pps = ph.enter_context(nc.psum_tensor("t_pps", [128, L], F32))
            cps = ph.enter_context(nc.psum_tensor("t_cps", [128, L], F32))

            q.vector.append(lambda e: e.memset(hpad[:, 0:8], 0.0).then_inc(sV.h, 1))
            sV.inc()

            proj_done = {}
            for et in range(64):
                pr = et % 2
                hidden = et < NET
                if et >= 2:
                    q.gpsimd.append(W(sP, proj_done[et - 2]))
                q.gpsimd.append(lambda e, et=et, pr=pr: e.dma_start(
                    out=winb[pr][:, :].rearrange("p (t e) -> p t e", e=128),
                    in_=d_win[:, et * 128:(et + 1) * 128].rearrange("(t p) e -> p t e", p=128)).then_inc(sGd.h, 16))
                sGd.n += 16
                vg = sGd.n
                q.gpsimd.append(W(sGd, vg))

                q.tensor.append(W(sGd, vg))
                if et >= 1:
                    q.tensor.append(W(sA, sA.n))
                for lt in range(4):
                    for kt in range(NKT):
                        fin = (lt == 3 and kt == NKT - 1)
                        q.tensor.append(lambda e, pr=pr, lt=lt, kt=kt, fin=fin: (
                            e.matmul(pps[:, lt * 512:(lt + 1) * 512],
                                     winb[pr][:, kt * 128:(kt + 1) * 128],
                                     sb_xlnT[:, kt, lt * 512:(lt + 1) * 512],
                                     start=(kt == 0), stop=(kt == NKT - 1)
                                     ).then_inc(sP.h, 1) if fin else
                            e.matmul(pps[:, lt * 512:(lt + 1) * 512],
                                     winb[pr][:, kt * 128:(kt + 1) * 128],
                                     sb_xlnT[:, kt, lt * 512:(lt + 1) * 512],
                                     start=(kt == 0), stop=(kt == NKT - 1))))
                proj_done[et] = sP.inc()

                if hidden:
                    q.scalar.append(W(sP, proj_done[et]))
                    q.scalar.append(lambda e: e.activation(
                        out=hpad[:, 8:8 + L], in_=pps[:, :], func=AF.Copy).then_inc(sA.h, 1))
                    vh = sA.inc()
                    q.vector.append(W(sP, sP.n))
                    q.vector.append(lambda e, et=et: [e.tensor_scalar_mul(
                        diag[:, k * 128:(k + 1) * 128], sb_id[:, :],
                        sb_ck[:, et * 4 + k:et * 4 + k + 1]) for k in range(4)][-1].then_inc(sV.h, 1))
                    vd = sV.inc()
                    q.tensor.append(W(sA, vh))
                    q.tensor.append(W(sV, vd))
                    for lt in range(4):
                        for k in range(4):
                            fin = (lt == 3 and k == 3)
                            q.tensor.append(lambda e, lt=lt, k=k, fin=fin: (
                                e.matmul(cps[:, lt * 512:(lt + 1) * 512],
                                         diag[:, k * 128:(k + 1) * 128],
                                         hpad[:, 5 + k + lt * 512: 5 + k + lt * 512 + 512],
                                         start=(k == 0), stop=(k == 3)).then_inc(sP.h, 1) if fin else
                                e.matmul(cps[:, lt * 512:(lt + 1) * 512],
                                         diag[:, k * 128:(k + 1) * 128],
                                         hpad[:, 5 + k + lt * 512: 5 + k + lt * 512 + 512],
                                         start=(k == 0), stop=(k == 3))))
                    vconv = sP.inc()
                    q.scalar.append(W(sP, vconv))
                    if et >= 1:
                        q.scalar.append(W(sSt, sSt.n))
                    q.scalar.append(lambda e, et=et: e.activation(
                        out=ubuf[:, :], in_=cps[:, :], func=AF.Silu,
                        bias=sb_cb[:, et:et + 1]).then_inc(sA.h, 1))
                    vu = sA.inc()
                    q.sync.append(W(sA, vu))
                    q.sync.append(lambda e, et=et: e.dma_start(
                        out=d_u[et * 128:(et + 1) * 128, :], in_=ubuf[:, :]).then_inc(sSt.h, 16))
                    sSt.n += 16
                else:
                    ge = et - NET
                    q.scalar.append(W(sP, proj_done[et]))
                    if et >= NET + 1:
                        q.scalar.append(W(sSt, sSt.n))
                    q.scalar.append(lambda e: e.activation(
                        out=gbuf[:, :], in_=pps[:, :], func=AF.Silu).then_inc(sA.h, 1))
                    vgate = sA.inc()
                    q.sync.append(W(sA, vgate))
                    q.sync.append(lambda e, ge=ge: e.dma_start(
                        out=d_gate[ge * 128:(ge + 1) * 128, :], in_=gbuf[:, :]).then_inc(sSt.h, 16))
                    sSt.n += 16
            q.sync.append(W(sSt, sSt.n))
            emit_block(nc, q)
        es1.close(); open_stacks.remove(es1)

        if "u" in taps or "gate" in taps:
            q = Q()
            if "u" in taps:
                q.sync.append(lambda e: e.dma_start(out=d_taps["u"], in_=d_u).then_inc(sSt.h, 16))
                sSt.n += 16
            if "gate" in taps:
                q.sync.append(lambda e: e.dma_start(out=d_taps["gate"], in_=d_gate).then_inc(sSt.h, 16))
                sSt.n += 16
            q.sync.append(W(sSt, sSt.n))
            emit_block(nc, q)

        if _skip("ph2"):
            return _stop()
        q = Q()
        with ExitStack() as ph:
            ust = [ph.enter_context(nc.sbuf_tensor(f"t_ust{i}", [128, NET * 512], BF16)) for i in range(2)]
            bcf = ph.enter_context(nc.sbuf_tensor("t_bcf", [32, L], BF16))
            dps = [ph.enter_context(nc.psum_tensor(f"t_dps{i}", [128, 512], F32)) for i in range(2)]
            bps = [ph.enter_context(nc.psum_tensor(f"t_bps{i}", [32, 512], F32)) for i in range(2)]

            evdt = {}
            evbc = {}
            mm2_done = {}
            for lt in range(4):
                pr = lt % 2
                if lt >= 2:
                    q.sync.append(W(sP, mm2_done[lt - 2]))
                q.sync.append(lambda e, lt=lt, pr=pr: e.dma_start(
                    out=ust[pr][:, :].rearrange("p (t m) -> p t m", m=512),
                    in_=d_u[:, lt * 512:(lt + 1) * 512].rearrange("(t p) m -> p t m", p=128)).then_inc(sL.h, 16))
                sL.n += 16
                vl = sL.n
                q.tensor.append(W(sL, vl))
                if lt >= 2:
                    q.tensor.append(W(sA, evdt[lt - 2]))
                    q.tensor.append(W(sV, evbc[lt - 2]))
                for kt in range(NET):
                    q.tensor.append(lambda e, pr=pr, kt=kt: e.matmul(
                        dps[pr][:, :], sb_wx[:, kt * 160:kt * 160 + 128],
                        ust[pr][:, kt * 512:(kt + 1) * 512],
                        start=(kt == 0), stop=(kt == NET - 1)))
                for kt in range(NET):
                    fin = kt == NET - 1
                    q.tensor.append(lambda e, pr=pr, kt=kt, fin=fin: (
                        e.matmul(bps[pr][:, :], sb_wx[:, kt * 160 + 128:kt * 160 + 160],
                                 ust[pr][:, kt * 512:(kt + 1) * 512],
                                 start=(kt == 0), stop=(kt == NET - 1)).then_inc(sP.h, 1) if fin else
                        e.matmul(bps[pr][:, :], sb_wx[:, kt * 160 + 128:kt * 160 + 160],
                                 ust[pr][:, kt * 512:(kt + 1) * 512],
                                 start=(kt == 0), stop=(kt == NET - 1))))
                vp = sP.inc()
                mm2_done[lt] = vp
                q.scalar.append(W(sP, vp))
                q.scalar.append(lambda e, lt=lt, pr=pr: e.activation(
                    out=sb_dtT[:, lt * 512:(lt + 1) * 512], in_=dps[pr][:, :],
                    func=AF.Copy).then_inc(sA.h, 1))
                evdt[lt] = sA.inc()
                q.vector.append(W(sP, vp))
                q.vector.append(lambda e, lt=lt, pr=pr: e.tensor_copy(
                    out=bcf[:, lt * 512:(lt + 1) * 512], in_=bps[pr][:, :]).then_inc(sV.h, 1))
                evbc[lt] = sV.inc()
            q.sync.append(W(sV, sV.n))
            q.sync.append(lambda e: e.dma_start(out=d_bcbf, in_=bcf[:, :]).then_inc(sSt.h, 16))
            sSt.n += 16
            if "dtT" in taps:
                q.sync.append(W(sA, sA.n))
                q.sync.append(lambda e: e.dma_start(out=d_taps["dtT"], in_=sb_dtT[:, :]).then_inc(sSt.h, 16))
                sSt.n += 16
            if "bc" in taps:
                q.sync.append(lambda e: e.dma_start(out=d_taps["bc"], in_=bcf[:, :]).then_inc(sSt.h, 16))
                sSt.n += 16
            q.sync.append(W(sSt, sSt.n))
            emit_block(nc, q)

        if _skip("ph3"):
            return _stop()
        for half in range(2):
            q = Q()
            with ExitStack() as ph:
                bcb = ph.enter_context(nc.sbuf_tensor(f"t_bcb{half}", [128, 16, L], BF16))
                dbu2 = [ph.enter_context(nc.sbuf_tensor(f"t_dbu2{half}_{i}", [128, L], BF16)) for i in range(2)]
                t1 = ph.enter_context(nc.sbuf_tensor(f"t_t1{half}", [128, L], F32))
                t2 = ph.enter_context(nc.sbuf_tensor(f"t_t2{half}", [128, L], F32))
                dts = ph.enter_context(nc.sbuf_tensor(f"t_dts{half}", [128, L], F32))
                du = ph.enter_context(nc.sbuf_tensor(f"t_du{half}", [128, L], BF16))
                uet = [ph.enter_context(nc.sbuf_tensor(f"t_uet{half}_{i}", [128, L], BF16)) for i in range(2)]
                dab = [ph.enter_context(nc.sbuf_tensor(f"t_dab{half}_{i}", [128, L], F32)) for i in range(2)]
                dbu = ph.enter_context(nc.sbuf_tensor(f"t_dbu{half}", [128, L], BF16))
                hh = ph.enter_context(nc.sbuf_tensor(f"t_hh{half}", [128, L], BF16))
                zb = [ph.enter_context(nc.sbuf_tensor(f"t_z{half}_{i}", [128, L], BF16)) for i in range(4)]
                yt = ph.enter_context(nc.sbuf_tensor(f"t_yt{half}", [128, L], F32))
                yg = ph.enter_context(nc.sbuf_tensor(f"t_yg{half}", [128, L], BF16))
                if half == 1:
                    get_ = [ph.enter_context(nc.sbuf_tensor(f"t_get{half}_{i}", [128, L], BF16)) for i in range(2)]
                    yh0b = [ph.enter_context(nc.sbuf_tensor(f"t_yh0{half}_{i}", [128, L], BF16)) for i in range(2)]
                sps = [ph.enter_context(nc.psum_tensor(f"t_sps{half}_{i}", [128, L], F32)) for i in range(2)]

                for i in range(8):
                    q.sync.append(lambda e, i=i, half=half: e.dma_start(
                        out=bcb[:, i, :],
                        in_=d_bcbf[half * 8 + i: half * 8 + i + 1, :].to_broadcast((128, L))).then_inc(sL.h, 16))
                    sL.n += 16
                    q.sync.append(lambda e, i=i, half=half: e.dma_start(
                        out=bcb[:, 8 + i, :],
                        in_=d_bcbf[16 + half * 8 + i: 16 + half * 8 + i + 1, :].to_broadcast((128, L))).then_inc(sL.h, 16))
                    sL.n += 16
                vbc = sL.n

                scan_base = sV2.n
                sidx = 0
                psum_free = {}
                yg_spill = {}
                for et in range(NET):
                    pr = et % 2
                    if et >= 2:
                        q.sync.append(W(sV, sV.n))
                    q.sync.append(lambda e, et=et, pr=pr: e.dma_start(
                        out=uet[pr][:, :], in_=d_u[et * 128:(et + 1) * 128, :]).then_inc(sL.h, 16))
                    sL.n += 16
                    if half == 1:
                        q.sync.append(lambda e, et=et, pr=pr: e.dma_start(
                            out=get_[pr][:, :], in_=d_gate[et * 128:(et + 1) * 128, :]).then_inc(sL.h, 16))
                        sL.n += 16
                        q.sync.append(lambda e, et=et, pr=pr: e.dma_start(
                            out=yh0b[pr][:, :], in_=d_yh0[et * 128:(et + 1) * 128, :]).then_inc(sL.h, 16))
                        sL.n += 16
                    vload = sL.n

                    if et >= 2:
                        q.tensor.append(W(sA, psum_free[et - 2]))
                    for lt in range(4):
                        fin = lt == 3
                        q.tensor.append(lambda e, pr=pr, et=et, lt=lt, fin=fin: (
                            e.matmul(sps[pr][:, lt * 512:(lt + 1) * 512],
                                     sb_wdt[:, et * 128:(et + 1) * 128],
                                     sb_dtT[:, lt * 512:(lt + 1) * 512],
                                     start=True, stop=True).then_inc(sP.h, 1) if fin else
                            e.matmul(sps[pr][:, lt * 512:(lt + 1) * 512],
                                     sb_wdt[:, et * 128:(et + 1) * 128],
                                     sb_dtT[:, lt * 512:(lt + 1) * 512],
                                     start=True, stop=True)))
                    vdts = sP.inc()

                    q.scalar.append(W(sP, vdts))
                    if et >= 1:
                        q.scalar.append(W(sV, sV.n))
                    q.scalar.append(lambda e, pr=pr, et=et: e.activation(
                        out=t1[:, :], in_=sps[pr][:, :], func=AF.Abs, bias=sb_bdt[:, et:et + 1]))
                    q.scalar.append(lambda e, pr=pr, et=et: e.activation(
                        out=t2[:, :], in_=sps[pr][:, :], func=AF.Relu,
                        bias=sb_bdt[:, et:et + 1]).then_inc(sA.h, 1))
                    psum_free[et] = sA.inc()
                    q.scalar.append(lambda e: e.activation(out=t1[:, :], in_=t1[:, :], func=AF.Exp, scale=-1.0))
                    q.scalar.append(lambda e: e.activation(out=t1[:, :], in_=t1[:, :], func=AF.Ln,
                                                           bias=1.0).then_inc(sA.h, 1))
                    vsp = sA.inc()

                    q.vector.append(W(sA, vsp))
                    q.vector.append(W(sL, vload))
                    q.vector.append(lambda e: e.tensor_add(dts[:, :], t1[:, :], t2[:, :]).then_inc(sV.h, 1))
                    vdtsd = sV.inc()
                    q.vector.append(lambda e, pr=pr: e.tensor_mul(
                        du[:, :], dts[:, :], uet[pr][:, :]).then_inc(sV.h, 1))
                    vdu = sV.inc()

                    q.gpsimd.append(W(sV, vdu))
                    if et == 0:
                        q.gpsimd.append(W(sL, vbc))
                    for nn in range(8):
                        n = half * 8 + nn
                        npr = sidx % 2
                        if sidx >= 2:
                            q.gpsimd.append(W(sV2, scan_base + sidx - 1))
                        q.gpsimd.append(lambda e, nn=nn: e.tensor_mul(
                            dbu2[nn % 2][:, :], du[:, :], bcb[:, nn, :]).then_inc(sGd.h, 1))
                        vdbu = sGd.inc()
                        q.scalar.append(W(sV, vdtsd))
                        if sidx >= 2:
                            q.scalar.append(W(sV2, scan_base + sidx - 1))
                        q.scalar.append(lambda e, npr=npr, et=et, n=n: e.activation(
                            out=dab[npr][:, :], in_=dts[:, :], func=AF.Exp,
                            scale=sb_aneg[:, et * NST + n:et * NST + n + 1]).then_inc(sA2.h, 1))
                        vda = sA2.inc()
                        q.vector.append(W(sA2, vda))
                        q.vector.append(W(sGd, vdbu))
                        if et == 0 and nn == 0:
                            q.vector.append(W(sL, vbc))
                        q.vector.append(lambda e, npr=npr, nn=nn: e.tensor_tensor_scan(
                            out=hh[:, :], data0=dab[npr][:, :], data1=dbu2[nn % 2][:, :],
                            initial=0.0, op0=ALU.mult, op1=ALU.add).then_inc(sV2.h, 1))
                        sV2.inc()
                        sidx += 1
                        q.vector.append(lambda e, nn=nn: e.tensor_mul(zb[nn % 4][:, :], hh[:, :], bcb[:, 8 + nn, :]))
                        if nn % 2 == 1:
                            q.vector.append(lambda e, nn=nn: e.tensor_add(
                                zb[nn - 1 if nn < 4 else nn - 5][:, :],
                                zb[(nn - 1) % 4][:, :], zb[nn % 4][:, :]))
                        if nn == 3:
                            q.vector.append(lambda e: e.tensor_add(yt[:, :], zb[0][:, :], zb[2][:, :]))
                        if nn == 7:
                            q.vector.append(lambda e: e.tensor_add(zb[0][:, :], zb[0][:, :], zb[2][:, :]))
                            q.vector.append(lambda e: e.tensor_add(yt[:, :], yt[:, :], zb[0][:, :]))

                    if half == 0:
                        if et >= 1:
                            q.vector.append(W(sSt, yg_spill[et - 1]))
                        q.vector.append(lambda e: e.tensor_copy(out=yg[:, :], in_=yt[:, :]).then_inc(sV.h, 1))
                        vyh = sV.inc()
                        q.sync.append(W(sV, vyh))
                        q.sync.append(lambda e, et=et: e.dma_start(
                            out=d_yh0[et * 128:(et + 1) * 128, :], in_=yg[:, :]).then_inc(sSt.h, 16))
                        sSt.n += 16
                        yg_spill[et] = sSt.n
                    else:
                        q.vector.append(lambda e, pr=pr: e.tensor_add(yt[:, :], yt[:, :], yh0b[pr][:, :]))
                        q.vector.append(lambda e, pr=pr, et=et: e.scalar_tensor_tensor(
                            out=yt[:, :], in0=uet[pr][:, :], scalar=sb_D[:, et:et + 1], in1=yt[:, :],
                            op0=ALU.mult, op1=ALU.add))
                        if et >= 1:
                            q.vector.append(W(sSt, yg_spill[et - 1]))
                        q.vector.append(lambda e, pr=pr: e.tensor_mul(
                            yg[:, :], yt[:, :], get_[pr][:, :]).then_inc(sV.h, 1))
                        vygd = sV.inc()
                        q.sync.append(W(sV, vygd))
                        q.sync.append(lambda e, et=et: e.dma_start(
                            out=d_yg[et * 128:(et + 1) * 128, :], in_=yg[:, :]).then_inc(sSt.h, 16))
                        sSt.n += 16
                        yg_spill[et] = sSt.n
                q.sync.append(W(sSt, sSt.n))
                emit_block(nc, q)

        if "yg" in taps:
            q = Q()
            q.sync.append(lambda e: e.dma_start(out=d_taps["yg"], in_=d_yg).then_inc(sSt.h, 16))
            sSt.n += 16
            q.sync.append(W(sSt, sSt.n))
            emit_block(nc, q)

        if _skip("ph4"):
            return _stop()
        q = Q()
        with ExitStack() as ph:
            wob = ph.enter_context(nc.sbuf_tensor("t_wob", [128, NET * 512], BF16))
            yst = [ph.enter_context(nc.sbuf_tensor(f"t_yst{i}", [128, NET * 128], BF16)) for i in range(2)]
            aev = [ph.enter_context(nc.sbuf_tensor(f"t_aev{i}", [128, 512], F32)) for i in range(2)]
            ops_ = [ph.enter_context(nc.psum_tensor(f"t_ops{i}", [128, 512], F32)) for i in range(2)]

            ev_done = {}
            mm_done = {}
            for htl in range(4):
                if htl > 0:
                    q.gpsimd.append(W(sP, sP.n))
                q.gpsimd.append(lambda e, htl=htl: e.dma_start(
                    out=wob[:, :].rearrange("p (t h) -> p t h", h=512),
                    in_=d_wout[:, htl * 512:(htl + 1) * 512].rearrange("(t p) h -> p t h", p=128)).then_inc(sGd.h, 16))
                sGd.n += 16
                vw = sGd.n
                q.gpsimd.append(W(sGd, vw))
                for lt in range(NLT):
                    pr = lt % 2
                    gl = htl * NLT + lt
                    if gl >= 2:
                        q.sync.append(W(sP, mm_done[gl - 2]))
                    q.sync.append(lambda e, lt=lt, pr=pr: e.dma_start(
                        out=yst[pr][:, :].rearrange("p (t m) -> p t m", m=128),
                        in_=d_yg[:, lt * 128:(lt + 1) * 128].rearrange("(t p) m -> p t m", p=128)).then_inc(sL.h, 16))
                    sL.n += 16
                    vy = sL.n
                    q.tensor.append(W(sGd, vw))
                    q.tensor.append(W(sL, vy))
                    if gl >= 2:
                        q.tensor.append(W(sA, ev_done[gl - 2]))
                    for kt in range(NET):
                        fin = kt == NET - 1
                        q.tensor.append(lambda e, pr=pr, kt=kt, fin=fin: (
                            e.matmul(ops_[pr][:, :], yst[pr][:, kt * 128:(kt + 1) * 128],
                                     wob[:, kt * 512:(kt + 1) * 512],
                                     start=(kt == 0), stop=(kt == NET - 1)).then_inc(sP.h, 1) if fin else
                            e.matmul(ops_[pr][:, :], yst[pr][:, kt * 128:(kt + 1) * 128],
                                     wob[:, kt * 512:(kt + 1) * 512],
                                     start=(kt == 0), stop=(kt == NET - 1))))
                    vp = sP.inc()
                    mm_done[gl] = vp
                    q.scalar.append(W(sP, vp))
                    if gl >= 2:
                        q.scalar.append(W(sSt, sSt.n))
                    q.scalar.append(lambda e, pr=pr: e.activation(
                        out=aev[pr][:, :], in_=ops_[pr][:, :], func=AF.Copy).then_inc(sA.h, 1))
                    ev_done[gl] = sA.inc()
                    q.sync.append(W(sA, ev_done[gl]))
                    q.sync.append(lambda e, lt=lt, htl=htl, pr=pr: e.dma_start(
                        out=d_attn[lt * 128:(lt + 1) * 128, htl * 512:(htl + 1) * 512],
                        in_=aev[pr][:, :]).then_inc(sSt.h, 16))
                    sSt.n += 16
            q.sync.append(W(sSt, sSt.n))
            if "attn" in taps:
                q.sync.append(lambda e: e.dma_start(out=d_taps["attn"], in_=d_attn).then_inc(sSt.h, 16))
                sSt.n += 16
                q.sync.append(W(sSt, sSt.n))
            emit_block(nc, q)

        if _skip("ph5"):
            return _stop()
        es3 = ExitStack(); open_stacks.append(es3)
        sb_xTp = es3.enter_context(nc.sbuf_tensor("p_xTp", [128, 16, L], BF16))
        es2 = ExitStack(); open_stacks.append(es2)
        sb_x1ln = es2.enter_context(nc.sbuf_tensor("p_x1ln", [128, NLT, H], BF16))
        q = Q()
        with ExitStack() as ph:
            at = [ph.enter_context(nc.sbuf_tensor("t_at0", [128, H], F32))]
            xt = [ph.enter_context(nc.sbuf_tensor("t_xt50", [128, H], F32))]
            x1t = [ph.enter_context(nc.sbuf_tensor(f"t_x1t{i}", [128, H], F32)) for i in range(2)]
            scr = ph.enter_context(nc.sbuf_tensor("t_scr5", [128, H], F32))
            c1 = ph.enter_context(nc.sbuf_tensor("t_c15", [128, 8], F32))

            x1ln_done = {}
            x1_done = {}
            for lt in range(NLT):
                pr = lt % 2
                if lt >= 2:
                    q.sync.append(W(sV, x1ln_done[lt - 2]))
                if lt >= 1:
                    q.sync.append(W(sV, x1_done[lt - 1]))
                q.sync.append(lambda e, lt=lt: e.dma_start(
                    out=at[0][:, :], in_=d_attn[lt * 128:(lt + 1) * 128, :]).then_inc(sL.h, 16))
                sL.n += 16
                q.sync.append(lambda e, lt=lt: e.dma_start(
                    out=xt[0][:, :], in_=d_x[lt * 128:(lt + 1) * 128, :]).then_inc(sL.h, 16))
                sL.n += 16
                vl = sL.n

                q.scalar.append(W(sL, vl))
                va = ln_stats(q, sA, sV, (lambda: at[0][:, :]), scr, c1, 1e-5)
                q.vector.append(W(sA, va))
                q.vector.append(lambda e: e.tensor_scalar(
                    out=scr[:, :], in0=at[0][:, :], scalar1=c1[:, 2:3], scalar2=c1[:, 7:8],
                    op0=ALU.subtract, op1=ALU.mult))
                q.vector.append(lambda e, pr=pr: e.tensor_add(
                    x1t[pr][:, :], xt[0][:, :], scr[:, :]).then_inc(sV.h, 1))
                vx1 = sV.inc()
                x1_done[lt] = vx1
                q.sync.append(W(sV, vx1))
                q.sync.append(lambda e, lt=lt, pr=pr: e.dma_start(
                    out=d_x1[lt * 128:(lt + 1) * 128, :], in_=x1t[pr][:, :]).then_inc(sSt.h, 16))
                sSt.n += 16
                q.scalar.append(W(sV, vx1))
                va2 = ln_stats(q, sA, sV, (lambda pr=pr: x1t[pr][:, :]), scr, c1, 1e-5)
                q.vector.append(W(sA, va2))
                q.vector.append(lambda e, pr=pr, lt=lt: e.tensor_scalar(
                    out=sb_x1ln[:, lt, :], in0=x1t[pr][:, :], scalar1=c1[:, 2:3], scalar2=c1[:, 7:8],
                    op0=ALU.subtract, op1=ALU.mult).then_inc(sV.h, 1))
                x1ln_done[lt] = sV.inc()
            q.sync.append(W(sV, sV.n))
            if "x1" in taps:
                q.sync.append(lambda e: e.dma_start(out=d_taps["x1"], in_=d_x1).then_inc(sSt.h, 16))
                sSt.n += 16
                q.sync.append(W(sSt, sSt.n))
            emit_block(nc, q)

        if _skip("ph6a"):
            return _stop()
        q = Q()
        with ExitStack() as ph:
            pt = [ph.enter_context(nc.psum_tensor(f"t_pt6{i}", [128, 1024], BF16)) for i in range(2)]
            ebase = sV2.n
            gidx = 0
            for kk in range(8):
                for jb in range(2):
                    pid = kk * 2 + jb
                    for g in range(4):
                        gpr = gidx % 2
                        if gidx >= 2:
                            q.tensor.append(W(sV2, ebase + gidx - 1))
                        for li in range(4):
                            lt = g * 4 + li
                            fin = li == 3
                            q.tensor.append(lambda e, gpr=gpr, li=li, lt=lt, kk=kk, jb=jb, fin=fin: (
                                e.transpose(pt[gpr][:, li * 128:(li + 1) * 128],
                                            sb_x1ln[:, lt, kk + 1024 * jb: kk + 1024 * jb + 8 * 127 + 1: 8],
                                            sb_id[:, :]).then_inc(sP.h, 1) if fin else
                                e.transpose(pt[gpr][:, li * 128:(li + 1) * 128],
                                            sb_x1ln[:, lt, kk + 1024 * jb: kk + 1024 * jb + 8 * 127 + 1: 8],
                                            sb_id[:, :])))
                        vp = sP.inc()
                        q.vector.append(W(sP, vp))
                        q.vector.append(lambda e, gpr=gpr, pid=pid, g=g: e.tensor_copy(
                            out=sb_xTp[:, pid, g * 512:(g + 1) * 512],
                            in_=pt[gpr][:, 0:512]).then_inc(sV2.h, 1))
                        sV2.inc()
                        gidx += 1
            q.vector.append(W(sV2, sV2.n))
            if "xtp" in taps:
                q.sync.append(W(sV2, sV2.n))
                q.sync.append(lambda e: e.dma_start(
                    out=d_taps["xtp"], in_=sb_xTp[:, :, :].rearrange("p a b -> p (a b)")).then_inc(sSt.h, 16))
                sSt.n += 16
                q.sync.append(W(sSt, sSt.n))
            emit_block(nc, q)
        es2.close(); open_stacks.remove(es2)

        if _skip("ph6b"):
            return _stop()
        q = Q()
        with ExitStack() as ph:
            lgt = [ph.enter_context(nc.sbuf_tensor(f"t_lgt{i}", [128, 16 * 128], BF16)) for i in range(2)]
            lut = [ph.enter_context(nc.sbuf_tensor(f"t_lut{i}", [128, 16 * 128], BF16)) for i in range(2)]
            sg = [ph.enter_context(nc.sbuf_tensor(f"t_sg{i}", [128, 512], BF16)) for i in range(2)]
            su = [ph.enter_context(nc.sbuf_tensor(f"t_su{i}", [128, 512], BF16)) for i in range(2)]
            hrow = ph.enter_context(nc.sbuf_tensor("t_hrow", [128, L], BF16))
            gps = [ph.enter_context(nc.psum_tensor(f"t_gps{i}", [128, 512], F32)) for i in range(2)]
            ups = [ph.enter_context(nc.psum_tensor(f"t_ups{i}", [128, 512], F32)) for i in range(2)]

            ev_done = {}
            mul_done = {}
            spill_done = {}
            mmct_done = {}
            q.sync.append(W(sCC, 1))
            for ct in range(44):
                pr = ct % 2
                if ct >= 2:
                    q.sync.append(W(sP, mmct_done[ct - 2]))
                q.sync.append(lambda e, ct=ct, pr=pr: e.dma_start(
                    out=lgt[pr][:, :].rearrange("p (pid c) -> p pid c", c=128),
                    in_=d_plgu[:, :, ct * 128:(ct + 1) * 128].rearrange("pid p c -> p pid c")).then_inc(sL.h, 16))
                sL.n += 16
                q.sync.append(lambda e, ct=ct, pr=pr: e.dma_start(
                    out=lut[pr][:, :].rearrange("p (pid c) -> p pid c", c=128),
                    in_=d_plgu[:, :, IFF + ct * 128: IFF + (ct + 1) * 128].rearrange("pid p c -> p pid c")).then_inc(sL.h, 16))
                sL.n += 16
                vl = sL.n
                q.tensor.append(W(sL, vl))
                for lt in range(4):
                    lpr = lt % 2
                    gq = ct * 4 + lt
                    if gq >= 2:
                        q.tensor.append(W(sA, ev_done[gq - 2]))
                    for pid in range(16):
                        q.tensor.append(lambda e, pr=pr, lpr=lpr, pid=pid, lt=lt: e.matmul(
                            gps[lpr][:, :], lgt[pr][:, pid * 128:(pid + 1) * 128],
                            sb_xTp[:, pid, lt * 512:(lt + 1) * 512],
                            start=(pid == 0), stop=(pid == 15)))
                    for pid in range(16):
                        fin = pid == 15
                        q.tensor.append(lambda e, pr=pr, lpr=lpr, pid=pid, lt=lt, fin=fin: (
                            e.matmul(ups[lpr][:, :], lut[pr][:, pid * 128:(pid + 1) * 128],
                                     sb_xTp[:, pid, lt * 512:(lt + 1) * 512],
                                     start=(pid == 0), stop=(pid == 15)).then_inc(sP.h, 1) if fin else
                            e.matmul(ups[lpr][:, :], lut[pr][:, pid * 128:(pid + 1) * 128],
                                     sb_xTp[:, pid, lt * 512:(lt + 1) * 512],
                                     start=(pid == 0), stop=(pid == 15))))
                    vp = sP.inc()
                    q.scalar.append(W(sP, vp))
                    if gq >= 2:
                        q.scalar.append(W(sV, mul_done[gq - 2]))
                    q.scalar.append(lambda e, lpr=lpr: e.activation(
                        out=sg[lpr][:, :], in_=gps[lpr][:, :], func=AF.Silu))
                    q.scalar.append(lambda e, lpr=lpr: e.activation(
                        out=su[lpr][:, :], in_=ups[lpr][:, :], func=AF.Copy).then_inc(sA.h, 1))
                    ev_done[gq] = sA.inc()
                    q.vector.append(W(sA, ev_done[gq]))
                    if ct >= 1 and lt == 0:
                        q.vector.append(W(sSt, spill_done[ct - 1]))
                    q.vector.append(lambda e, lt=lt, lpr=lpr: e.tensor_mul(
                        hrow[:, lt * 512:(lt + 1) * 512], sg[lpr][:, :], su[lpr][:, :]).then_inc(sV.h, 1))
                    mul_done[gq] = sV.inc()
                mmct_done[ct] = sP.n
                q.sync.append(W(sV, mul_done[ct * 4 + 3]))
                q.sync.append(lambda e, ct=ct: e.dma_start(
                    out=d_hact[ct * 128:(ct + 1) * 128, :], in_=hrow[:, :]).then_inc(sSt.h, 16))
                sSt.n += 16
                spill_done[ct] = sSt.n
            q.sync.append(W(sSt, sSt.n))
            if "hact" in taps:
                q.sync.append(lambda e: e.dma_start(out=d_taps["hact"], in_=d_hact).then_inc(sSt.h, 16))
                sSt.n += 16
                q.sync.append(W(sSt, sSt.n))
            emit_block(nc, q)
        es3.close(); open_stacks.remove(es3)

        if _skip("ph6c"):
            return _stop()
        q = Q()
        with ExitStack() as ph:
            wdn = ph.enter_context(nc.sbuf_tensor("t_wdn", [128, 44 * 512], BF16))
            hst = [ph.enter_context(nc.sbuf_tensor(f"t_hst{i}", [128, 44 * 128], BF16)) for i in range(2)]
            fev = [ph.enter_context(nc.sbuf_tensor(f"t_fev{i}", [128, 512], F32)) for i in range(2)]
            dps6 = [ph.enter_context(nc.psum_tensor(f"t_dps6{i}", [128, 512], F32)) for i in range(2)]

            ev_done = {}
            mm_done = {}
            q.sync.append(W(sCC, 2))
            for htl in range(4):
                if htl > 0:
                    q.sync.append(W(sP, sP.n))
                q.sync.append(lambda e, htl=htl: e.dma_start(
                    out=wdn[:, :].rearrange("p (t h) -> p t h", h=512),
                    in_=d_pldn[:, htl * 512:(htl + 1) * 512].rearrange("(t p) h -> p t h", p=128)).then_inc(sL.h, 16))
                sL.n += 16
                vw = sL.n
                for lt in range(NLT):
                    pr = lt % 2
                    gl = htl * NLT + lt
                    if gl >= 2:
                        q.sync.append(W(sP, mm_done[gl - 2]))
                    q.sync.append(lambda e, lt=lt, pr=pr: e.dma_start(
                        out=hst[pr][:, :].rearrange("p (t m) -> p t m", m=128),
                        in_=d_hact[:, lt * 128:(lt + 1) * 128].rearrange("(t p) m -> p t m", p=128)).then_inc(sL.h, 16))
                    sL.n += 16
                    vy = sL.n
                    q.tensor.append(W(sL, vy))
                    q.tensor.append(W(sL, vw))
                    if gl >= 2:
                        q.tensor.append(W(sA, ev_done[gl - 2]))
                    for kt in range(44):
                        fin = kt == 43
                        q.tensor.append(lambda e, pr=pr, kt=kt, fin=fin: (
                            e.matmul(dps6[pr][:, :], hst[pr][:, kt * 128:(kt + 1) * 128],
                                     wdn[:, kt * 512:(kt + 1) * 512],
                                     start=(kt == 0), stop=(kt == 43)).then_inc(sP.h, 1) if fin else
                            e.matmul(dps6[pr][:, :], hst[pr][:, kt * 128:(kt + 1) * 128],
                                     wdn[:, kt * 512:(kt + 1) * 512],
                                     start=(kt == 0), stop=(kt == 43))))
                    vp = sP.inc()
                    mm_done[gl] = vp
                    q.scalar.append(W(sP, vp))
                    if gl >= 2:
                        q.scalar.append(W(sSt, sSt.n))
                    q.scalar.append(lambda e, pr=pr: e.activation(
                        out=fev[pr][:, :], in_=dps6[pr][:, :], func=AF.Copy).then_inc(sA.h, 1))
                    ev_done[gl] = sA.inc()
                    q.sync.append(W(sA, ev_done[gl]))
                    q.sync.append(lambda e, lt=lt, htl=htl, pr=pr: e.dma_start(
                        out=d_ffn[lt * 128:(lt + 1) * 128, htl * 512:(htl + 1) * 512],
                        in_=fev[pr][:, :]).then_inc(sSt.h, 16))
                    sSt.n += 16
            q.sync.append(W(sSt, sSt.n))
            if "ffn" in taps:
                q.sync.append(lambda e: e.dma_start(out=d_taps["ffn"], in_=d_ffn).then_inc(sSt.h, 16))
                sSt.n += 16
                q.sync.append(W(sSt, sSt.n))
            emit_block(nc, q)

        if _skip("ph7"):
            return _stop()
        q = Q()
        with ExitStack() as ph:
            ft = [ph.enter_context(nc.sbuf_tensor(f"t_ft{i}", [128, H], F32)) for i in range(2)]
            x1t = [ph.enter_context(nc.sbuf_tensor(f"t_x1t7{i}", [128, H], F32)) for i in range(2)]
            vt = ph.enter_context(nc.sbuf_tensor("t_vt", [128, H], F32))
            ot = [ph.enter_context(nc.sbuf_tensor(f"t_ot{i}", [128, H], F32)) for i in range(2)]
            scr = ph.enter_context(nc.sbuf_tensor("t_scr7", [128, H], F32))
            c1 = ph.enter_context(nc.sbuf_tensor("t_c17", [128, 8], F32))

            out_done = {}
            for lt in range(NLT):
                pr = lt % 2
                if lt >= 2:
                    q.sync.append(W(sV, out_done[lt - 2]))
                q.sync.append(lambda e, lt=lt, pr=pr: e.dma_start(
                    out=ft[pr][:, :], in_=d_ffn[lt * 128:(lt + 1) * 128, :]).then_inc(sL.h, 16))
                sL.n += 16
                q.sync.append(lambda e, lt=lt, pr=pr: e.dma_start(
                    out=x1t[pr][:, :], in_=d_x1[lt * 128:(lt + 1) * 128, :]).then_inc(sL.h, 16))
                sL.n += 16
                vl = sL.n
                q.scalar.append(W(sL, vl))
                q.scalar.append(lambda e, pr=pr: e.activation(
                    out=scr[:, :], in_=ft[pr][:, :], func=AF.Square, accum_out=c1[:, 0:1]).then_inc(sA.h, 1))
                va = sA.inc()
                q.vector.append(W(sA, va))
                q.vector.append(lambda e: e.tensor_scalar(
                    out=c1[:, 1:2], in0=c1[:, 0:1], scalar1=1.0 / H, scalar2=1e-6,
                    op0=ALU.mult, op1=ALU.add).then_inc(sV.h, 1))
                vv = sV.inc()
                q.scalar.append(W(sV, vv))
                q.scalar.append(lambda e: e.activation(out=c1[:, 2:3], in_=c1[:, 1:2], func=AF.Ln).then_inc(sA.h, 1))
                vln = sA.inc()
                q.scalar.append(W(sA, vln))
                q.scalar.append(lambda e: e.activation(out=c1[:, 3:4], in_=c1[:, 2:3], func=AF.Exp,
                                                       scale=-0.5).then_inc(sA.h, 1))
                va2 = sA.inc()
                q.vector.append(W(sA, va2))
                q.vector.append(lambda e, pr=pr: e.tensor_scalar_mul(
                    vt[:, :], ft[pr][:, :], c1[:, 3:4]).then_inc(sV.h, 1))
                vv2 = sV.inc()
                q.scalar.append(W(sV, vv2))
                va3 = ln_stats(q, sA, sV, (lambda: vt[:, :]), scr, c1, 1e-5)
                q.vector.append(W(sA, va3))
                q.vector.append(lambda e: e.tensor_scalar(
                    out=scr[:, :], in0=vt[:, :], scalar1=c1[:, 2:3], scalar2=c1[:, 7:8],
                    op0=ALU.subtract, op1=ALU.mult))
                q.vector.append(lambda e, pr=pr: e.tensor_add(
                    ot[pr][:, :], x1t[pr][:, :], scr[:, :]).then_inc(sV.h, 1))
                out_done[lt] = sV.inc()
                q.sync.append(W(sV, out_done[lt]))
                q.sync.append(lambda e, lt=lt, pr=pr: e.dma_start(
                    out=d_out[lt * 128:(lt + 1) * 128, :], in_=ot[pr][:, :]).then_inc(sSt.h, 16))
                sSt.n += 16
            q.sync.append(W(sSt, sSt.n))
            emit_block(nc, q)

    return nc


def _prep_inputs(inputs):
    x = np.ascontiguousarray(np.asarray(inputs["L0_attn_input_states"], np.float32))
    common = dict(
        w_in=np.ascontiguousarray(np.asarray(inputs["L0_attn_W_in"], np.float32)),
        conv_k=np.ascontiguousarray(np.asarray(inputs["L0_attn_conv_kernel"], np.float32).reshape(E, 4)),
        conv_b=np.ascontiguousarray(np.asarray(inputs["L0_attn_conv_bias"], np.float32).reshape(NET, 128).T),
        w_x=np.ascontiguousarray(np.asarray(inputs["L0_attn_W_x"], np.float32)),
        w_dt=np.ascontiguousarray(np.asarray(inputs["L0_attn_W_dt"], np.float32)),
        b_dt=np.ascontiguousarray(np.asarray(inputs["L0_attn_b_dt"], np.float32).reshape(NET, 128).T),
        w_out=np.ascontiguousarray(np.asarray(inputs["L0_attn_W_out"], np.float32)),
        a_log=np.ascontiguousarray(np.asarray(inputs["L0_attn_A_log"], np.float32)),
        dvec=np.ascontiguousarray(np.asarray(inputs["L0_attn_D"], np.float32).reshape(NET, 128).T),

        s_gu=np.ascontiguousarray(np.asarray(inputs["L0_ffn_gate_up_scales"], np.float16)[:, _SGU_PERM]),

        s_dn=np.ascontiguousarray(np.asarray(inputs["L0_ffn_down_scales"], np.float16)),
        ident=np.eye(128, dtype=np.float32),
    )
    pgu_t = np.ascontiguousarray(np.asarray(inputs["L0_ffn_gate_up_packed"], np.int32).T)
    pdn_t = np.ascontiguousarray(np.asarray(inputs["L0_ffn_down_packed"], np.int32).T)
    pgu16 = pgu_t.view(np.int16)
    pdn16 = pdn_t.view(np.int16)
    in_maps = []
    for b in range(8):
        m = dict(common)
        m["x"] = np.ascontiguousarray(x[b])
        h16 = b // 4
        kpos = b % 4
        m["pgu_h"] = np.ascontiguousarray(pgu16[:, h16::2][:, _SGU_PERM])
        m["pdn_h"] = np.ascontiguousarray(pdn16[:, h16::2])
        sh = np.empty((128, 2), np.int32)
        sh[:, 0] = 4 * kpos
        sh[:, 1] = 12 - 4 * kpos
        m["dqsh"] = sh
        in_maps.append(m)
    return in_maps


def kernel(**inputs):
    in_maps = _prep_inputs(inputs)
    nc = build_program(taps=DEBUG_TAPS)
    res = run_bass_kernel_spmd(nc, in_maps, core_ids=list(range(8)))
    out = np.stack([np.asarray(r["out"], np.float32) for r in res.results], axis=0)
    if DEBUG_TAPS:
        kernel.last_results = res
    return out


# version 07
# speedup vs baseline: 1.0788x; 1.0788x over previous
import numpy as np
import ml_dtypes
from contextlib import ExitStack

import concourse.bass as bass
import concourse.mybir as mybir
from concourse.bass_utils import run_bass_kernel_spmd

F32 = mybir.dt.float32
BF16 = mybir.dt.bfloat16
F16 = mybir.dt.float16
I16 = mybir.dt.int16
I32 = mybir.dt.int32
AF = mybir.ActivationFunctionType
ALU = mybir.AluOpType

L = 2048
H = 2048
E = 4096
NST = 16
TR = 128
IFF = 5632
OGU = 11264
NET = 32
NKT = 16
NLT = 16

DEBUG_TAPS = ()

_c = np.arange(OGU if False else 11264)
_half = _c // 5632
_r = _c % 5632
_SGU_PERM = _half * 5632 + 8 * (_r % 704) + (_r // 704)


class Sem:
    def __init__(self, stack, nc, name):
        self.h = stack.enter_context(nc.semaphore(name))
        self.n = 0

    def inc(self, k=1):
        self.n += k
        return self.n


class Q:
    def __init__(self):
        self.sync = []
        self.tensor = []
        self.vector = []
        self.scalar = []
        self.gpsimd = []


def emit_block(nc, q):
    with nc.Block() as block:
        if q.sync:
            block.sync(lambda e, ops=tuple(q.sync): [f(e) for f in ops])
        if q.tensor:
            block.tensor(lambda e, ops=tuple(q.tensor): [f(e) for f in ops])
        if q.vector:
            block.vector(lambda e, ops=tuple(q.vector): [f(e) for f in ops])
        if q.scalar:
            block.scalar(lambda e, ops=tuple(q.scalar): [f(e) for f in ops])
        if q.gpsimd:
            block.gpsimd(lambda e, ops=tuple(q.gpsimd): [f(e) for f in ops])


def W(sem, v):
    assert v <= sem.n, f"wait beyond planned count on {sem.h}: {v} > {sem.n}"
    return lambda e, h=sem.h, v=v: e.wait_ge(h, v)


def ln_stats(q, sA, sV, src_f, scr, c1, eps):
    q.scalar.append(lambda e: e.activation(out=scr[:, :], in_=src_f(), func=AF.Identity,
                                           accum_out=c1[:, 0:1]))
    q.scalar.append(lambda e: e.activation(out=scr[:, :], in_=src_f(), func=AF.Square,
                                           accum_out=c1[:, 1:2]).then_inc(sA.h, 1))
    va = sA.inc()
    q.vector.append(W(sA, va))
    q.vector.append(lambda e: e.tensor_scalar_mul(c1[:, 2:3], c1[:, 0:1], 1.0 / H))
    q.vector.append(lambda e: e.tensor_scalar_mul(c1[:, 3:4], c1[:, 1:2], 1.0 / H).then_inc(sV.h, 1))
    g1 = sV.inc()
    q.vector.append(W(sV, g1))
    q.vector.append(lambda e: e.tensor_mul(c1[:, 4:5], c1[:, 2:3], c1[:, 2:3]).then_inc(sV.h, 1))
    g2 = sV.inc()
    q.vector.append(W(sV, g2))
    q.vector.append(lambda e: e.tensor_scalar(
        out=c1[:, 5:6], in0=c1[:, 3:4], scalar1=c1[:, 4:5], scalar2=float(eps),
        op0=ALU.subtract, op1=ALU.add).then_inc(sV.h, 1))
    vv = sV.inc()
    q.scalar.append(W(sV, vv))
    q.scalar.append(lambda e: e.activation(out=c1[:, 6:7], in_=c1[:, 5:6], func=AF.Ln).then_inc(sA.h, 1))
    ga = sA.inc()
    q.scalar.append(W(sA, ga))
    q.scalar.append(lambda e: e.activation(out=c1[:, 7:8], in_=c1[:, 6:7], func=AF.Exp,
                                           scale=-0.5).then_inc(sA.h, 1))
    return sA.inc()


def dequant_ops(q, sV, nib, ta, tb, tc, ts_, wpl, sfull_ap, src, kpos, rows, width):
    r = rows
    q.vector.append(lambda e: e.tensor_scalar(
        out=nib[0:r, 0:width], in0=src, scalar1=int(4 * kpos), scalar2=int(7),
        op0=ALU.logical_shift_right, op1=ALU.bitwise_and))
    q.vector.append(lambda e: e.tensor_scalar(
        out=ta[0:r, 0:width], in0=nib[0:r, 0:width], scalar1=int(2), scalar2=int(0x3F00),
        op0=ALU.min, op1=ALU.mult))
    q.vector.append(lambda e: e.tensor_scalar(
        out=tb[0:r, 0:width], in0=nib[0:r, 0:width], scalar1=int(0x80), scalar2=int(0x3E80),
        op0=ALU.mult, op1=ALU.add))
    q.vector.append(lambda e: e.tensor_scalar(
        out=tc[0:r, 0:width], in0=nib[0:r, 0:width], scalar1=int(0x40), scalar2=int(0x3F00),
        op0=ALU.mult, op1=ALU.add))
    q.vector.append(lambda e: e.tensor_tensor(
        out=ta[0:r, 0:width], in0=ta[0:r, 0:width], in1=tb[0:r, 0:width], op=ALU.min))
    q.vector.append(lambda e: e.tensor_tensor(
        out=ta[0:r, 0:width], in0=ta[0:r, 0:width], in1=tc[0:r, 0:width], op=ALU.min))
    q.vector.append(lambda e: e.tensor_scalar(
        out=ts_[0:r, 0:width], in0=src, scalar1=int(12 - 4 * kpos), scalar2=int(-0x8000),
        op0=ALU.logical_shift_left, op1=ALU.bitwise_and))
    q.vector.append(lambda e: e.tensor_tensor(
        out=ta[0:r, 0:width], in0=ta[0:r, 0:width], in1=ts_[0:r, 0:width], op=ALU.bitwise_or))
    q.vector.append(lambda e: e.tensor_mul(
        wpl[0:r, 0:width], ta[0:r, 0:width].bitcast(BF16), sfull_ap).then_inc(sV.h, 1))
    return sV.inc()




def make_deq_gu_feeder(nc, q, ph, sems, d_pgu, d_sgu_bf, d_plgu):
    sL, sGd, sSt, sV = sems
    pk32 = ph.enter_context(nc.sbuf_tensor("t_pk32", [128, OGU], I32))
    nib = ph.enter_context(nc.sbuf_tensor("t_nib", [128, 704], I16))
    ta = ph.enter_context(nc.sbuf_tensor("t_ta6", [128, 704], I16))
    tb = ph.enter_context(nc.sbuf_tensor("t_tb6", [128, 704], I16))
    tc = ph.enter_context(nc.sbuf_tensor("t_tc6", [128, 704], I16))
    ts_ = ph.enter_context(nc.sbuf_tensor("t_ts6", [128, 704], I16))
    sfull = ph.enter_context(nc.sbuf_tensor("t_sf6", [128, 704], BF16))
    wpl = ph.enter_context(nc.sbuf_tensor("t_wpl", [128, 704], BF16))
    st = {"i": 0, "prev_spill": 0, "loaded_jb": -1}
    order = [(jb, half, kp, kk) for jb in range(2) for half in range(2)
             for kp in range(8) for kk in range(8)]

    def feed(n):
        for _ in range(n):
            if st["i"] >= len(order):
                return
            jb, half, kp, kk = order[st["i"]]
            st["i"] += 1
            if jb != st["loaded_jb"]:
                if jb > 0:
                    q.sync.append(W(sSt, sSt.n))
                q.sync.append(lambda e, jb=jb: e.dma_start(
                    out=pk32[:, :], in_=d_pgu[jb * 128:(jb + 1) * 128, :]).then_inc(sL.h, 16))
                sL.n += 16
                st["vl"] = sL.n
                st["loaded_jb"] = jb
            if kk == 0:
                if st["i"] > 1:
                    q.gpsimd.append(W(sV, sV.n))
                for g in range(8):
                    q.gpsimd.append(lambda e, jb=jb, half=half, kp=kp, g=g: e.dma_start(
                        out=sfull[g * 16:(g + 1) * 16, :],
                        in_=bass.AP(d_sgu_bf.tensor, (jb * 8 + g) * OGU + half * IFF + kp * 704,
                                    [[0, 16], [1, 704]])).then_inc(sGd.h, 16))
                    sGd.n += 16
                st["vsg"] = sGd.n
                q.gpsimd.append(W(sGd, st["vsg"]))
            h16 = kk // 4
            kpos = kk % 4
            pk16 = pk32[:, :].bitcast(I16)
            base = 2 * (half * IFF + kp) + h16
            src = pk16[:, base: base + 16 * 703 + 1: 16]
            q.vector.append(W(sL, st["vl"]))
            q.vector.append(W(sGd, st["vsg"]))
            if st["prev_spill"]:
                q.vector.append(W(sSt, st["prev_spill"]))
            dequant_ops(q, sV, nib, ta, tb, tc, ts_, wpl, sfull[:, :], src, kpos, 128, 704)
            vw = sV.n
            pid = kk * 2 + jb
            q.sync.append(W(sV, vw))
            q.sync.append(lambda e, pid=pid, half=half, kp=kp: e.dma_start(
                out=d_plgu[pid, :, half * IFF + kp * 704: half * IFF + (kp + 1) * 704],
                in_=wpl[:, :]).then_inc(sSt.h, 16))
            sSt.n += 16
            st["prev_spill"] = sSt.n

    def remaining():
        return len(order) - st["i"]
    feed.remaining = remaining
    return feed


def make_deq_dn_feeder(nc, q, ph, sems, d_pdn, d_sdn_bf, d_pldn):
    sL, sGd, sSt, sV = sems
    pk32 = ph.enter_context(nc.sbuf_tensor("t_pk32d", [128, 6 * H], I32))
    nib = ph.enter_context(nc.sbuf_tensor("t_nibd", [128, H], I16))
    ta = ph.enter_context(nc.sbuf_tensor("t_tad", [128, H], I16))
    tb = ph.enter_context(nc.sbuf_tensor("t_tbd", [128, H], I16))
    tc = ph.enter_context(nc.sbuf_tensor("t_tcd", [128, H], I16))
    ts_ = ph.enter_context(nc.sbuf_tensor("t_tsd", [128, H], I16))
    sfull = ph.enter_context(nc.sbuf_tensor("t_sfd", [128, 6 * H], BF16))
    wpl = ph.enter_context(nc.sbuf_tensor("t_wpld", [128, H], BF16))
    st = {"i": 0, "prev_spill": 0, "init": False}
    order = [(kp, pt_) for kp in range(8) for pt_ in range(6)]

    def feed(n):
        if not st["init"]:
            st["init"] = True
            q.sync.append(lambda e: e.dma_start(
                out=pk32[:, 0:5 * H].rearrange("p (t h) -> p t h", t=5),
                in_=d_pdn[0:640, :].rearrange("(t p) h -> p t h", p=128)).then_inc(sL.h, 16))
            sL.n += 16
            q.sync.append(lambda e: e.dma_start(
                out=pk32[0:64, 5 * H:6 * H], in_=d_pdn[640:704, :]).then_inc(sL.h, 16))
            sL.n += 16
            st["vl"] = sL.n
            for pt_ in range(6):
                ng = 8 if pt_ < 5 else 4
                for g in range(ng):
                    q.gpsimd.append(lambda e, pt_=pt_, g=g: e.dma_start(
                        out=sfull[g * 16:(g + 1) * 16, pt_ * H:(pt_ + 1) * H],
                        in_=bass.AP(d_sdn_bf.tensor, (pt_ * 8 + g) * H, [[0, 16], [1, H]])).then_inc(sGd.h, 16))
                    sGd.n += 16
            q.gpsimd.append(W(sGd, sGd.n))
            st["vg"] = sGd.n
        for _ in range(n):
            if st["i"] >= len(order):
                return
            kp, pt_ = order[st["i"]]
            st["i"] += 1
            rows = 128 if pt_ < 5 else 64
            pk16 = pk32[:, :].bitcast(I16)
            h16 = kp // 4
            kpos = kp % 4
            src = pk16[0:rows, 2 * pt_ * H + h16: 2 * pt_ * H + h16 + 2 * (H - 1) + 1: 2]
            q.vector.append(W(sL, st["vl"]))
            q.vector.append(W(sGd, st["vg"]))
            if st["prev_spill"]:
                q.vector.append(W(sSt, st["prev_spill"]))
            dequant_ops(q, sV, nib, ta, tb, tc, ts_, wpl,
                        sfull[0:rows, pt_ * H:(pt_ + 1) * H], src, kpos, rows, H)
            vw = sV.n
            q.sync.append(W(sV, vw))
            q.sync.append(lambda e, kp=kp, pt_=pt_, rows=rows: e.dma_start(
                out=d_pldn[kp * 704 + pt_ * 128: kp * 704 + pt_ * 128 + rows, :],
                in_=wpl[0:rows, :]).then_inc(sSt.h, 16))
            sSt.n += 16
            st["prev_spill"] = sSt.n

    def remaining():
        return len(order) - st["i"]
    feed.remaining = remaining
    return feed


PHASE_ORDER = ["ph0", "ph0b", "ph1", "ph2", "ph3", "ph4",
               "ph5", "ph6a", "ph6b", "ph6c", "ph7"]


def dequant_ops_col(q, sV, nib, ta, tb, tc, ts_, wpl, sfull_ap, src, shc, rows, width):
    r = rows
    q.vector.append(lambda e: e.tensor_single_scalar(
        out=nib[0:r, 0:width], in_=src, scalar=shc[0:r, 0:1],
        op=ALU.logical_shift_right))
    q.vector.append(lambda e: e.tensor_single_scalar(
        out=nib[0:r, 0:width], in_=nib[0:r, 0:width], scalar=int(7),
        op=ALU.bitwise_and))
    q.vector.append(lambda e: e.tensor_scalar(
        out=ta[0:r, 0:width], in0=nib[0:r, 0:width], scalar1=int(2), scalar2=int(0x3F00),
        op0=ALU.min, op1=ALU.mult))
    q.vector.append(lambda e: e.tensor_scalar(
        out=tb[0:r, 0:width], in0=nib[0:r, 0:width], scalar1=int(0x80), scalar2=int(0x3E80),
        op0=ALU.mult, op1=ALU.add))
    q.vector.append(lambda e: e.tensor_scalar(
        out=tc[0:r, 0:width], in0=nib[0:r, 0:width], scalar1=int(0x40), scalar2=int(0x3F00),
        op0=ALU.mult, op1=ALU.add))
    q.vector.append(lambda e: e.tensor_tensor(
        out=ta[0:r, 0:width], in0=ta[0:r, 0:width], in1=tb[0:r, 0:width], op=ALU.min))
    q.vector.append(lambda e: e.tensor_tensor(
        out=ta[0:r, 0:width], in0=ta[0:r, 0:width], in1=tc[0:r, 0:width], op=ALU.min))
    q.vector.append(lambda e: e.tensor_single_scalar(
        out=ts_[0:r, 0:width], in_=src, scalar=shc[0:r, 1:2],
        op=ALU.logical_shift_left))
    q.vector.append(lambda e: e.tensor_single_scalar(
        out=ts_[0:r, 0:width], in_=ts_[0:r, 0:width], scalar=int(-0x8000),
        op=ALU.bitwise_and))
    q.vector.append(lambda e: e.tensor_tensor(
        out=ta[0:r, 0:width], in0=ta[0:r, 0:width], in1=ts_[0:r, 0:width], op=ALU.bitwise_or))
    q.vector.append(lambda e: e.tensor_mul(
        wpl[0:r, 0:width], ta[0:r, 0:width].bitcast(BF16), sfull_ap).then_inc(sV.h, 1))
    return sV.inc()


def build_program(taps=(), upto="ph7"):
    _lim = PHASE_ORDER.index(upto)

    def _skip(name):
        return PHASE_ORDER.index(name) > _lim

    nc = bass.Bass("TRN2", target_bir_lowering=False, detect_race_conditions=False)
    open_stacks = []

    def _stop():
        for st_ in reversed(open_stacks):
            st_.close()
        return nc

    d_x = nc.dram_tensor("x", [L, H], F32, kind="ExternalInput").ap()
    d_win = nc.dram_tensor("w_in", [H, 2 * E], F32, kind="ExternalInput").ap()
    d_ck = nc.dram_tensor("conv_k", [E, 4], F32, kind="ExternalInput").ap()
    d_cb = nc.dram_tensor("conv_b", [128, NET], F32, kind="ExternalInput").ap()
    d_wx = nc.dram_tensor("w_x", [E, 160], F32, kind="ExternalInput").ap()
    d_wdt = nc.dram_tensor("w_dt", [TR, E], F32, kind="ExternalInput").ap()
    d_bdt = nc.dram_tensor("b_dt", [128, NET], F32, kind="ExternalInput").ap()
    d_wout = nc.dram_tensor("w_out", [E, H], F32, kind="ExternalInput").ap()
    d_alog = nc.dram_tensor("a_log", [E, NST], F32, kind="ExternalInput").ap()
    d_D = nc.dram_tensor("dvec", [128, NET], F32, kind="ExternalInput").ap()
    d_pguh = nc.dram_tensor("pgu_h", [256, OGU], I16, kind="ExternalInput").ap()
    d_sgu = nc.dram_tensor("s_gu", [16, OGU], F16, kind="ExternalInput").ap()
    d_pdnh = nc.dram_tensor("pdn_h", [704, H], I16, kind="ExternalInput").ap()
    d_dqsh = nc.dram_tensor("dqsh", [128, 2], I32, kind="ExternalInput").ap()
    d_sdn = nc.dram_tensor("s_dn", [44, H], F16, kind="ExternalInput").ap()
    d_id = nc.dram_tensor("ident", [128, 128], F32, kind="ExternalInput").ap()

    d_u = nc.dram_tensor("sc_u", [E, L], BF16, kind="Internal").ap()
    d_gate = nc.dram_tensor("sc_gate", [E, L], BF16, kind="Internal").ap()
    d_bcbf = nc.dram_tensor("sc_bc", [32, L], BF16, kind="Internal").ap()
    d_yh0 = nc.dram_tensor("sc_yh0", [E, L], BF16, kind="Internal").ap()
    d_yg = nc.dram_tensor("sc_yg", [E, L], BF16, kind="Internal").ap()
    d_attn = nc.dram_tensor("sc_attn", [L, H], F32, kind="Internal").ap()
    d_x1 = nc.dram_tensor("sc_x1", [L, H], F32, kind="Internal").ap()
    d_sgu_bf = nc.dram_tensor("sc_sgu", [16, OGU], BF16, kind="Internal").ap()
    d_sdn_bf = nc.dram_tensor("sc_sdn", [44, H], BF16, kind="Internal").ap()
    d_plgu = nc.dram_tensor("sc_plgu", [16, 128, OGU], BF16, kind="Internal").ap()
    d_plgu_loc = nc.dram_tensor("sc_plgu_loc", [256, OGU], BF16, kind="Internal").ap()
    d_pldn_loc = nc.dram_tensor("sc_pldn_loc", [704, H], BF16, kind="Internal").ap()
    d_hact = nc.dram_tensor("sc_hact", [IFF, L], BF16, kind="Internal").ap()
    d_pldn = nc.dram_tensor("sc_pldn", [IFF, H], BF16, kind="Internal").ap()
    d_ffn = nc.dram_tensor("sc_ffn", [L, H], F32, kind="Internal").ap()

    d_out = nc.dram_tensor("out", [L, H], F32, kind="ExternalOutput").ap()
    d_taps = {}
    for t in taps:
        shp = {"xlnT": [128, NKT * L], "u": [E, L], "gate": [E, L], "dtT": [TR, L],
               "bc": [32, L], "yg": [E, L], "attn": [L, H], "x1": [L, H],
               "hact": [IFF, L], "ffn": [L, H], "plgu": [16 * 128, OGU],
               "xtp": [128, 16 * L]}[t]
        dt = F32 if t in ("attn", "x1", "ffn") else BF16
        d_taps[t] = nc.dram_tensor("tap_" + t, shp, dt, kind="ExternalOutput").ap()

    stack = ExitStack()
    with stack:
        sL = Sem(stack, nc, "sL")
        sGd = Sem(stack, nc, "sGd")
        sSt = Sem(stack, nc, "sSt")
        sP = Sem(stack, nc, "sP")
        sA = Sem(stack, nc, "sA")
        sV = Sem(stack, nc, "sV")
        sA2 = Sem(stack, nc, "sA2")
        sV2 = Sem(stack, nc, "sV2")
        sCC = Sem(stack, nc, "sCC")

        sb_id = stack.enter_context(nc.sbuf_tensor("c_id", [128, 128], BF16))
        sb_aneg = stack.enter_context(nc.sbuf_tensor("c_aneg", [128, NET * NST], F32))
        sb_D = stack.enter_context(nc.sbuf_tensor("c_D", [128, NET], F32))
        sb_cb = stack.enter_context(nc.sbuf_tensor("c_cb", [128, NET], F32))
        sb_bdt = stack.enter_context(nc.sbuf_tensor("c_bdt", [128, NET], F32))
        sb_ck = stack.enter_context(nc.sbuf_tensor("c_ck", [128, NET * 4], F32))
        sb_wdt = stack.enter_context(nc.sbuf_tensor("c_wdt", [128, E], BF16))
        sb_wx = stack.enter_context(nc.sbuf_tensor("c_wx", [128, NET * 160], BF16))
        sb_dtT = stack.enter_context(nc.sbuf_tensor("p_dtT", [128, L], BF16))

        q = Q()
        ph0 = ExitStack()
        sb_alog = ph0.enter_context(nc.sbuf_tensor("t_alog", [128, NET * NST], F32))

        def _ph0_sync(e):
            e.dma_start(out=sb_alog[:, :].rearrange("p (t n) -> p t n", n=NST), in_=d_alog.rearrange("(t p) n -> p t n", p=128)).then_inc(sL.h, 16)
            e.dma_start(out=sb_D[:, :], in_=d_D).then_inc(sL.h, 16)
            e.dma_start(out=sb_cb[:, :], in_=d_cb).then_inc(sL.h, 16)
            e.dma_start(out=sb_bdt[:, :], in_=d_bdt).then_inc(sL.h, 16)
            e.dma_start(out=sb_ck[:, :].rearrange("p (t k) -> p t k", k=4), in_=d_ck.rearrange("(t p) k -> p t k", p=128)).then_inc(sL.h, 16)
        sL.n += 5 * 16
        q.sync.append(_ph0_sync)

        def _ph0_gp(e):
            e.dma_start(out=sb_id[:, :], in_=d_id).then_inc(sGd.h, 16)
            e.dma_start(out=sb_wdt[:, :], in_=d_wdt).then_inc(sGd.h, 16)
            e.dma_start(out=sb_wx[:, :].rearrange("p (t d) -> p t d", d=160), in_=d_wx.rearrange("(t p) d -> p t d", p=128)).then_inc(sGd.h, 16)
            e.dma_start(out=d_sgu_bf, in_=d_sgu).then_inc(sGd.h, 16)
            e.dma_start(out=d_sdn_bf, in_=d_sdn).then_inc(sGd.h, 16)
            e.wait_ge(sGd.h, 80)
        sGd.n += 5 * 16
        q.gpsimd.append(_ph0_gp)

        q.scalar.append(W(sL, sL.n))
        q.scalar.append(lambda e: e.activation(out=sb_aneg[:, :], in_=sb_alog[:, :],
                                               func=AF.Exp).then_inc(sA.h, 1))
        va = sA.inc()
        q.vector.append(W(sA, va))
        q.vector.append(lambda e: e.tensor_scalar_mul(sb_aneg[:, :], sb_aneg[:, :], -1.0).then_inc(sV.h, 1))
        sV.inc()
        emit_block(nc, q)
        ph0.close()


        q = Q()
        with ExitStack() as ph:
            pkg = ph.enter_context(nc.sbuf_tensor("t_pkg", [128, OGU], I16))
            pkd = ph.enter_context(nc.sbuf_tensor("t_pkd", [128, 6 * H], I16))
            shc = ph.enter_context(nc.sbuf_tensor("t_shc", [128, 2], I32))
            nib = ph.enter_context(nc.sbuf_tensor("t_nibq", [128, H], I16))
            ta = ph.enter_context(nc.sbuf_tensor("t_taq", [128, H], I16))
            tb = ph.enter_context(nc.sbuf_tensor("t_tbq", [128, H], I16))
            tc = ph.enter_context(nc.sbuf_tensor("t_tcq", [128, H], I16))
            ts_ = ph.enter_context(nc.sbuf_tensor("t_tsq", [128, H], I16))
            sfull = ph.enter_context(nc.sbuf_tensor("t_sfq", [128, 6 * H], BF16))
            wpl = ph.enter_context(nc.sbuf_tensor("t_wplq", [128, H], BF16))

            q.sync.append(lambda e: e.dma_start(out=shc[:, :], in_=d_dqsh).then_inc(sL.h, 16))
            sL.n += 16
            prev_spill = 0
            for jb in range(2):
                if jb > 0:
                    q.sync.append(W(sSt, sSt.n))
                q.sync.append(lambda e, jb=jb: e.dma_start(
                    out=pkg[:, :], in_=d_pguh[jb * 128:(jb + 1) * 128, :]).then_inc(sL.h, 16))
                sL.n += 16
                vl = sL.n
                for half in range(2):
                    for kp in range(8):
                        if not (jb == 0 and half == 0 and kp == 0):
                            q.gpsimd.append(W(sV, sV.n))
                        for g in range(8):
                            q.gpsimd.append(lambda e, jb=jb, half=half, kp=kp, g=g: e.dma_start(
                                out=sfull[g * 16:(g + 1) * 16, 0:704],
                                in_=bass.AP(d_sgu_bf.tensor, (jb * 8 + g) * OGU + half * IFF + kp * 704,
                                            [[0, 16], [1, 704]])).then_inc(sGd.h, 16))
                            sGd.n += 16
                        vsg = sGd.n
                        q.gpsimd.append(W(sGd, vsg))
                        src = pkg[:, half * IFF + kp * 704: half * IFF + (kp + 1) * 704]
                        q.vector.append(W(sL, vl))
                        q.vector.append(W(sGd, vsg))
                        if prev_spill:
                            q.vector.append(W(sSt, prev_spill))
                        dequant_ops_col(q, sV, nib, ta, tb, tc, ts_, wpl,
                                        sfull[:, 0:704], src, shc, 128, 704)
                        vw = sV.n
                        q.sync.append(W(sV, vw))
                        q.sync.append(lambda e, jb=jb, half=half, kp=kp: e.dma_start(
                            out=d_plgu_loc[jb * 128:(jb + 1) * 128,
                                           half * IFF + kp * 704: half * IFF + (kp + 1) * 704],
                            in_=wpl[:, 0:704]).then_inc(sSt.h, 16))
                        sSt.n += 16
                        prev_spill = sSt.n
            q.sync.append(lambda e: e.dma_start(
                out=pkd[:, 0:5 * H].rearrange("p (t h) -> p t h", t=5),
                in_=d_pdnh[0:640, :].rearrange("(t p) h -> p t h", p=128)).then_inc(sL.h, 16))
            sL.n += 16
            q.sync.append(lambda e: e.dma_start(
                out=pkd[0:64, 5 * H:6 * H], in_=d_pdnh[640:704, :]).then_inc(sL.h, 16))
            sL.n += 16
            vld = sL.n
            for pt_ in range(6):
                ng = 8 if pt_ < 5 else 4
                q.gpsimd.append(W(sV, sV.n))
                for g in range(ng):
                    q.gpsimd.append(lambda e, pt_=pt_, g=g: e.dma_start(
                        out=sfull[g * 16:(g + 1) * 16, pt_ * H:(pt_ + 1) * H],
                        in_=bass.AP(d_sdn_bf.tensor, (pt_ * 8 + g) * H, [[0, 16], [1, H]])).then_inc(sGd.h, 16))
                    sGd.n += 16
                vsg = sGd.n
                q.gpsimd.append(W(sGd, vsg))
                rows = 128 if pt_ < 5 else 64
                src = pkd[0:rows, pt_ * H:(pt_ + 1) * H]
                q.vector.append(W(sL, vld))
                q.vector.append(W(sGd, vsg))
                if prev_spill:
                    q.vector.append(W(sSt, prev_spill))
                dequant_ops_col(q, sV, nib, ta, tb, tc, ts_, wpl,
                                sfull[0:rows, pt_ * H:(pt_ + 1) * H], src, shc, rows, H)
                vw = sV.n
                q.sync.append(W(sV, vw))
                q.sync.append(lambda e, pt_=pt_, rows=rows: e.dma_start(
                    out=d_pldn_loc[pt_ * 128: pt_ * 128 + rows, :],
                    in_=wpl[0:rows, :]).then_inc(sSt.h, 16))
                sSt.n += 16
                prev_spill = sSt.n
            q.gpsimd.append(W(sSt, sSt.n))
            q.gpsimd.append(lambda e: e.collective_compute(
                "AllGather", ALU.bypass, replica_groups=[list(range(8))],
                ins=[d_plgu_loc.opt()], outs=[d_plgu.rearrange("pid p c -> (pid p) c").opt()]).then_inc(sCC.h))
            sCC.n += 1
            q.gpsimd.append(lambda e: e.collective_compute(
                "AllGather", ALU.bypass, replica_groups=[list(range(8))],
                ins=[d_pldn_loc.opt()], outs=[d_pldn.opt()]).then_inc(sCC.h))
            sCC.n += 1
            q.gpsimd.append(W(sCC, 2))
            q.sync.append(W(sSt, sSt.n))
            if "plgu" in taps:
                q.sync.append(W(sCC, 1))
                q.sync.append(lambda e: e.dma_start(out=d_taps["plgu"], in_=d_plgu.rearrange("pid p c -> (pid p) c")).then_inc(sSt.h, 16))
                sSt.n += 16
                q.sync.append(W(sSt, sSt.n))
            emit_block(nc, q)

        if _skip("ph0b"):
            return _stop()
        es1 = ExitStack(); open_stacks.append(es1)
        sb_xlnT = es1.enter_context(nc.sbuf_tensor("p_xlnT", [128, NKT, L], BF16))
        q = Q()
        with ExitStack() as ph:
            xt = [ph.enter_context(nc.sbuf_tensor(f"t_x{i}", [128, H], F32)) for i in range(2)]
            xln = [ph.enter_context(nc.sbuf_tensor(f"t_xl{i}", [128, H], BF16)) for i in range(2)]
            scr = ph.enter_context(nc.sbuf_tensor("t_scr", [128, H], F32))
            c1 = ph.enter_context(nc.sbuf_tensor("t_c1", [128, 8], F32))
            pt = [ph.enter_context(nc.psum_tensor(f"t_pt{i}", [128, 1024], BF16)) for i in range(2)]

            norm_done = {}
            evac_done = {}
            gi = 0
            for lt in range(NLT):
                pr = lt % 2
                if lt >= 2:
                    q.sync.append(W(sV, norm_done[lt - 2]))
                q.sync.append(lambda e, lt=lt, pr=pr: e.dma_start(
                    out=xt[pr][:, :], in_=d_x[lt * 128:(lt + 1) * 128, :]).then_inc(sL.h, 16))
                sL.n += 16
                vl = sL.n

                q.scalar.append(W(sL, vl))
                va = ln_stats(q, sA, sV, (lambda pr=pr: xt[pr][:, :]), scr, c1, 1e-5)
                q.vector.append(W(sA, va))
                if lt >= 2:
                    q.vector.append(W(sP, sP.n))
                q.vector.append(lambda e, pr=pr: e.tensor_scalar(
                    out=xln[pr][:, :], in0=xt[pr][:, :], scalar1=c1[:, 2:3], scalar2=c1[:, 7:8],
                    op0=ALU.subtract, op1=ALU.mult).then_inc(sV.h, 1))
                norm_done[lt] = sV.inc()

                import os as _os
                if _os.environ.get("K_PH0B_CUT") == "1":
                    continue
                _noevac = _os.environ.get("K_PH0B_NOEVAC") == "1"
                for g in range(4):
                    gpr = gi % 2
                    q.tensor.append(W(sV, norm_done[lt]))
                    if gi >= 2:
                        q.tensor.append(W(sV2, evac_done[gi - 2]))
                    for hb in range(4):
                        ht = g * 4 + hb
                        q.tensor.append(lambda e, pr=pr, gpr=gpr, ht=ht, hb=hb: (
                            e.transpose(pt[gpr][:, hb * 128:(hb + 1) * 128],
                                        xln[pr][:, ht * 128:(ht + 1) * 128],
                                        sb_id[:, :]).then_inc(sP.h, 1) if hb == 3 else
                            e.transpose(pt[gpr][:, hb * 128:(hb + 1) * 128],
                                        xln[pr][:, ht * 128:(ht + 1) * 128], sb_id[:, :])))
                    vp = sP.inc()
                    q.vector.append(W(sP, vp))
                    if _noevac:
                        q.vector.append(lambda e: e.engine_nop().then_inc(sV2.h, 1))
                    else:
                        for hb in range(4):
                            ht = g * 4 + hb
                            last = hb == 3
                            q.vector.append(lambda e, gpr=gpr, hb=hb, ht=ht, lt=lt, last=last: (
                                e.tensor_copy(out=sb_xlnT[:, ht, lt * 128:(lt + 1) * 128],
                                              in_=pt[gpr][:, hb * 128:(hb + 1) * 128]).then_inc(sV2.h, 1)
                                if last else
                                e.tensor_copy(out=sb_xlnT[:, ht, lt * 128:(lt + 1) * 128],
                                              in_=pt[gpr][:, hb * 128:(hb + 1) * 128])))
                    evac_done[gi] = sV2.inc()
                    gi += 1
            if "xlnT" in taps:
                q.sync.append(W(sV2, sV2.n))
                q.sync.append(lambda e: e.dma_start(
                    out=d_taps["xlnT"], in_=sb_xlnT[:, :, :].rearrange("p a b -> p (a b)")).then_inc(sSt.h, 16))
                sSt.n += 16
                q.sync.append(W(sSt, sSt.n))
            emit_block(nc, q)

        if _skip("ph1"):
            return _stop()
        q = Q()
        with ExitStack() as ph:
            winb = [ph.enter_context(nc.sbuf_tensor(f"t_win{i}", [128, NKT * 128], BF16)) for i in range(2)]
            hpad = ph.enter_context(nc.sbuf_tensor("t_hpad", [128, L + 8], BF16))
            diag = ph.enter_context(nc.sbuf_tensor("t_diag", [128, 4 * 128], BF16))
            ubuf = ph.enter_context(nc.sbuf_tensor("t_ubuf", [128, L], BF16))
            gbuf = ph.enter_context(nc.sbuf_tensor("t_gbuf", [128, L], BF16))
            pps = ph.enter_context(nc.psum_tensor("t_pps", [128, L], F32))
            cps = ph.enter_context(nc.psum_tensor("t_cps", [128, L], F32))

            q.vector.append(lambda e: e.memset(hpad[:, 0:8], 0.0).then_inc(sV.h, 1))
            sV.inc()

            proj_done = {}
            for et in range(64):
                pr = et % 2
                hidden = et < NET
                if et >= 2:
                    q.gpsimd.append(W(sP, proj_done[et - 2]))
                q.gpsimd.append(lambda e, et=et, pr=pr: e.dma_start(
                    out=winb[pr][:, :].rearrange("p (t e) -> p t e", e=128),
                    in_=d_win[:, et * 128:(et + 1) * 128].rearrange("(t p) e -> p t e", p=128)).then_inc(sGd.h, 16))
                sGd.n += 16
                vg = sGd.n
                q.gpsimd.append(W(sGd, vg))

                q.tensor.append(W(sGd, vg))
                if et >= 1:
                    q.tensor.append(W(sA, sA.n))
                for lt in range(4):
                    for kt in range(NKT):
                        fin = (lt == 3 and kt == NKT - 1)
                        q.tensor.append(lambda e, pr=pr, lt=lt, kt=kt, fin=fin: (
                            e.matmul(pps[:, lt * 512:(lt + 1) * 512],
                                     winb[pr][:, kt * 128:(kt + 1) * 128],
                                     sb_xlnT[:, kt, lt * 512:(lt + 1) * 512],
                                     start=(kt == 0), stop=(kt == NKT - 1)
                                     ).then_inc(sP.h, 1) if fin else
                            e.matmul(pps[:, lt * 512:(lt + 1) * 512],
                                     winb[pr][:, kt * 128:(kt + 1) * 128],
                                     sb_xlnT[:, kt, lt * 512:(lt + 1) * 512],
                                     start=(kt == 0), stop=(kt == NKT - 1))))
                proj_done[et] = sP.inc()

                if hidden:
                    q.scalar.append(W(sP, proj_done[et]))
                    q.scalar.append(lambda e: e.activation(
                        out=hpad[:, 8:8 + L], in_=pps[:, :], func=AF.Copy).then_inc(sA.h, 1))
                    vh = sA.inc()
                    q.vector.append(W(sP, sP.n))
                    q.vector.append(lambda e, et=et: [e.tensor_scalar_mul(
                        diag[:, k * 128:(k + 1) * 128], sb_id[:, :],
                        sb_ck[:, et * 4 + k:et * 4 + k + 1]) for k in range(4)][-1].then_inc(sV.h, 1))
                    vd = sV.inc()
                    q.tensor.append(W(sA, vh))
                    q.tensor.append(W(sV, vd))
                    for lt in range(4):
                        for k in range(4):
                            fin = (lt == 3 and k == 3)
                            q.tensor.append(lambda e, lt=lt, k=k, fin=fin: (
                                e.matmul(cps[:, lt * 512:(lt + 1) * 512],
                                         diag[:, k * 128:(k + 1) * 128],
                                         hpad[:, 5 + k + lt * 512: 5 + k + lt * 512 + 512],
                                         start=(k == 0), stop=(k == 3)).then_inc(sP.h, 1) if fin else
                                e.matmul(cps[:, lt * 512:(lt + 1) * 512],
                                         diag[:, k * 128:(k + 1) * 128],
                                         hpad[:, 5 + k + lt * 512: 5 + k + lt * 512 + 512],
                                         start=(k == 0), stop=(k == 3))))
                    vconv = sP.inc()
                    q.scalar.append(W(sP, vconv))
                    if et >= 1:
                        q.scalar.append(W(sSt, sSt.n))
                    q.scalar.append(lambda e, et=et: e.activation(
                        out=ubuf[:, :], in_=cps[:, :], func=AF.Silu,
                        bias=sb_cb[:, et:et + 1]).then_inc(sA.h, 1))
                    vu = sA.inc()
                    q.sync.append(W(sA, vu))
                    q.sync.append(lambda e, et=et: e.dma_start(
                        out=d_u[et * 128:(et + 1) * 128, :], in_=ubuf[:, :]).then_inc(sSt.h, 16))
                    sSt.n += 16
                else:
                    ge = et - NET
                    q.scalar.append(W(sP, proj_done[et]))
                    if et >= NET + 1:
                        q.scalar.append(W(sSt, sSt.n))
                    q.scalar.append(lambda e: e.activation(
                        out=gbuf[:, :], in_=pps[:, :], func=AF.Silu).then_inc(sA.h, 1))
                    vgate = sA.inc()
                    q.sync.append(W(sA, vgate))
                    q.sync.append(lambda e, ge=ge: e.dma_start(
                        out=d_gate[ge * 128:(ge + 1) * 128, :], in_=gbuf[:, :]).then_inc(sSt.h, 16))
                    sSt.n += 16
            q.sync.append(W(sSt, sSt.n))
            emit_block(nc, q)
        es1.close(); open_stacks.remove(es1)

        if "u" in taps or "gate" in taps:
            q = Q()
            if "u" in taps:
                q.sync.append(lambda e: e.dma_start(out=d_taps["u"], in_=d_u).then_inc(sSt.h, 16))
                sSt.n += 16
            if "gate" in taps:
                q.sync.append(lambda e: e.dma_start(out=d_taps["gate"], in_=d_gate).then_inc(sSt.h, 16))
                sSt.n += 16
            q.sync.append(W(sSt, sSt.n))
            emit_block(nc, q)

        if _skip("ph2"):
            return _stop()
        q = Q()
        with ExitStack() as ph:
            ust = [ph.enter_context(nc.sbuf_tensor(f"t_ust{i}", [128, NET * 512], BF16)) for i in range(2)]
            bcf = ph.enter_context(nc.sbuf_tensor("t_bcf", [32, L], BF16))
            dps = [ph.enter_context(nc.psum_tensor(f"t_dps{i}", [128, 512], F32)) for i in range(2)]
            bps = [ph.enter_context(nc.psum_tensor(f"t_bps{i}", [32, 512], F32)) for i in range(2)]

            evdt = {}
            evbc = {}
            mm2_done = {}
            for lt in range(4):
                pr = lt % 2
                if lt >= 2:
                    q.sync.append(W(sP, mm2_done[lt - 2]))
                q.sync.append(lambda e, lt=lt, pr=pr: e.dma_start(
                    out=ust[pr][:, :].rearrange("p (t m) -> p t m", m=512),
                    in_=d_u[:, lt * 512:(lt + 1) * 512].rearrange("(t p) m -> p t m", p=128)).then_inc(sL.h, 16))
                sL.n += 16
                vl = sL.n
                q.tensor.append(W(sL, vl))
                if lt >= 2:
                    q.tensor.append(W(sA, evdt[lt - 2]))
                    q.tensor.append(W(sV, evbc[lt - 2]))
                for kt in range(NET):
                    q.tensor.append(lambda e, pr=pr, kt=kt: e.matmul(
                        dps[pr][:, :], sb_wx[:, kt * 160:kt * 160 + 128],
                        ust[pr][:, kt * 512:(kt + 1) * 512],
                        start=(kt == 0), stop=(kt == NET - 1)))
                for kt in range(NET):
                    fin = kt == NET - 1
                    q.tensor.append(lambda e, pr=pr, kt=kt, fin=fin: (
                        e.matmul(bps[pr][:, :], sb_wx[:, kt * 160 + 128:kt * 160 + 160],
                                 ust[pr][:, kt * 512:(kt + 1) * 512],
                                 start=(kt == 0), stop=(kt == NET - 1)).then_inc(sP.h, 1) if fin else
                        e.matmul(bps[pr][:, :], sb_wx[:, kt * 160 + 128:kt * 160 + 160],
                                 ust[pr][:, kt * 512:(kt + 1) * 512],
                                 start=(kt == 0), stop=(kt == NET - 1))))
                vp = sP.inc()
                mm2_done[lt] = vp
                q.scalar.append(W(sP, vp))
                q.scalar.append(lambda e, lt=lt, pr=pr: e.activation(
                    out=sb_dtT[:, lt * 512:(lt + 1) * 512], in_=dps[pr][:, :],
                    func=AF.Copy).then_inc(sA.h, 1))
                evdt[lt] = sA.inc()
                q.vector.append(W(sP, vp))
                q.vector.append(lambda e, lt=lt, pr=pr: e.tensor_copy(
                    out=bcf[:, lt * 512:(lt + 1) * 512], in_=bps[pr][:, :]).then_inc(sV.h, 1))
                evbc[lt] = sV.inc()
            q.sync.append(W(sV, sV.n))
            q.sync.append(lambda e: e.dma_start(out=d_bcbf, in_=bcf[:, :]).then_inc(sSt.h, 16))
            sSt.n += 16
            if "dtT" in taps:
                q.sync.append(W(sA, sA.n))
                q.sync.append(lambda e: e.dma_start(out=d_taps["dtT"], in_=sb_dtT[:, :]).then_inc(sSt.h, 16))
                sSt.n += 16
            if "bc" in taps:
                q.sync.append(lambda e: e.dma_start(out=d_taps["bc"], in_=bcf[:, :]).then_inc(sSt.h, 16))
                sSt.n += 16
            q.sync.append(W(sSt, sSt.n))
            emit_block(nc, q)

        if _skip("ph3"):
            return _stop()
        for half in range(2):
            q = Q()
            with ExitStack() as ph:
                bcb = ph.enter_context(nc.sbuf_tensor(f"t_bcb{half}", [128, 16, L], BF16))
                t1 = ph.enter_context(nc.sbuf_tensor(f"t_t1{half}", [128, L], F32))
                t2 = ph.enter_context(nc.sbuf_tensor(f"t_t2{half}", [128, L], F32))
                dts = ph.enter_context(nc.sbuf_tensor(f"t_dts{half}", [128, L], F32))
                du = ph.enter_context(nc.sbuf_tensor(f"t_du{half}", [128, L], BF16))
                uet = [ph.enter_context(nc.sbuf_tensor(f"t_uet{half}_{i}", [128, L], BF16)) for i in range(2)]
                dab = [ph.enter_context(nc.sbuf_tensor(f"t_dab{half}_{i}", [128, L], F32)) for i in range(2)]
                dbu = ph.enter_context(nc.sbuf_tensor(f"t_dbu{half}", [128, L], BF16))
                hh = ph.enter_context(nc.sbuf_tensor(f"t_hh{half}", [128, L], BF16))
                zb = [ph.enter_context(nc.sbuf_tensor(f"t_z{half}_{i}", [128, L], BF16)) for i in range(4)]
                yt = ph.enter_context(nc.sbuf_tensor(f"t_yt{half}", [128, L], F32))
                yg = ph.enter_context(nc.sbuf_tensor(f"t_yg{half}", [128, L], BF16))
                if half == 1:
                    get_ = [ph.enter_context(nc.sbuf_tensor(f"t_get{half}_{i}", [128, L], BF16)) for i in range(2)]
                    yh0b = [ph.enter_context(nc.sbuf_tensor(f"t_yh0{half}_{i}", [128, L], BF16)) for i in range(2)]
                sps = [ph.enter_context(nc.psum_tensor(f"t_sps{half}_{i}", [128, L], F32)) for i in range(2)]

                for i in range(8):
                    q.sync.append(lambda e, i=i, half=half: e.dma_start(
                        out=bcb[:, i, :],
                        in_=d_bcbf[half * 8 + i: half * 8 + i + 1, :].to_broadcast((128, L))).then_inc(sL.h, 16))
                    sL.n += 16
                    q.sync.append(lambda e, i=i, half=half: e.dma_start(
                        out=bcb[:, 8 + i, :],
                        in_=d_bcbf[16 + half * 8 + i: 16 + half * 8 + i + 1, :].to_broadcast((128, L))).then_inc(sL.h, 16))
                    sL.n += 16
                vbc = sL.n

                scan_base = sV2.n
                sidx = 0
                psum_free = {}
                yg_spill = {}
                for et in range(NET):
                    pr = et % 2
                    if et >= 2:
                        q.sync.append(W(sV, sV.n))
                    q.sync.append(lambda e, et=et, pr=pr: e.dma_start(
                        out=uet[pr][:, :], in_=d_u[et * 128:(et + 1) * 128, :]).then_inc(sL.h, 16))
                    sL.n += 16
                    if half == 1:
                        q.sync.append(lambda e, et=et, pr=pr: e.dma_start(
                            out=get_[pr][:, :], in_=d_gate[et * 128:(et + 1) * 128, :]).then_inc(sL.h, 16))
                        sL.n += 16
                        q.sync.append(lambda e, et=et, pr=pr: e.dma_start(
                            out=yh0b[pr][:, :], in_=d_yh0[et * 128:(et + 1) * 128, :]).then_inc(sL.h, 16))
                        sL.n += 16
                    vload = sL.n

                    if et >= 2:
                        q.tensor.append(W(sA, psum_free[et - 2]))
                    for lt in range(4):
                        fin = lt == 3
                        q.tensor.append(lambda e, pr=pr, et=et, lt=lt, fin=fin: (
                            e.matmul(sps[pr][:, lt * 512:(lt + 1) * 512],
                                     sb_wdt[:, et * 128:(et + 1) * 128],
                                     sb_dtT[:, lt * 512:(lt + 1) * 512],
                                     start=True, stop=True).then_inc(sP.h, 1) if fin else
                            e.matmul(sps[pr][:, lt * 512:(lt + 1) * 512],
                                     sb_wdt[:, et * 128:(et + 1) * 128],
                                     sb_dtT[:, lt * 512:(lt + 1) * 512],
                                     start=True, stop=True)))
                    vdts = sP.inc()

                    q.scalar.append(W(sP, vdts))
                    if et >= 1:
                        q.scalar.append(W(sV, sV.n))
                    q.scalar.append(lambda e, pr=pr, et=et: e.activation(
                        out=t1[:, :], in_=sps[pr][:, :], func=AF.Abs, bias=sb_bdt[:, et:et + 1]))
                    q.scalar.append(lambda e, pr=pr, et=et: e.activation(
                        out=t2[:, :], in_=sps[pr][:, :], func=AF.Relu,
                        bias=sb_bdt[:, et:et + 1]).then_inc(sA.h, 1))
                    psum_free[et] = sA.inc()
                    q.scalar.append(lambda e: e.activation(out=t1[:, :], in_=t1[:, :], func=AF.Exp, scale=-1.0))
                    q.scalar.append(lambda e: e.activation(out=t1[:, :], in_=t1[:, :], func=AF.Ln,
                                                           bias=1.0).then_inc(sA.h, 1))
                    vsp = sA.inc()

                    q.vector.append(W(sA, vsp))
                    q.vector.append(W(sL, vload))
                    q.vector.append(lambda e: e.tensor_add(dts[:, :], t1[:, :], t2[:, :]).then_inc(sV.h, 1))
                    vdtsd = sV.inc()
                    q.vector.append(lambda e, pr=pr: e.tensor_mul(du[:, :], dts[:, :], uet[pr][:, :]))

                    for nn in range(8):
                        n = half * 8 + nn
                        npr = sidx % 2
                        q.scalar.append(W(sV, vdtsd))
                        if sidx >= 2:
                            q.scalar.append(W(sV2, scan_base + sidx - 1))
                        q.scalar.append(lambda e, npr=npr, et=et, n=n: e.activation(
                            out=dab[npr][:, :], in_=dts[:, :], func=AF.Exp,
                            scale=sb_aneg[:, et * NST + n:et * NST + n + 1]).then_inc(sA2.h, 1))
                        vda = sA2.inc()
                        q.vector.append(W(sA2, vda))
                        if et == 0 and nn == 0:
                            q.vector.append(W(sL, vbc))
                        q.vector.append(lambda e, nn=nn: e.tensor_mul(dbu[:, :], du[:, :], bcb[:, nn, :]))
                        q.vector.append(lambda e, npr=npr: e.tensor_tensor_scan(
                            out=hh[:, :], data0=dab[npr][:, :], data1=dbu[:, :],
                            initial=0.0, op0=ALU.mult, op1=ALU.add).then_inc(sV2.h, 1))
                        sV2.inc()
                        sidx += 1
                        q.vector.append(lambda e, nn=nn: e.tensor_mul(zb[nn % 4][:, :], hh[:, :], bcb[:, 8 + nn, :]))
                        if nn % 2 == 1:
                            q.vector.append(lambda e, nn=nn: e.tensor_add(
                                zb[nn - 1 if nn < 4 else nn - 5][:, :],
                                zb[(nn - 1) % 4][:, :], zb[nn % 4][:, :]))
                        if nn == 3:
                            q.vector.append(lambda e: e.tensor_add(yt[:, :], zb[0][:, :], zb[2][:, :]))
                        if nn == 7:
                            q.vector.append(lambda e: e.tensor_add(zb[0][:, :], zb[0][:, :], zb[2][:, :]))
                            q.vector.append(lambda e: e.tensor_add(yt[:, :], yt[:, :], zb[0][:, :]))

                    if half == 0:
                        if et >= 1:
                            q.vector.append(W(sSt, yg_spill[et - 1]))
                        q.vector.append(lambda e: e.tensor_copy(out=yg[:, :], in_=yt[:, :]).then_inc(sV.h, 1))
                        vyh = sV.inc()
                        q.sync.append(W(sV, vyh))
                        q.sync.append(lambda e, et=et: e.dma_start(
                            out=d_yh0[et * 128:(et + 1) * 128, :], in_=yg[:, :]).then_inc(sSt.h, 16))
                        sSt.n += 16
                        yg_spill[et] = sSt.n
                    else:
                        q.vector.append(lambda e, pr=pr: e.tensor_add(yt[:, :], yt[:, :], yh0b[pr][:, :]))
                        q.vector.append(lambda e, pr=pr, et=et: e.scalar_tensor_tensor(
                            out=yt[:, :], in0=uet[pr][:, :], scalar=sb_D[:, et:et + 1], in1=yt[:, :],
                            op0=ALU.mult, op1=ALU.add))
                        if et >= 1:
                            q.vector.append(W(sSt, yg_spill[et - 1]))
                        q.vector.append(lambda e, pr=pr: e.tensor_mul(
                            yg[:, :], yt[:, :], get_[pr][:, :]).then_inc(sV.h, 1))
                        vygd = sV.inc()
                        q.sync.append(W(sV, vygd))
                        q.sync.append(lambda e, et=et: e.dma_start(
                            out=d_yg[et * 128:(et + 1) * 128, :], in_=yg[:, :]).then_inc(sSt.h, 16))
                        sSt.n += 16
                        yg_spill[et] = sSt.n
                q.sync.append(W(sSt, sSt.n))
                emit_block(nc, q)

        if "yg" in taps:
            q = Q()
            q.sync.append(lambda e: e.dma_start(out=d_taps["yg"], in_=d_yg).then_inc(sSt.h, 16))
            sSt.n += 16
            q.sync.append(W(sSt, sSt.n))
            emit_block(nc, q)

        if _skip("ph4"):
            return _stop()
        q = Q()
        with ExitStack() as ph:
            wob = ph.enter_context(nc.sbuf_tensor("t_wob", [128, NET * 512], BF16))
            yst = [ph.enter_context(nc.sbuf_tensor(f"t_yst{i}", [128, NET * 128], BF16)) for i in range(2)]
            aev = [ph.enter_context(nc.sbuf_tensor(f"t_aev{i}", [128, 512], F32)) for i in range(2)]
            ops_ = [ph.enter_context(nc.psum_tensor(f"t_ops{i}", [128, 512], F32)) for i in range(2)]

            ev_done = {}
            mm_done = {}
            for htl in range(4):
                if htl > 0:
                    q.gpsimd.append(W(sP, sP.n))
                q.gpsimd.append(lambda e, htl=htl: e.dma_start(
                    out=wob[:, :].rearrange("p (t h) -> p t h", h=512),
                    in_=d_wout[:, htl * 512:(htl + 1) * 512].rearrange("(t p) h -> p t h", p=128)).then_inc(sGd.h, 16))
                sGd.n += 16
                vw = sGd.n
                q.gpsimd.append(W(sGd, vw))
                for lt in range(NLT):
                    pr = lt % 2
                    gl = htl * NLT + lt
                    if gl >= 2:
                        q.sync.append(W(sP, mm_done[gl - 2]))
                    q.sync.append(lambda e, lt=lt, pr=pr: e.dma_start(
                        out=yst[pr][:, :].rearrange("p (t m) -> p t m", m=128),
                        in_=d_yg[:, lt * 128:(lt + 1) * 128].rearrange("(t p) m -> p t m", p=128)).then_inc(sL.h, 16))
                    sL.n += 16
                    vy = sL.n
                    q.tensor.append(W(sGd, vw))
                    q.tensor.append(W(sL, vy))
                    if gl >= 2:
                        q.tensor.append(W(sA, ev_done[gl - 2]))
                    for kt in range(NET):
                        fin = kt == NET - 1
                        q.tensor.append(lambda e, pr=pr, kt=kt, fin=fin: (
                            e.matmul(ops_[pr][:, :], yst[pr][:, kt * 128:(kt + 1) * 128],
                                     wob[:, kt * 512:(kt + 1) * 512],
                                     start=(kt == 0), stop=(kt == NET - 1)).then_inc(sP.h, 1) if fin else
                            e.matmul(ops_[pr][:, :], yst[pr][:, kt * 128:(kt + 1) * 128],
                                     wob[:, kt * 512:(kt + 1) * 512],
                                     start=(kt == 0), stop=(kt == NET - 1))))
                    vp = sP.inc()
                    mm_done[gl] = vp
                    q.scalar.append(W(sP, vp))
                    if gl >= 2:
                        q.scalar.append(W(sSt, sSt.n))
                    q.scalar.append(lambda e, pr=pr: e.activation(
                        out=aev[pr][:, :], in_=ops_[pr][:, :], func=AF.Copy).then_inc(sA.h, 1))
                    ev_done[gl] = sA.inc()
                    q.sync.append(W(sA, ev_done[gl]))
                    q.sync.append(lambda e, lt=lt, htl=htl, pr=pr: e.dma_start(
                        out=d_attn[lt * 128:(lt + 1) * 128, htl * 512:(htl + 1) * 512],
                        in_=aev[pr][:, :]).then_inc(sSt.h, 16))
                    sSt.n += 16
            q.sync.append(W(sSt, sSt.n))
            if "attn" in taps:
                q.sync.append(lambda e: e.dma_start(out=d_taps["attn"], in_=d_attn).then_inc(sSt.h, 16))
                sSt.n += 16
                q.sync.append(W(sSt, sSt.n))
            emit_block(nc, q)

        if _skip("ph5"):
            return _stop()
        es3 = ExitStack(); open_stacks.append(es3)
        sb_xTp = es3.enter_context(nc.sbuf_tensor("p_xTp", [128, 16, L], BF16))
        es2 = ExitStack(); open_stacks.append(es2)
        sb_x1ln = es2.enter_context(nc.sbuf_tensor("p_x1ln", [128, NLT, H], BF16))
        q = Q()
        with ExitStack() as ph:
            at = [ph.enter_context(nc.sbuf_tensor("t_at0", [128, H], F32))]
            xt = [ph.enter_context(nc.sbuf_tensor("t_xt50", [128, H], F32))]
            x1t = [ph.enter_context(nc.sbuf_tensor(f"t_x1t{i}", [128, H], F32)) for i in range(2)]
            scr = ph.enter_context(nc.sbuf_tensor("t_scr5", [128, H], F32))
            c1 = ph.enter_context(nc.sbuf_tensor("t_c15", [128, 8], F32))

            x1ln_done = {}
            x1_done = {}
            for lt in range(NLT):
                pr = lt % 2
                if lt >= 2:
                    q.sync.append(W(sV, x1ln_done[lt - 2]))
                if lt >= 1:
                    q.sync.append(W(sV, x1_done[lt - 1]))
                q.sync.append(lambda e, lt=lt: e.dma_start(
                    out=at[0][:, :], in_=d_attn[lt * 128:(lt + 1) * 128, :]).then_inc(sL.h, 16))
                sL.n += 16
                q.sync.append(lambda e, lt=lt: e.dma_start(
                    out=xt[0][:, :], in_=d_x[lt * 128:(lt + 1) * 128, :]).then_inc(sL.h, 16))
                sL.n += 16
                vl = sL.n

                q.scalar.append(W(sL, vl))
                va = ln_stats(q, sA, sV, (lambda: at[0][:, :]), scr, c1, 1e-5)
                q.vector.append(W(sA, va))
                q.vector.append(lambda e: e.tensor_scalar(
                    out=scr[:, :], in0=at[0][:, :], scalar1=c1[:, 2:3], scalar2=c1[:, 7:8],
                    op0=ALU.subtract, op1=ALU.mult))
                q.vector.append(lambda e, pr=pr: e.tensor_add(
                    x1t[pr][:, :], xt[0][:, :], scr[:, :]).then_inc(sV.h, 1))
                vx1 = sV.inc()
                x1_done[lt] = vx1
                q.sync.append(W(sV, vx1))
                q.sync.append(lambda e, lt=lt, pr=pr: e.dma_start(
                    out=d_x1[lt * 128:(lt + 1) * 128, :], in_=x1t[pr][:, :]).then_inc(sSt.h, 16))
                sSt.n += 16
                q.scalar.append(W(sV, vx1))
                va2 = ln_stats(q, sA, sV, (lambda pr=pr: x1t[pr][:, :]), scr, c1, 1e-5)
                q.vector.append(W(sA, va2))
                q.vector.append(lambda e, pr=pr, lt=lt: e.tensor_scalar(
                    out=sb_x1ln[:, lt, :], in0=x1t[pr][:, :], scalar1=c1[:, 2:3], scalar2=c1[:, 7:8],
                    op0=ALU.subtract, op1=ALU.mult).then_inc(sV.h, 1))
                x1ln_done[lt] = sV.inc()
            q.sync.append(W(sV, sV.n))
            if "x1" in taps:
                q.sync.append(lambda e: e.dma_start(out=d_taps["x1"], in_=d_x1).then_inc(sSt.h, 16))
                sSt.n += 16
                q.sync.append(W(sSt, sSt.n))
            emit_block(nc, q)

        if _skip("ph6a"):
            return _stop()
        q = Q()
        with ExitStack() as ph:
            pt = [ph.enter_context(nc.psum_tensor(f"t_pt6{i}", [128, 1024], BF16)) for i in range(2)]
            ebase = sV2.n
            gidx = 0
            for kk in range(8):
                for jb in range(2):
                    pid = kk * 2 + jb
                    for g in range(4):
                        gpr = gidx % 2
                        if gidx >= 2:
                            q.tensor.append(W(sV2, ebase + gidx - 1))
                        for li in range(4):
                            lt = g * 4 + li
                            fin = li == 3
                            q.tensor.append(lambda e, gpr=gpr, li=li, lt=lt, kk=kk, jb=jb, fin=fin: (
                                e.transpose(pt[gpr][:, li * 128:(li + 1) * 128],
                                            sb_x1ln[:, lt, kk + 1024 * jb: kk + 1024 * jb + 8 * 127 + 1: 8],
                                            sb_id[:, :]).then_inc(sP.h, 1) if fin else
                                e.transpose(pt[gpr][:, li * 128:(li + 1) * 128],
                                            sb_x1ln[:, lt, kk + 1024 * jb: kk + 1024 * jb + 8 * 127 + 1: 8],
                                            sb_id[:, :])))
                        vp = sP.inc()
                        q.vector.append(W(sP, vp))
                        q.vector.append(lambda e, gpr=gpr, pid=pid, g=g: e.tensor_copy(
                            out=sb_xTp[:, pid, g * 512:(g + 1) * 512],
                            in_=pt[gpr][:, 0:512]).then_inc(sV2.h, 1))
                        sV2.inc()
                        gidx += 1
            q.vector.append(W(sV2, sV2.n))
            if "xtp" in taps:
                q.sync.append(W(sV2, sV2.n))
                q.sync.append(lambda e: e.dma_start(
                    out=d_taps["xtp"], in_=sb_xTp[:, :, :].rearrange("p a b -> p (a b)")).then_inc(sSt.h, 16))
                sSt.n += 16
                q.sync.append(W(sSt, sSt.n))
            emit_block(nc, q)
        es2.close(); open_stacks.remove(es2)

        if _skip("ph6b"):
            return _stop()
        q = Q()
        with ExitStack() as ph:
            lgt = [ph.enter_context(nc.sbuf_tensor(f"t_lgt{i}", [128, 16 * 128], BF16)) for i in range(2)]
            lut = [ph.enter_context(nc.sbuf_tensor(f"t_lut{i}", [128, 16 * 128], BF16)) for i in range(2)]
            sg = [ph.enter_context(nc.sbuf_tensor(f"t_sg{i}", [128, 512], BF16)) for i in range(2)]
            su = [ph.enter_context(nc.sbuf_tensor(f"t_su{i}", [128, 512], BF16)) for i in range(2)]
            hrow = ph.enter_context(nc.sbuf_tensor("t_hrow", [128, L], BF16))
            gps = [ph.enter_context(nc.psum_tensor(f"t_gps{i}", [128, 512], F32)) for i in range(2)]
            ups = [ph.enter_context(nc.psum_tensor(f"t_ups{i}", [128, 512], F32)) for i in range(2)]

            ev_done = {}
            mul_done = {}
            spill_done = {}
            mmct_done = {}
            q.sync.append(W(sCC, 1))
            for ct in range(44):
                pr = ct % 2
                if ct >= 2:
                    q.sync.append(W(sP, mmct_done[ct - 2]))
                q.sync.append(lambda e, ct=ct, pr=pr: e.dma_start(
                    out=lgt[pr][:, :].rearrange("p (pid c) -> p pid c", c=128),
                    in_=d_plgu[:, :, ct * 128:(ct + 1) * 128].rearrange("pid p c -> p pid c")).then_inc(sL.h, 16))
                sL.n += 16
                q.sync.append(lambda e, ct=ct, pr=pr: e.dma_start(
                    out=lut[pr][:, :].rearrange("p (pid c) -> p pid c", c=128),
                    in_=d_plgu[:, :, IFF + ct * 128: IFF + (ct + 1) * 128].rearrange("pid p c -> p pid c")).then_inc(sL.h, 16))
                sL.n += 16
                vl = sL.n
                q.tensor.append(W(sL, vl))
                for lt in range(4):
                    lpr = lt % 2
                    gq = ct * 4 + lt
                    if gq >= 2:
                        q.tensor.append(W(sA, ev_done[gq - 2]))
                    for pid in range(16):
                        q.tensor.append(lambda e, pr=pr, lpr=lpr, pid=pid, lt=lt: e.matmul(
                            gps[lpr][:, :], lgt[pr][:, pid * 128:(pid + 1) * 128],
                            sb_xTp[:, pid, lt * 512:(lt + 1) * 512],
                            start=(pid == 0), stop=(pid == 15)))
                    for pid in range(16):
                        fin = pid == 15
                        q.tensor.append(lambda e, pr=pr, lpr=lpr, pid=pid, lt=lt, fin=fin: (
                            e.matmul(ups[lpr][:, :], lut[pr][:, pid * 128:(pid + 1) * 128],
                                     sb_xTp[:, pid, lt * 512:(lt + 1) * 512],
                                     start=(pid == 0), stop=(pid == 15)).then_inc(sP.h, 1) if fin else
                            e.matmul(ups[lpr][:, :], lut[pr][:, pid * 128:(pid + 1) * 128],
                                     sb_xTp[:, pid, lt * 512:(lt + 1) * 512],
                                     start=(pid == 0), stop=(pid == 15))))
                    vp = sP.inc()
                    q.scalar.append(W(sP, vp))
                    if gq >= 2:
                        q.scalar.append(W(sV, mul_done[gq - 2]))
                    q.scalar.append(lambda e, lpr=lpr: e.activation(
                        out=sg[lpr][:, :], in_=gps[lpr][:, :], func=AF.Silu))
                    q.scalar.append(lambda e, lpr=lpr: e.activation(
                        out=su[lpr][:, :], in_=ups[lpr][:, :], func=AF.Copy).then_inc(sA.h, 1))
                    ev_done[gq] = sA.inc()
                    q.vector.append(W(sA, ev_done[gq]))
                    if ct >= 1 and lt == 0:
                        q.vector.append(W(sSt, spill_done[ct - 1]))
                    q.vector.append(lambda e, lt=lt, lpr=lpr: e.tensor_mul(
                        hrow[:, lt * 512:(lt + 1) * 512], sg[lpr][:, :], su[lpr][:, :]).then_inc(sV.h, 1))
                    mul_done[gq] = sV.inc()
                mmct_done[ct] = sP.n
                q.sync.append(W(sV, mul_done[ct * 4 + 3]))
                q.sync.append(lambda e, ct=ct: e.dma_start(
                    out=d_hact[ct * 128:(ct + 1) * 128, :], in_=hrow[:, :]).then_inc(sSt.h, 16))
                sSt.n += 16
                spill_done[ct] = sSt.n
            q.sync.append(W(sSt, sSt.n))
            if "hact" in taps:
                q.sync.append(lambda e: e.dma_start(out=d_taps["hact"], in_=d_hact).then_inc(sSt.h, 16))
                sSt.n += 16
                q.sync.append(W(sSt, sSt.n))
            emit_block(nc, q)
        es3.close(); open_stacks.remove(es3)

        if _skip("ph6c"):
            return _stop()
        q = Q()
        with ExitStack() as ph:
            wdn = ph.enter_context(nc.sbuf_tensor("t_wdn", [128, 44 * 512], BF16))
            hst8 = ph.enter_context(nc.sbuf_tensor("t_hst8", [128, 8, 44 * 128], BF16))
            fev = [ph.enter_context(nc.sbuf_tensor(f"t_fev{i}", [128, 512], F32)) for i in range(2)]
            dps6 = [ph.enter_context(nc.psum_tensor(f"t_dps6{i}", [128, 512], F32)) for i in range(2)]

            ev_done = {}
            mm_done = {}
            gl = 0
            q.sync.append(W(sCC, 2))
            q.sync.append(W(sSt, sSt.n))
            for hf in range(2):
                if hf > 0:
                    q.sync.append(W(sP, mm_done[gl - 1]))
                for lt8 in range(8):
                    q.sync.append(lambda e, lt8=lt8, hf=hf: e.dma_start(
                        out=hst8[:, lt8, :].rearrange("p (t m) -> p t m", m=128),
                        in_=d_hact[:, (hf * 8 + lt8) * 128:(hf * 8 + lt8 + 1) * 128].rearrange(
                            "(t p) m -> p t m", p=128)).then_inc(sL.h, 16))
                    sL.n += 16
                vh = sL.n
                for htl in range(4):
                    if gl > 0:
                        q.gpsimd.append(W(sP, mm_done[gl - 1]))
                    q.gpsimd.append(lambda e, htl=htl: e.dma_start(
                        out=wdn[:, :].rearrange("p (t h) -> p t h", h=512),
                        in_=d_pldn[:, htl * 512:(htl + 1) * 512].rearrange(
                            "(t p) h -> p t h", p=128)).then_inc(sGd.h, 16))
                    sGd.n += 16
                    vw = sGd.n
                    for lt8 in range(8):
                        lt = hf * 8 + lt8
                        pr = gl % 2
                        q.tensor.append(W(sL, vh))
                        q.tensor.append(W(sGd, vw))
                        if gl >= 2:
                            q.tensor.append(W(sA, ev_done[gl - 2]))
                        for kt in range(44):
                            fin = kt == 43
                            q.tensor.append(lambda e, pr=pr, lt8=lt8, kt=kt, fin=fin: (
                                e.matmul(dps6[pr][:, :], hst8[:, lt8, kt * 128:(kt + 1) * 128],
                                         wdn[:, kt * 512:(kt + 1) * 512],
                                         start=(kt == 0), stop=(kt == 43)).then_inc(sP.h, 1) if fin else
                                e.matmul(dps6[pr][:, :], hst8[:, lt8, kt * 128:(kt + 1) * 128],
                                         wdn[:, kt * 512:(kt + 1) * 512],
                                         start=(kt == 0), stop=(kt == 43))))
                        vp = sP.inc()
                        mm_done[gl] = vp
                        q.scalar.append(W(sP, vp))
                        if gl >= 2:
                            q.scalar.append(W(sSt, sSt.n))
                        q.scalar.append(lambda e, pr=pr: e.activation(
                            out=fev[pr][:, :], in_=dps6[pr][:, :], func=AF.Copy).then_inc(sA.h, 1))
                        ev_done[gl] = sA.inc()
                        q.sync.append(W(sA, ev_done[gl]))
                        q.sync.append(lambda e, lt=lt, htl=htl, pr=pr: e.dma_start(
                            out=d_ffn[lt * 128:(lt + 1) * 128, htl * 512:(htl + 1) * 512],
                            in_=fev[pr][:, :]).then_inc(sSt.h, 16))
                        sSt.n += 16
                        gl += 1
            q.sync.append(W(sSt, sSt.n))
            if "ffn" in taps:
                q.sync.append(lambda e: e.dma_start(out=d_taps["ffn"], in_=d_ffn).then_inc(sSt.h, 16))
                sSt.n += 16
                q.sync.append(W(sSt, sSt.n))
            emit_block(nc, q)

        if _skip("ph7"):
            return _stop()
        q = Q()
        with ExitStack() as ph:
            ft = [ph.enter_context(nc.sbuf_tensor(f"t_ft{i}", [128, H], F32)) for i in range(2)]
            x1t = [ph.enter_context(nc.sbuf_tensor(f"t_x1t7{i}", [128, H], F32)) for i in range(2)]
            vt = ph.enter_context(nc.sbuf_tensor("t_vt", [128, H], F32))
            ot = [ph.enter_context(nc.sbuf_tensor(f"t_ot{i}", [128, H], F32)) for i in range(2)]
            scr = ph.enter_context(nc.sbuf_tensor("t_scr7", [128, H], F32))
            c1 = ph.enter_context(nc.sbuf_tensor("t_c17", [128, 8], F32))

            out_done = {}
            for lt in range(NLT):
                pr = lt % 2
                if lt >= 2:
                    q.sync.append(W(sV, out_done[lt - 2]))
                q.sync.append(lambda e, lt=lt, pr=pr: e.dma_start(
                    out=ft[pr][:, :], in_=d_ffn[lt * 128:(lt + 1) * 128, :]).then_inc(sL.h, 16))
                sL.n += 16
                q.sync.append(lambda e, lt=lt, pr=pr: e.dma_start(
                    out=x1t[pr][:, :], in_=d_x1[lt * 128:(lt + 1) * 128, :]).then_inc(sL.h, 16))
                sL.n += 16
                vl = sL.n
                q.scalar.append(W(sL, vl))
                q.scalar.append(lambda e, pr=pr: e.activation(
                    out=scr[:, :], in_=ft[pr][:, :], func=AF.Square, accum_out=c1[:, 0:1]).then_inc(sA.h, 1))
                va = sA.inc()
                q.vector.append(W(sA, va))
                q.vector.append(lambda e: e.tensor_scalar(
                    out=c1[:, 1:2], in0=c1[:, 0:1], scalar1=1.0 / H, scalar2=1e-6,
                    op0=ALU.mult, op1=ALU.add).then_inc(sV.h, 1))
                vv = sV.inc()
                q.scalar.append(W(sV, vv))
                q.scalar.append(lambda e: e.activation(out=c1[:, 2:3], in_=c1[:, 1:2], func=AF.Ln).then_inc(sA.h, 1))
                vln = sA.inc()
                q.scalar.append(W(sA, vln))
                q.scalar.append(lambda e: e.activation(out=c1[:, 3:4], in_=c1[:, 2:3], func=AF.Exp,
                                                       scale=-0.5).then_inc(sA.h, 1))
                va2 = sA.inc()
                q.vector.append(W(sA, va2))
                q.vector.append(lambda e, pr=pr: e.tensor_scalar_mul(
                    vt[:, :], ft[pr][:, :], c1[:, 3:4]).then_inc(sV.h, 1))
                vv2 = sV.inc()
                q.scalar.append(W(sV, vv2))
                va3 = ln_stats(q, sA, sV, (lambda: vt[:, :]), scr, c1, 1e-5)
                q.vector.append(W(sA, va3))
                q.vector.append(lambda e: e.tensor_scalar(
                    out=scr[:, :], in0=vt[:, :], scalar1=c1[:, 2:3], scalar2=c1[:, 7:8],
                    op0=ALU.subtract, op1=ALU.mult))
                q.vector.append(lambda e, pr=pr: e.tensor_add(
                    ot[pr][:, :], x1t[pr][:, :], scr[:, :]).then_inc(sV.h, 1))
                out_done[lt] = sV.inc()
                q.sync.append(W(sV, out_done[lt]))
                q.sync.append(lambda e, lt=lt, pr=pr: e.dma_start(
                    out=d_out[lt * 128:(lt + 1) * 128, :], in_=ot[pr][:, :]).then_inc(sSt.h, 16))
                sSt.n += 16
            q.sync.append(W(sSt, sSt.n))
            emit_block(nc, q)

    return nc


def _prep_inputs(inputs):
    x = np.ascontiguousarray(np.asarray(inputs["L0_attn_input_states"], np.float32))
    common = dict(
        w_in=np.ascontiguousarray(np.asarray(inputs["L0_attn_W_in"], np.float32)),
        conv_k=np.ascontiguousarray(np.asarray(inputs["L0_attn_conv_kernel"], np.float32).reshape(E, 4)),
        conv_b=np.ascontiguousarray(np.asarray(inputs["L0_attn_conv_bias"], np.float32).reshape(NET, 128).T),
        w_x=np.ascontiguousarray(np.asarray(inputs["L0_attn_W_x"], np.float32)),
        w_dt=np.ascontiguousarray(np.asarray(inputs["L0_attn_W_dt"], np.float32)),
        b_dt=np.ascontiguousarray(np.asarray(inputs["L0_attn_b_dt"], np.float32).reshape(NET, 128).T),
        w_out=np.ascontiguousarray(np.asarray(inputs["L0_attn_W_out"], np.float32)),
        a_log=np.ascontiguousarray(np.asarray(inputs["L0_attn_A_log"], np.float32)),
        dvec=np.ascontiguousarray(np.asarray(inputs["L0_attn_D"], np.float32).reshape(NET, 128).T),

        s_gu=np.ascontiguousarray(np.asarray(inputs["L0_ffn_gate_up_scales"], np.float16)[:, _SGU_PERM]),

        s_dn=np.ascontiguousarray(np.asarray(inputs["L0_ffn_down_scales"], np.float16)),
        ident=np.eye(128, dtype=np.float32),
    )
    pgu_t = np.ascontiguousarray(np.asarray(inputs["L0_ffn_gate_up_packed"], np.int32).T)
    pdn_t = np.ascontiguousarray(np.asarray(inputs["L0_ffn_down_packed"], np.int32).T)
    pgu16 = pgu_t.view(np.int16)
    pdn16 = pdn_t.view(np.int16)
    in_maps = []
    for b in range(8):
        m = dict(common)
        m["x"] = np.ascontiguousarray(x[b])
        h16 = b // 4
        kpos = b % 4
        m["pgu_h"] = np.ascontiguousarray(pgu16[:, h16::2][:, _SGU_PERM])
        m["pdn_h"] = np.ascontiguousarray(pdn16[:, h16::2])
        sh = np.empty((128, 2), np.int32)
        sh[:, 0] = 4 * kpos
        sh[:, 1] = 12 - 4 * kpos
        m["dqsh"] = sh
        in_maps.append(m)
    return in_maps


def kernel(**inputs):
    in_maps = _prep_inputs(inputs)
    nc = build_program(taps=DEBUG_TAPS)
    res = run_bass_kernel_spmd(nc, in_maps, core_ids=list(range(8)))
    out = np.stack([np.asarray(r["out"], np.float32) for r in res.results], axis=0)
    if DEBUG_TAPS:
        kernel.last_results = res
    return out


# version 08 (final)
# speedup vs baseline: 1.1358x; 1.0529x over previous
import numpy as np
import ml_dtypes
from contextlib import ExitStack

import concourse.bass as bass
import concourse.mybir as mybir
from concourse.bass_utils import run_bass_kernel_spmd

F32 = mybir.dt.float32
BF16 = mybir.dt.bfloat16
F16 = mybir.dt.float16
I16 = mybir.dt.int16
I32 = mybir.dt.int32
AF = mybir.ActivationFunctionType
ALU = mybir.AluOpType

L = 2048
H = 2048
E = 4096
NST = 16
TR = 128
IFF = 5632
OGU = 11264
NET = 32
NKT = 16
NLT = 16

DEBUG_TAPS = ()

_c = np.arange(OGU if False else 11264)
_half = _c // 5632
_r = _c % 5632
_SGU_PERM = _half * 5632 + 8 * (_r % 704) + (_r // 704)


class Sem:
    def __init__(self, stack, nc, name):
        self.h = stack.enter_context(nc.semaphore(name))
        self.n = 0

    def inc(self, k=1):
        self.n += k
        return self.n


class Q:
    def __init__(self):
        self.sync = []
        self.tensor = []
        self.vector = []
        self.scalar = []
        self.gpsimd = []


def emit_block(nc, q):
    with nc.Block() as block:
        if q.sync:
            block.sync(lambda e, ops=tuple(q.sync): [f(e) for f in ops])
        if q.tensor:
            block.tensor(lambda e, ops=tuple(q.tensor): [f(e) for f in ops])
        if q.vector:
            block.vector(lambda e, ops=tuple(q.vector): [f(e) for f in ops])
        if q.scalar:
            block.scalar(lambda e, ops=tuple(q.scalar): [f(e) for f in ops])
        if q.gpsimd:
            block.gpsimd(lambda e, ops=tuple(q.gpsimd): [f(e) for f in ops])


def W(sem, v):
    assert v <= sem.n, f"wait beyond planned count on {sem.h}: {v} > {sem.n}"
    return lambda e, h=sem.h, v=v: e.wait_ge(h, v)


def ln_stats(q, sA, sV, src_f, scr, c1, eps):
    q.scalar.append(lambda e: e.activation(out=scr[:, :], in_=src_f(), func=AF.Identity,
                                           accum_out=c1[:, 0:1]))
    q.scalar.append(lambda e: e.activation(out=scr[:, :], in_=src_f(), func=AF.Square,
                                           accum_out=c1[:, 1:2]).then_inc(sA.h, 1))
    va = sA.inc()
    q.vector.append(W(sA, va))
    q.vector.append(lambda e: e.tensor_scalar_mul(c1[:, 2:3], c1[:, 0:1], 1.0 / H))
    q.vector.append(lambda e: e.tensor_scalar_mul(c1[:, 3:4], c1[:, 1:2], 1.0 / H).then_inc(sV.h, 1))
    g1 = sV.inc()
    q.vector.append(W(sV, g1))
    q.vector.append(lambda e: e.tensor_mul(c1[:, 4:5], c1[:, 2:3], c1[:, 2:3]).then_inc(sV.h, 1))
    g2 = sV.inc()
    q.vector.append(W(sV, g2))
    q.vector.append(lambda e: e.tensor_scalar(
        out=c1[:, 5:6], in0=c1[:, 3:4], scalar1=c1[:, 4:5], scalar2=float(eps),
        op0=ALU.subtract, op1=ALU.add).then_inc(sV.h, 1))
    vv = sV.inc()
    q.scalar.append(W(sV, vv))
    q.scalar.append(lambda e: e.activation(out=c1[:, 6:7], in_=c1[:, 5:6], func=AF.Ln).then_inc(sA.h, 1))
    ga = sA.inc()
    q.scalar.append(W(sA, ga))
    q.scalar.append(lambda e: e.activation(out=c1[:, 7:8], in_=c1[:, 6:7], func=AF.Exp,
                                           scale=-0.5).then_inc(sA.h, 1))
    return sA.inc()


def dequant_ops(q, sV, nib, ta, tb, tc, ts_, wpl, sfull_ap, src, kpos, rows, width):
    r = rows
    q.vector.append(lambda e: e.tensor_scalar(
        out=nib[0:r, 0:width], in0=src, scalar1=int(4 * kpos), scalar2=int(7),
        op0=ALU.logical_shift_right, op1=ALU.bitwise_and))
    q.vector.append(lambda e: e.tensor_scalar(
        out=ta[0:r, 0:width], in0=nib[0:r, 0:width], scalar1=int(2), scalar2=int(0x3F00),
        op0=ALU.min, op1=ALU.mult))
    q.vector.append(lambda e: e.tensor_scalar(
        out=tb[0:r, 0:width], in0=nib[0:r, 0:width], scalar1=int(0x80), scalar2=int(0x3E80),
        op0=ALU.mult, op1=ALU.add))
    q.vector.append(lambda e: e.tensor_scalar(
        out=tc[0:r, 0:width], in0=nib[0:r, 0:width], scalar1=int(0x40), scalar2=int(0x3F00),
        op0=ALU.mult, op1=ALU.add))
    q.vector.append(lambda e: e.tensor_tensor(
        out=ta[0:r, 0:width], in0=ta[0:r, 0:width], in1=tb[0:r, 0:width], op=ALU.min))
    q.vector.append(lambda e: e.tensor_tensor(
        out=ta[0:r, 0:width], in0=ta[0:r, 0:width], in1=tc[0:r, 0:width], op=ALU.min))
    q.vector.append(lambda e: e.tensor_scalar(
        out=ts_[0:r, 0:width], in0=src, scalar1=int(12 - 4 * kpos), scalar2=int(-0x8000),
        op0=ALU.logical_shift_left, op1=ALU.bitwise_and))
    q.vector.append(lambda e: e.tensor_tensor(
        out=ta[0:r, 0:width], in0=ta[0:r, 0:width], in1=ts_[0:r, 0:width], op=ALU.bitwise_or))
    q.vector.append(lambda e: e.tensor_mul(
        wpl[0:r, 0:width], ta[0:r, 0:width].bitcast(BF16), sfull_ap).then_inc(sV.h, 1))
    return sV.inc()




def make_deq_gu_feeder(nc, q, ph, sems, d_pgu, d_sgu_bf, d_plgu):
    sL, sGd, sSt, sV = sems
    pk32 = ph.enter_context(nc.sbuf_tensor("t_pk32", [128, OGU], I32))
    nib = ph.enter_context(nc.sbuf_tensor("t_nib", [128, 704], I16))
    ta = ph.enter_context(nc.sbuf_tensor("t_ta6", [128, 704], I16))
    tb = ph.enter_context(nc.sbuf_tensor("t_tb6", [128, 704], I16))
    tc = ph.enter_context(nc.sbuf_tensor("t_tc6", [128, 704], I16))
    ts_ = ph.enter_context(nc.sbuf_tensor("t_ts6", [128, 704], I16))
    sfull = ph.enter_context(nc.sbuf_tensor("t_sf6", [128, 704], BF16))
    wpl = ph.enter_context(nc.sbuf_tensor("t_wpl", [128, 704], BF16))
    st = {"i": 0, "prev_spill": 0, "loaded_jb": -1}
    order = [(jb, half, kp, kk) for jb in range(2) for half in range(2)
             for kp in range(8) for kk in range(8)]

    def feed(n):
        for _ in range(n):
            if st["i"] >= len(order):
                return
            jb, half, kp, kk = order[st["i"]]
            st["i"] += 1
            if jb != st["loaded_jb"]:
                if jb > 0:
                    q.sync.append(W(sSt, sSt.n))
                q.sync.append(lambda e, jb=jb: e.dma_start(
                    out=pk32[:, :], in_=d_pgu[jb * 128:(jb + 1) * 128, :]).then_inc(sL.h, 16))
                sL.n += 16
                st["vl"] = sL.n
                st["loaded_jb"] = jb
            if kk == 0:
                if st["i"] > 1:
                    q.gpsimd.append(W(sV, sV.n))
                for g in range(8):
                    q.gpsimd.append(lambda e, jb=jb, half=half, kp=kp, g=g: e.dma_start(
                        out=sfull[g * 16:(g + 1) * 16, :],
                        in_=bass.AP(d_sgu_bf.tensor, (jb * 8 + g) * OGU + half * IFF + kp * 704,
                                    [[0, 16], [1, 704]])).then_inc(sGd.h, 16))
                    sGd.n += 16
                st["vsg"] = sGd.n
                q.gpsimd.append(W(sGd, st["vsg"]))
            h16 = kk // 4
            kpos = kk % 4
            pk16 = pk32[:, :].bitcast(I16)
            base = 2 * (half * IFF + kp) + h16
            src = pk16[:, base: base + 16 * 703 + 1: 16]
            q.vector.append(W(sL, st["vl"]))
            q.vector.append(W(sGd, st["vsg"]))
            if st["prev_spill"]:
                q.vector.append(W(sSt, st["prev_spill"]))
            dequant_ops(q, sV, nib, ta, tb, tc, ts_, wpl, sfull[:, :], src, kpos, 128, 704)
            vw = sV.n
            pid = kk * 2 + jb
            q.sync.append(W(sV, vw))
            q.sync.append(lambda e, pid=pid, half=half, kp=kp: e.dma_start(
                out=d_plgu[pid, :, half * IFF + kp * 704: half * IFF + (kp + 1) * 704],
                in_=wpl[:, :]).then_inc(sSt.h, 16))
            sSt.n += 16
            st["prev_spill"] = sSt.n

    def remaining():
        return len(order) - st["i"]
    feed.remaining = remaining
    return feed


def make_deq_dn_feeder(nc, q, ph, sems, d_pdn, d_sdn_bf, d_pldn):
    sL, sGd, sSt, sV = sems
    pk32 = ph.enter_context(nc.sbuf_tensor("t_pk32d", [128, 6 * H], I32))
    nib = ph.enter_context(nc.sbuf_tensor("t_nibd", [128, H], I16))
    ta = ph.enter_context(nc.sbuf_tensor("t_tad", [128, H], I16))
    tb = ph.enter_context(nc.sbuf_tensor("t_tbd", [128, H], I16))
    tc = ph.enter_context(nc.sbuf_tensor("t_tcd", [128, H], I16))
    ts_ = ph.enter_context(nc.sbuf_tensor("t_tsd", [128, H], I16))
    sfull = ph.enter_context(nc.sbuf_tensor("t_sfd", [128, 6 * H], BF16))
    wpl = ph.enter_context(nc.sbuf_tensor("t_wpld", [128, H], BF16))
    st = {"i": 0, "prev_spill": 0, "init": False}
    order = [(kp, pt_) for kp in range(8) for pt_ in range(6)]

    def feed(n):
        if not st["init"]:
            st["init"] = True
            q.sync.append(lambda e: e.dma_start(
                out=pk32[:, 0:5 * H].rearrange("p (t h) -> p t h", t=5),
                in_=d_pdn[0:640, :].rearrange("(t p) h -> p t h", p=128)).then_inc(sL.h, 16))
            sL.n += 16
            q.sync.append(lambda e: e.dma_start(
                out=pk32[0:64, 5 * H:6 * H], in_=d_pdn[640:704, :]).then_inc(sL.h, 16))
            sL.n += 16
            st["vl"] = sL.n
            for pt_ in range(6):
                ng = 8 if pt_ < 5 else 4
                for g in range(ng):
                    q.gpsimd.append(lambda e, pt_=pt_, g=g: e.dma_start(
                        out=sfull[g * 16:(g + 1) * 16, pt_ * H:(pt_ + 1) * H],
                        in_=bass.AP(d_sdn_bf.tensor, (pt_ * 8 + g) * H, [[0, 16], [1, H]])).then_inc(sGd.h, 16))
                    sGd.n += 16
            q.gpsimd.append(W(sGd, sGd.n))
            st["vg"] = sGd.n
        for _ in range(n):
            if st["i"] >= len(order):
                return
            kp, pt_ = order[st["i"]]
            st["i"] += 1
            rows = 128 if pt_ < 5 else 64
            pk16 = pk32[:, :].bitcast(I16)
            h16 = kp // 4
            kpos = kp % 4
            src = pk16[0:rows, 2 * pt_ * H + h16: 2 * pt_ * H + h16 + 2 * (H - 1) + 1: 2]
            q.vector.append(W(sL, st["vl"]))
            q.vector.append(W(sGd, st["vg"]))
            if st["prev_spill"]:
                q.vector.append(W(sSt, st["prev_spill"]))
            dequant_ops(q, sV, nib, ta, tb, tc, ts_, wpl,
                        sfull[0:rows, pt_ * H:(pt_ + 1) * H], src, kpos, rows, H)
            vw = sV.n
            q.sync.append(W(sV, vw))
            q.sync.append(lambda e, kp=kp, pt_=pt_, rows=rows: e.dma_start(
                out=d_pldn[kp * 704 + pt_ * 128: kp * 704 + pt_ * 128 + rows, :],
                in_=wpl[0:rows, :]).then_inc(sSt.h, 16))
            sSt.n += 16
            st["prev_spill"] = sSt.n

    def remaining():
        return len(order) - st["i"]
    feed.remaining = remaining
    return feed


PHASE_ORDER = ["ph0", "ph0b", "ph1", "ph2", "ph3", "ph4",
               "ph5", "ph6a", "ph6b", "ph6c", "ph7"]


def dequant_ops_col(q, sV, nib, ta, tb, tc, ts_, wpl, sfull_ap, src, shc, rows, width):
    r = rows
    q.vector.append(lambda e: e.tensor_single_scalar(
        out=nib[0:r, 0:width], in_=src, scalar=shc[0:r, 0:1],
        op=ALU.logical_shift_right))
    q.vector.append(lambda e: e.tensor_single_scalar(
        out=nib[0:r, 0:width], in_=nib[0:r, 0:width], scalar=int(7),
        op=ALU.bitwise_and))
    q.vector.append(lambda e: e.tensor_scalar(
        out=ta[0:r, 0:width], in0=nib[0:r, 0:width], scalar1=int(2), scalar2=int(0x3F00),
        op0=ALU.min, op1=ALU.mult))
    q.vector.append(lambda e: e.tensor_scalar(
        out=tb[0:r, 0:width], in0=nib[0:r, 0:width], scalar1=int(0x80), scalar2=int(0x3E80),
        op0=ALU.mult, op1=ALU.add))
    q.vector.append(lambda e: e.tensor_scalar(
        out=tc[0:r, 0:width], in0=nib[0:r, 0:width], scalar1=int(0x40), scalar2=int(0x3F00),
        op0=ALU.mult, op1=ALU.add))
    q.vector.append(lambda e: e.tensor_tensor(
        out=ta[0:r, 0:width], in0=ta[0:r, 0:width], in1=tb[0:r, 0:width], op=ALU.min))
    q.vector.append(lambda e: e.tensor_tensor(
        out=ta[0:r, 0:width], in0=ta[0:r, 0:width], in1=tc[0:r, 0:width], op=ALU.min))
    q.vector.append(lambda e: e.tensor_single_scalar(
        out=ts_[0:r, 0:width], in_=src, scalar=shc[0:r, 1:2],
        op=ALU.logical_shift_left))
    q.vector.append(lambda e: e.tensor_single_scalar(
        out=ts_[0:r, 0:width], in_=ts_[0:r, 0:width], scalar=int(-0x8000),
        op=ALU.bitwise_and))
    q.vector.append(lambda e: e.tensor_tensor(
        out=ta[0:r, 0:width], in0=ta[0:r, 0:width], in1=ts_[0:r, 0:width], op=ALU.bitwise_or))
    q.vector.append(lambda e: e.tensor_mul(
        wpl[0:r, 0:width], ta[0:r, 0:width].bitcast(BF16), sfull_ap).then_inc(sV.h, 1))
    return sV.inc()


def build_program(taps=(), upto="ph7"):
    _lim = PHASE_ORDER.index(upto)

    def _skip(name):
        return PHASE_ORDER.index(name) > _lim

    nc = bass.Bass("TRN2", target_bir_lowering=False, detect_race_conditions=False)
    open_stacks = []

    def _stop():
        for st_ in reversed(open_stacks):
            st_.close()
        return nc

    d_x = nc.dram_tensor("x", [L, H], F32, kind="ExternalInput").ap()
    d_win = nc.dram_tensor("w_in", [H, 2 * E], F32, kind="ExternalInput").ap()
    d_ck = nc.dram_tensor("conv_k", [E, 4], F32, kind="ExternalInput").ap()
    d_cb = nc.dram_tensor("conv_b", [128, NET], F32, kind="ExternalInput").ap()
    d_wx = nc.dram_tensor("w_x", [E, 160], F32, kind="ExternalInput").ap()
    d_wdt = nc.dram_tensor("w_dt", [TR, E], F32, kind="ExternalInput").ap()
    d_bdt = nc.dram_tensor("b_dt", [128, NET], F32, kind="ExternalInput").ap()
    d_wout = nc.dram_tensor("w_out", [E, H], F32, kind="ExternalInput").ap()
    d_alog = nc.dram_tensor("a_log", [E, NST], F32, kind="ExternalInput").ap()
    d_D = nc.dram_tensor("dvec", [128, NET], F32, kind="ExternalInput").ap()
    d_pguh = nc.dram_tensor("pgu_h", [256, OGU], I16, kind="ExternalInput").ap()
    d_sgu = nc.dram_tensor("s_gu", [16, OGU], F16, kind="ExternalInput").ap()
    d_pdnh = nc.dram_tensor("pdn_h", [704, H], I16, kind="ExternalInput").ap()
    d_dqsh = nc.dram_tensor("dqsh", [128, 2], I32, kind="ExternalInput").ap()
    d_sdn = nc.dram_tensor("s_dn", [44, H], F16, kind="ExternalInput").ap()
    d_id = nc.dram_tensor("ident", [128, 128], F32, kind="ExternalInput").ap()

    d_u = nc.dram_tensor("sc_u", [E, L], BF16, kind="Internal").ap()
    d_gate = nc.dram_tensor("sc_gate", [E, L], BF16, kind="Internal").ap()
    d_bcbf = nc.dram_tensor("sc_bc", [32, L], BF16, kind="Internal").ap()
    d_yh0 = nc.dram_tensor("sc_yh0", [E, L], BF16, kind="Internal").ap()
    d_yg = nc.dram_tensor("sc_yg", [E, L], BF16, kind="Internal").ap()
    d_attn = nc.dram_tensor("sc_attn", [L, H], F32, kind="Internal").ap()
    d_x1 = nc.dram_tensor("sc_x1", [L, H], F32, kind="Internal").ap()
    d_sgu_bf = nc.dram_tensor("sc_sgu", [16, OGU], BF16, kind="Internal").ap()
    d_sdn_bf = nc.dram_tensor("sc_sdn", [44, H], BF16, kind="Internal").ap()
    d_plgu = nc.dram_tensor("sc_plgu", [16, 128, OGU], BF16, kind="Internal").ap()
    d_plgu_loc = nc.dram_tensor("sc_plgu_loc", [256, OGU], BF16, kind="Internal").ap()
    d_pldn_loc = nc.dram_tensor("sc_pldn_loc", [704, H], BF16, kind="Internal").ap()
    d_hact = nc.dram_tensor("sc_hact", [IFF, L], BF16, kind="Internal").ap()
    d_pldn = nc.dram_tensor("sc_pldn", [IFF, H], BF16, kind="Internal").ap()
    d_ffn = nc.dram_tensor("sc_ffn", [L, H], F32, kind="Internal").ap()

    d_out = nc.dram_tensor("out", [L, H], F32, kind="ExternalOutput").ap()
    d_taps = {}
    for t in taps:
        shp = {"xlnT": [128, NKT * L], "u": [E, L], "gate": [E, L], "dtT": [TR, L],
               "bc": [32, L], "yg": [E, L], "attn": [L, H], "x1": [L, H],
               "hact": [IFF, L], "ffn": [L, H], "plgu": [16 * 128, OGU],
               "xtp": [128, 16 * L]}[t]
        dt = F32 if t in ("attn", "x1", "ffn") else BF16
        d_taps[t] = nc.dram_tensor("tap_" + t, shp, dt, kind="ExternalOutput").ap()

    stack = ExitStack()
    with stack:
        sL = Sem(stack, nc, "sL")
        sGd = Sem(stack, nc, "sGd")
        sSt = Sem(stack, nc, "sSt")
        sP = Sem(stack, nc, "sP")
        sA = Sem(stack, nc, "sA")
        sV = Sem(stack, nc, "sV")
        sA2 = Sem(stack, nc, "sA2")
        sV2 = Sem(stack, nc, "sV2")
        sCC = Sem(stack, nc, "sCC")

        sb_id = stack.enter_context(nc.sbuf_tensor("c_id", [128, 128], BF16))
        sb_aneg = stack.enter_context(nc.sbuf_tensor("c_aneg", [128, NET * NST], F32))
        sb_D = stack.enter_context(nc.sbuf_tensor("c_D", [128, NET], F32))
        sb_cb = stack.enter_context(nc.sbuf_tensor("c_cb", [128, NET], F32))
        sb_bdt = stack.enter_context(nc.sbuf_tensor("c_bdt", [128, NET], F32))
        sb_ck = stack.enter_context(nc.sbuf_tensor("c_ck", [128, NET * 4], F32))
        sb_wdt = stack.enter_context(nc.sbuf_tensor("c_wdt", [128, E], BF16))
        sb_wx = stack.enter_context(nc.sbuf_tensor("c_wx", [128, NET * 160], BF16))
        sb_dtT = stack.enter_context(nc.sbuf_tensor("p_dtT", [128, L], BF16))

        q = Q()
        ph0 = ExitStack()
        sb_alog = ph0.enter_context(nc.sbuf_tensor("t_alog", [128, NET * NST], F32))

        def _ph0_sync(e):
            e.dma_start(out=sb_alog[:, :].rearrange("p (t n) -> p t n", n=NST), in_=d_alog.rearrange("(t p) n -> p t n", p=128)).then_inc(sL.h, 16)
            e.dma_start(out=sb_D[:, :], in_=d_D).then_inc(sL.h, 16)
            e.dma_start(out=sb_cb[:, :], in_=d_cb).then_inc(sL.h, 16)
            e.dma_start(out=sb_bdt[:, :], in_=d_bdt).then_inc(sL.h, 16)
            e.dma_start(out=sb_ck[:, :].rearrange("p (t k) -> p t k", k=4), in_=d_ck.rearrange("(t p) k -> p t k", p=128)).then_inc(sL.h, 16)
        sL.n += 5 * 16
        q.sync.append(_ph0_sync)

        def _ph0_gp(e):
            e.dma_start(out=sb_id[:, :], in_=d_id).then_inc(sGd.h, 16)
            e.dma_start(out=sb_wdt[:, :], in_=d_wdt).then_inc(sGd.h, 16)
            e.dma_start(out=sb_wx[:, :].rearrange("p (t d) -> p t d", d=160), in_=d_wx.rearrange("(t p) d -> p t d", p=128)).then_inc(sGd.h, 16)
            e.dma_start(out=d_sgu_bf, in_=d_sgu).then_inc(sGd.h, 16)
            e.dma_start(out=d_sdn_bf, in_=d_sdn).then_inc(sGd.h, 16)
            e.wait_ge(sGd.h, 80)
        sGd.n += 5 * 16
        q.gpsimd.append(_ph0_gp)

        q.scalar.append(W(sL, sL.n))
        q.scalar.append(lambda e: e.activation(out=sb_aneg[:, :], in_=sb_alog[:, :],
                                               func=AF.Exp).then_inc(sA.h, 1))
        va = sA.inc()
        q.vector.append(W(sA, va))
        q.vector.append(lambda e: e.tensor_scalar_mul(sb_aneg[:, :], sb_aneg[:, :], -1.0).then_inc(sV.h, 1))
        sV.inc()
        emit_block(nc, q)
        ph0.close()


        q = Q()
        with ExitStack() as ph:
            pkg = ph.enter_context(nc.sbuf_tensor("t_pkg", [128, OGU], I16))
            pkd = ph.enter_context(nc.sbuf_tensor("t_pkd", [128, 6 * H], I16))
            shc = ph.enter_context(nc.sbuf_tensor("t_shc", [128, 2], I32))
            nib = ph.enter_context(nc.sbuf_tensor("t_nibq", [128, H], I16))
            ta = ph.enter_context(nc.sbuf_tensor("t_taq", [128, H], I16))
            tb = ph.enter_context(nc.sbuf_tensor("t_tbq", [128, H], I16))
            tc = ph.enter_context(nc.sbuf_tensor("t_tcq", [128, H], I16))
            ts_ = ph.enter_context(nc.sbuf_tensor("t_tsq", [128, H], I16))
            sfull = ph.enter_context(nc.sbuf_tensor("t_sfq", [128, 6 * H], BF16))
            wpl = ph.enter_context(nc.sbuf_tensor("t_wplq", [128, H], BF16))

            q.sync.append(lambda e: e.dma_start(out=shc[:, :], in_=d_dqsh).then_inc(sL.h, 16))
            sL.n += 16
            prev_spill = 0
            for jb in range(2):
                if jb > 0:
                    q.sync.append(W(sSt, sSt.n))
                q.sync.append(lambda e, jb=jb: e.dma_start(
                    out=pkg[:, :], in_=d_pguh[jb * 128:(jb + 1) * 128, :]).then_inc(sL.h, 16))
                sL.n += 16
                vl = sL.n
                for half in range(2):
                    for kp in range(8):
                        if not (jb == 0 and half == 0 and kp == 0):
                            q.gpsimd.append(W(sV, sV.n))
                        for g in range(8):
                            q.gpsimd.append(lambda e, jb=jb, half=half, kp=kp, g=g: e.dma_start(
                                out=sfull[g * 16:(g + 1) * 16, 0:704],
                                in_=bass.AP(d_sgu_bf.tensor, (jb * 8 + g) * OGU + half * IFF + kp * 704,
                                            [[0, 16], [1, 704]])).then_inc(sGd.h, 16))
                            sGd.n += 16
                        vsg = sGd.n
                        q.gpsimd.append(W(sGd, vsg))
                        src = pkg[:, half * IFF + kp * 704: half * IFF + (kp + 1) * 704]
                        q.vector.append(W(sL, vl))
                        q.vector.append(W(sGd, vsg))
                        if prev_spill:
                            q.vector.append(W(sSt, prev_spill))
                        dequant_ops_col(q, sV, nib, ta, tb, tc, ts_, wpl,
                                        sfull[:, 0:704], src, shc, 128, 704)
                        vw = sV.n
                        q.sync.append(W(sV, vw))
                        q.sync.append(lambda e, jb=jb, half=half, kp=kp: e.dma_start(
                            out=d_plgu_loc[jb * 128:(jb + 1) * 128,
                                           half * IFF + kp * 704: half * IFF + (kp + 1) * 704],
                            in_=wpl[:, 0:704]).then_inc(sSt.h, 16))
                        sSt.n += 16
                        prev_spill = sSt.n
            q.sync.append(lambda e: e.dma_start(
                out=pkd[:, 0:5 * H].rearrange("p (t h) -> p t h", t=5),
                in_=d_pdnh[0:640, :].rearrange("(t p) h -> p t h", p=128)).then_inc(sL.h, 16))
            sL.n += 16
            q.sync.append(lambda e: e.dma_start(
                out=pkd[0:64, 5 * H:6 * H], in_=d_pdnh[640:704, :]).then_inc(sL.h, 16))
            sL.n += 16
            vld = sL.n
            for pt_ in range(6):
                ng = 8 if pt_ < 5 else 4
                q.gpsimd.append(W(sV, sV.n))
                for g in range(ng):
                    q.gpsimd.append(lambda e, pt_=pt_, g=g: e.dma_start(
                        out=sfull[g * 16:(g + 1) * 16, pt_ * H:(pt_ + 1) * H],
                        in_=bass.AP(d_sdn_bf.tensor, (pt_ * 8 + g) * H, [[0, 16], [1, H]])).then_inc(sGd.h, 16))
                    sGd.n += 16
                vsg = sGd.n
                q.gpsimd.append(W(sGd, vsg))
                rows = 128 if pt_ < 5 else 64
                src = pkd[0:rows, pt_ * H:(pt_ + 1) * H]
                q.vector.append(W(sL, vld))
                q.vector.append(W(sGd, vsg))
                if prev_spill:
                    q.vector.append(W(sSt, prev_spill))
                dequant_ops_col(q, sV, nib, ta, tb, tc, ts_, wpl,
                                sfull[0:rows, pt_ * H:(pt_ + 1) * H], src, shc, rows, H)
                vw = sV.n
                q.sync.append(W(sV, vw))
                q.sync.append(lambda e, pt_=pt_, rows=rows: e.dma_start(
                    out=d_pldn_loc[pt_ * 128: pt_ * 128 + rows, :],
                    in_=wpl[0:rows, :]).then_inc(sSt.h, 16))
                sSt.n += 16
                prev_spill = sSt.n
            q.gpsimd.append(W(sSt, sSt.n))
            q.gpsimd.append(lambda e: e.collective_compute(
                "AllGather", ALU.bypass, replica_groups=[list(range(8))],
                ins=[d_plgu_loc.opt()], outs=[d_plgu.rearrange("pid p c -> (pid p) c").opt()]).then_inc(sCC.h))
            sCC.n += 1
            q.gpsimd.append(lambda e: e.collective_compute(
                "AllGather", ALU.bypass, replica_groups=[list(range(8))],
                ins=[d_pldn_loc.opt()], outs=[d_pldn.opt()]).then_inc(sCC.h))
            sCC.n += 1
            q.gpsimd.append(W(sCC, 2))
            q.sync.append(W(sSt, sSt.n))
            if "plgu" in taps:
                q.sync.append(W(sCC, 1))
                q.sync.append(lambda e: e.dma_start(out=d_taps["plgu"], in_=d_plgu.rearrange("pid p c -> (pid p) c")).then_inc(sSt.h, 16))
                sSt.n += 16
                q.sync.append(W(sSt, sSt.n))
            emit_block(nc, q)

        if _skip("ph0b"):
            return _stop()
        es1 = ExitStack(); open_stacks.append(es1)
        sb_xlnT = es1.enter_context(nc.sbuf_tensor("p_xlnT", [128, NKT, L], BF16))
        q = Q()
        with ExitStack() as ph:
            xt = [ph.enter_context(nc.sbuf_tensor(f"t_x{i}", [128, H], F32)) for i in range(2)]
            xln = [ph.enter_context(nc.sbuf_tensor(f"t_xl{i}", [128, H], BF16)) for i in range(2)]
            scr = ph.enter_context(nc.sbuf_tensor("t_scr", [128, H], F32))
            c1 = ph.enter_context(nc.sbuf_tensor("t_c1", [128, 8], F32))
            pt = [ph.enter_context(nc.psum_tensor(f"t_pt{i}", [128, 1024], BF16)) for i in range(2)]

            norm_done = {}
            evac_done = {}
            gi = 0
            for lt in range(NLT):
                pr = lt % 2
                if lt >= 2:
                    q.sync.append(W(sV, norm_done[lt - 2]))
                q.sync.append(lambda e, lt=lt, pr=pr: e.dma_start(
                    out=xt[pr][:, :], in_=d_x[lt * 128:(lt + 1) * 128, :]).then_inc(sL.h, 16))
                sL.n += 16
                vl = sL.n

                q.scalar.append(W(sL, vl))
                va = ln_stats(q, sA, sV, (lambda pr=pr: xt[pr][:, :]), scr, c1, 1e-5)
                q.vector.append(W(sA, va))
                if lt >= 2:
                    q.vector.append(W(sP, sP.n))
                q.vector.append(lambda e, pr=pr: e.tensor_scalar(
                    out=xln[pr][:, :], in0=xt[pr][:, :], scalar1=c1[:, 2:3], scalar2=c1[:, 7:8],
                    op0=ALU.subtract, op1=ALU.mult).then_inc(sV.h, 1))
                norm_done[lt] = sV.inc()

                import os as _os
                if _os.environ.get("K_PH0B_CUT") == "1":
                    continue
                _noevac = _os.environ.get("K_PH0B_NOEVAC") == "1"
                for g in range(4):
                    gpr = gi % 2
                    q.tensor.append(W(sV, norm_done[lt]))
                    if gi >= 2:
                        q.tensor.append(W(sV2, evac_done[gi - 2]))
                    for hb in range(4):
                        ht = g * 4 + hb
                        q.tensor.append(lambda e, pr=pr, gpr=gpr, ht=ht, hb=hb: (
                            e.transpose(pt[gpr][:, hb * 128:(hb + 1) * 128],
                                        xln[pr][:, ht * 128:(ht + 1) * 128],
                                        sb_id[:, :]).then_inc(sP.h, 1) if hb == 3 else
                            e.transpose(pt[gpr][:, hb * 128:(hb + 1) * 128],
                                        xln[pr][:, ht * 128:(ht + 1) * 128], sb_id[:, :])))
                    vp = sP.inc()
                    q.vector.append(W(sP, vp))
                    if _noevac:
                        q.vector.append(lambda e: e.engine_nop().then_inc(sV2.h, 1))
                    else:
                        for hb in range(4):
                            ht = g * 4 + hb
                            last = hb == 3
                            q.vector.append(lambda e, gpr=gpr, hb=hb, ht=ht, lt=lt, last=last: (
                                e.tensor_copy(out=sb_xlnT[:, ht, lt * 128:(lt + 1) * 128],
                                              in_=pt[gpr][:, hb * 128:(hb + 1) * 128]).then_inc(sV2.h, 1)
                                if last else
                                e.tensor_copy(out=sb_xlnT[:, ht, lt * 128:(lt + 1) * 128],
                                              in_=pt[gpr][:, hb * 128:(hb + 1) * 128])))
                    evac_done[gi] = sV2.inc()
                    gi += 1
            if "xlnT" in taps:
                q.sync.append(W(sV2, sV2.n))
                q.sync.append(lambda e: e.dma_start(
                    out=d_taps["xlnT"], in_=sb_xlnT[:, :, :].rearrange("p a b -> p (a b)")).then_inc(sSt.h, 16))
                sSt.n += 16
                q.sync.append(W(sSt, sSt.n))
            emit_block(nc, q)

        if _skip("ph1"):
            return _stop()
        q = Q()
        with ExitStack() as ph:
            winb = [ph.enter_context(nc.sbuf_tensor(f"t_win{i}", [128, NKT * 128], BF16)) for i in range(2)]
            hpad = ph.enter_context(nc.sbuf_tensor("t_hpad", [128, L + 8], BF16))
            diag = ph.enter_context(nc.sbuf_tensor("t_diag", [128, 4 * 128], BF16))
            ubuf = ph.enter_context(nc.sbuf_tensor("t_ubuf", [128, L], BF16))
            gbuf = ph.enter_context(nc.sbuf_tensor("t_gbuf", [128, L], BF16))
            pps = ph.enter_context(nc.psum_tensor("t_pps", [128, L], F32))
            cps = ph.enter_context(nc.psum_tensor("t_cps", [128, L], F32))

            q.vector.append(lambda e: e.memset(hpad[:, 0:8], 0.0).then_inc(sV.h, 1))
            sV.inc()

            proj_done = {}
            for et in range(64):
                pr = et % 2
                hidden = et < NET
                if et >= 2:
                    q.gpsimd.append(W(sP, proj_done[et - 2]))
                q.gpsimd.append(lambda e, et=et, pr=pr: e.dma_start(
                    out=winb[pr][:, :].rearrange("p (t e) -> p t e", e=128),
                    in_=d_win[:, et * 128:(et + 1) * 128].rearrange("(t p) e -> p t e", p=128)).then_inc(sGd.h, 16))
                sGd.n += 16
                vg = sGd.n
                q.gpsimd.append(W(sGd, vg))

                q.tensor.append(W(sGd, vg))
                if et >= 1:
                    q.tensor.append(W(sA, sA.n))
                for lt in range(4):
                    for kt in range(NKT):
                        fin = (lt == 3 and kt == NKT - 1)
                        q.tensor.append(lambda e, pr=pr, lt=lt, kt=kt, fin=fin: (
                            e.matmul(pps[:, lt * 512:(lt + 1) * 512],
                                     winb[pr][:, kt * 128:(kt + 1) * 128],
                                     sb_xlnT[:, kt, lt * 512:(lt + 1) * 512],
                                     start=(kt == 0), stop=(kt == NKT - 1)
                                     ).then_inc(sP.h, 1) if fin else
                            e.matmul(pps[:, lt * 512:(lt + 1) * 512],
                                     winb[pr][:, kt * 128:(kt + 1) * 128],
                                     sb_xlnT[:, kt, lt * 512:(lt + 1) * 512],
                                     start=(kt == 0), stop=(kt == NKT - 1))))
                proj_done[et] = sP.inc()

                if hidden:
                    q.scalar.append(W(sP, proj_done[et]))
                    q.scalar.append(lambda e: e.activation(
                        out=hpad[:, 8:8 + L], in_=pps[:, :], func=AF.Copy).then_inc(sA.h, 1))
                    vh = sA.inc()
                    q.vector.append(W(sP, sP.n))
                    q.vector.append(lambda e, et=et: [e.tensor_scalar_mul(
                        diag[:, k * 128:(k + 1) * 128], sb_id[:, :],
                        sb_ck[:, et * 4 + k:et * 4 + k + 1]) for k in range(4)][-1].then_inc(sV.h, 1))
                    vd = sV.inc()
                    q.tensor.append(W(sA, vh))
                    q.tensor.append(W(sV, vd))
                    for lt in range(4):
                        for k in range(4):
                            fin = (lt == 3 and k == 3)
                            q.tensor.append(lambda e, lt=lt, k=k, fin=fin: (
                                e.matmul(cps[:, lt * 512:(lt + 1) * 512],
                                         diag[:, k * 128:(k + 1) * 128],
                                         hpad[:, 5 + k + lt * 512: 5 + k + lt * 512 + 512],
                                         start=(k == 0), stop=(k == 3)).then_inc(sP.h, 1) if fin else
                                e.matmul(cps[:, lt * 512:(lt + 1) * 512],
                                         diag[:, k * 128:(k + 1) * 128],
                                         hpad[:, 5 + k + lt * 512: 5 + k + lt * 512 + 512],
                                         start=(k == 0), stop=(k == 3))))
                    vconv = sP.inc()
                    q.scalar.append(W(sP, vconv))
                    if et >= 1:
                        q.scalar.append(W(sSt, sSt.n))
                    q.scalar.append(lambda e, et=et: e.activation(
                        out=ubuf[:, :], in_=cps[:, :], func=AF.Silu,
                        bias=sb_cb[:, et:et + 1]).then_inc(sA.h, 1))
                    vu = sA.inc()
                    q.sync.append(W(sA, vu))
                    q.sync.append(lambda e, et=et: e.dma_start(
                        out=d_u[et * 128:(et + 1) * 128, :], in_=ubuf[:, :]).then_inc(sSt.h, 16))
                    sSt.n += 16
                else:
                    ge = et - NET
                    q.scalar.append(W(sP, proj_done[et]))
                    if et >= NET + 1:
                        q.scalar.append(W(sSt, sSt.n))
                    q.scalar.append(lambda e: e.activation(
                        out=gbuf[:, :], in_=pps[:, :], func=AF.Silu).then_inc(sA.h, 1))
                    vgate = sA.inc()
                    q.sync.append(W(sA, vgate))
                    q.sync.append(lambda e, ge=ge: e.dma_start(
                        out=d_gate[ge * 128:(ge + 1) * 128, :], in_=gbuf[:, :]).then_inc(sSt.h, 16))
                    sSt.n += 16
            q.sync.append(W(sSt, sSt.n))
            emit_block(nc, q)
        es1.close(); open_stacks.remove(es1)

        if "u" in taps or "gate" in taps:
            q = Q()
            if "u" in taps:
                q.sync.append(lambda e: e.dma_start(out=d_taps["u"], in_=d_u).then_inc(sSt.h, 16))
                sSt.n += 16
            if "gate" in taps:
                q.sync.append(lambda e: e.dma_start(out=d_taps["gate"], in_=d_gate).then_inc(sSt.h, 16))
                sSt.n += 16
            q.sync.append(W(sSt, sSt.n))
            emit_block(nc, q)

        if _skip("ph2"):
            return _stop()
        q = Q()
        with ExitStack() as ph:
            ust = [ph.enter_context(nc.sbuf_tensor(f"t_ust{i}", [128, NET * 512], BF16)) for i in range(2)]
            bcf = ph.enter_context(nc.sbuf_tensor("t_bcf", [32, L], BF16))
            dps = [ph.enter_context(nc.psum_tensor(f"t_dps{i}", [128, 512], F32)) for i in range(2)]
            bps = [ph.enter_context(nc.psum_tensor(f"t_bps{i}", [32, 512], F32)) for i in range(2)]

            evdt = {}
            evbc = {}
            mm2_done = {}
            for lt in range(4):
                pr = lt % 2
                if lt >= 2:
                    q.sync.append(W(sP, mm2_done[lt - 2]))
                q.sync.append(lambda e, lt=lt, pr=pr: e.dma_start(
                    out=ust[pr][:, :].rearrange("p (t m) -> p t m", m=512),
                    in_=d_u[:, lt * 512:(lt + 1) * 512].rearrange("(t p) m -> p t m", p=128)).then_inc(sL.h, 16))
                sL.n += 16
                vl = sL.n
                q.tensor.append(W(sL, vl))
                if lt >= 2:
                    q.tensor.append(W(sA, evdt[lt - 2]))
                    q.tensor.append(W(sV, evbc[lt - 2]))
                for kt in range(NET):
                    q.tensor.append(lambda e, pr=pr, kt=kt: e.matmul(
                        dps[pr][:, :], sb_wx[:, kt * 160:kt * 160 + 128],
                        ust[pr][:, kt * 512:(kt + 1) * 512],
                        start=(kt == 0), stop=(kt == NET - 1)))
                for kt in range(NET):
                    fin = kt == NET - 1
                    q.tensor.append(lambda e, pr=pr, kt=kt, fin=fin: (
                        e.matmul(bps[pr][:, :], sb_wx[:, kt * 160 + 128:kt * 160 + 160],
                                 ust[pr][:, kt * 512:(kt + 1) * 512],
                                 start=(kt == 0), stop=(kt == NET - 1)).then_inc(sP.h, 1) if fin else
                        e.matmul(bps[pr][:, :], sb_wx[:, kt * 160 + 128:kt * 160 + 160],
                                 ust[pr][:, kt * 512:(kt + 1) * 512],
                                 start=(kt == 0), stop=(kt == NET - 1))))
                vp = sP.inc()
                mm2_done[lt] = vp
                q.scalar.append(W(sP, vp))
                q.scalar.append(lambda e, lt=lt, pr=pr: e.activation(
                    out=sb_dtT[:, lt * 512:(lt + 1) * 512], in_=dps[pr][:, :],
                    func=AF.Copy).then_inc(sA.h, 1))
                evdt[lt] = sA.inc()
                q.vector.append(W(sP, vp))
                q.vector.append(lambda e, lt=lt, pr=pr: e.tensor_copy(
                    out=bcf[:, lt * 512:(lt + 1) * 512], in_=bps[pr][:, :]).then_inc(sV.h, 1))
                evbc[lt] = sV.inc()
            q.sync.append(W(sV, sV.n))
            q.sync.append(lambda e: e.dma_start(out=d_bcbf, in_=bcf[:, :]).then_inc(sSt.h, 16))
            sSt.n += 16
            if "dtT" in taps:
                q.sync.append(W(sA, sA.n))
                q.sync.append(lambda e: e.dma_start(out=d_taps["dtT"], in_=sb_dtT[:, :]).then_inc(sSt.h, 16))
                sSt.n += 16
            if "bc" in taps:
                q.sync.append(lambda e: e.dma_start(out=d_taps["bc"], in_=bcf[:, :]).then_inc(sSt.h, 16))
                sSt.n += 16
            q.sync.append(W(sSt, sSt.n))
            emit_block(nc, q)

        if _skip("ph3"):
            return _stop()
        for half in range(2):
            q = Q()
            with ExitStack() as ph:
                bcb = ph.enter_context(nc.sbuf_tensor(f"t_bcb{half}", [128, 16, L], BF16))
                t1 = ph.enter_context(nc.sbuf_tensor(f"t_t1{half}", [128, L], F32))
                t2 = ph.enter_context(nc.sbuf_tensor(f"t_t2{half}", [128, L], F32))
                dts = ph.enter_context(nc.sbuf_tensor(f"t_dts{half}", [128, L], F32))
                du = ph.enter_context(nc.sbuf_tensor(f"t_du{half}", [128, L], BF16))
                uet = [ph.enter_context(nc.sbuf_tensor(f"t_uet{half}_{i}", [128, L], BF16)) for i in range(2)]
                dab = [ph.enter_context(nc.sbuf_tensor(f"t_dab{half}_{i}", [128, L], F32)) for i in range(2)]
                dbu = ph.enter_context(nc.sbuf_tensor(f"t_dbu{half}", [128, L], BF16))
                hh = ph.enter_context(nc.sbuf_tensor(f"t_hh{half}", [128, L], BF16))
                zb = [ph.enter_context(nc.sbuf_tensor(f"t_z{half}_{i}", [128, L], BF16)) for i in range(4)]
                yt = ph.enter_context(nc.sbuf_tensor(f"t_yt{half}", [128, L], F32))
                yg = ph.enter_context(nc.sbuf_tensor(f"t_yg{half}", [128, L], BF16))
                if half == 1:
                    get_ = [ph.enter_context(nc.sbuf_tensor(f"t_get{half}_{i}", [128, L], BF16)) for i in range(2)]
                    yh0b = [ph.enter_context(nc.sbuf_tensor(f"t_yh0{half}_{i}", [128, L], BF16)) for i in range(2)]
                sps = [ph.enter_context(nc.psum_tensor(f"t_sps{half}_{i}", [128, L], F32)) for i in range(2)]

                for i in range(8):
                    q.sync.append(lambda e, i=i, half=half: e.dma_start(
                        out=bcb[:, i, :],
                        in_=d_bcbf[half * 8 + i: half * 8 + i + 1, :].to_broadcast((128, L))).then_inc(sL.h, 16))
                    sL.n += 16
                    q.sync.append(lambda e, i=i, half=half: e.dma_start(
                        out=bcb[:, 8 + i, :],
                        in_=d_bcbf[16 + half * 8 + i: 16 + half * 8 + i + 1, :].to_broadcast((128, L))).then_inc(sL.h, 16))
                    sL.n += 16
                vbc = sL.n

                scan_base = sV2.n
                sidx = 0
                psum_free = {}
                yg_spill = {}
                for et in range(NET):
                    pr = et % 2
                    if et >= 2:
                        q.sync.append(W(sV, sV.n))
                    q.sync.append(lambda e, et=et, pr=pr: e.dma_start(
                        out=uet[pr][:, :], in_=d_u[et * 128:(et + 1) * 128, :]).then_inc(sL.h, 16))
                    sL.n += 16
                    if half == 1:
                        q.sync.append(lambda e, et=et, pr=pr: e.dma_start(
                            out=get_[pr][:, :], in_=d_gate[et * 128:(et + 1) * 128, :]).then_inc(sL.h, 16))
                        sL.n += 16
                        q.sync.append(lambda e, et=et, pr=pr: e.dma_start(
                            out=yh0b[pr][:, :], in_=d_yh0[et * 128:(et + 1) * 128, :]).then_inc(sL.h, 16))
                        sL.n += 16
                    vload = sL.n

                    if et >= 2:
                        q.tensor.append(W(sA, psum_free[et - 2]))
                    for lt in range(4):
                        fin = lt == 3
                        q.tensor.append(lambda e, pr=pr, et=et, lt=lt, fin=fin: (
                            e.matmul(sps[pr][:, lt * 512:(lt + 1) * 512],
                                     sb_wdt[:, et * 128:(et + 1) * 128],
                                     sb_dtT[:, lt * 512:(lt + 1) * 512],
                                     start=True, stop=True).then_inc(sP.h, 1) if fin else
                            e.matmul(sps[pr][:, lt * 512:(lt + 1) * 512],
                                     sb_wdt[:, et * 128:(et + 1) * 128],
                                     sb_dtT[:, lt * 512:(lt + 1) * 512],
                                     start=True, stop=True)))
                    vdts = sP.inc()

                    q.scalar.append(W(sP, vdts))
                    if et >= 1:
                        q.scalar.append(W(sV, sV.n))
                    q.scalar.append(lambda e, pr=pr, et=et: e.activation(
                        out=t1[:, :], in_=sps[pr][:, :], func=AF.Abs, bias=sb_bdt[:, et:et + 1]))
                    q.scalar.append(lambda e, pr=pr, et=et: e.activation(
                        out=t2[:, :], in_=sps[pr][:, :], func=AF.Relu,
                        bias=sb_bdt[:, et:et + 1]).then_inc(sA.h, 1))
                    psum_free[et] = sA.inc()
                    q.scalar.append(lambda e: e.activation(out=t1[:, :], in_=t1[:, :], func=AF.Exp, scale=-1.0))
                    q.scalar.append(lambda e: e.activation(out=t1[:, :], in_=t1[:, :], func=AF.Ln,
                                                           bias=1.0).then_inc(sA.h, 1))
                    vsp = sA.inc()

                    q.vector.append(W(sA, vsp))
                    q.vector.append(W(sL, vload))
                    q.vector.append(lambda e: e.tensor_add(dts[:, :], t1[:, :], t2[:, :]).then_inc(sV.h, 1))
                    vdtsd = sV.inc()
                    q.vector.append(lambda e, pr=pr: e.tensor_mul(du[:, :], dts[:, :], uet[pr][:, :]))

                    for nn in range(8):
                        n = half * 8 + nn
                        npr = sidx % 2
                        q.scalar.append(W(sV, vdtsd))
                        if sidx >= 2:
                            q.scalar.append(W(sV2, scan_base + sidx - 1))
                        q.scalar.append(lambda e, npr=npr, et=et, n=n: e.activation(
                            out=dab[npr][:, :], in_=dts[:, :], func=AF.Exp,
                            scale=sb_aneg[:, et * NST + n:et * NST + n + 1]).then_inc(sA2.h, 1))
                        vda = sA2.inc()
                        q.vector.append(W(sA2, vda))
                        if et == 0 and nn == 0:
                            q.vector.append(W(sL, vbc))
                        q.vector.append(lambda e, nn=nn: e.tensor_mul(dbu[:, :], du[:, :], bcb[:, nn, :]))
                        q.vector.append(lambda e, npr=npr: e.tensor_tensor_scan(
                            out=hh[:, :], data0=dab[npr][:, :], data1=dbu[:, :],
                            initial=0.0, op0=ALU.mult, op1=ALU.add).then_inc(sV2.h, 1))
                        sV2.inc()
                        sidx += 1
                        q.vector.append(lambda e, nn=nn: e.tensor_mul(zb[nn % 4][:, :], hh[:, :], bcb[:, 8 + nn, :]))
                        if nn % 2 == 1:
                            q.vector.append(lambda e, nn=nn: e.tensor_add(
                                zb[nn - 1 if nn < 4 else nn - 5][:, :],
                                zb[(nn - 1) % 4][:, :], zb[nn % 4][:, :]))
                        if nn == 3:
                            q.vector.append(lambda e: e.tensor_add(yt[:, :], zb[0][:, :], zb[2][:, :]))
                        if nn == 7:
                            q.vector.append(lambda e: e.tensor_add(zb[0][:, :], zb[0][:, :], zb[2][:, :]))
                            q.vector.append(lambda e: e.tensor_add(yt[:, :], yt[:, :], zb[0][:, :]))

                    if half == 0:
                        if et >= 1:
                            q.vector.append(W(sSt, yg_spill[et - 1]))
                        q.vector.append(lambda e: e.tensor_copy(out=yg[:, :], in_=yt[:, :]).then_inc(sV.h, 1))
                        vyh = sV.inc()
                        q.sync.append(W(sV, vyh))
                        q.sync.append(lambda e, et=et: e.dma_start(
                            out=d_yh0[et * 128:(et + 1) * 128, :], in_=yg[:, :]).then_inc(sSt.h, 16))
                        sSt.n += 16
                        yg_spill[et] = sSt.n
                    else:
                        q.vector.append(lambda e, pr=pr: e.tensor_add(yt[:, :], yt[:, :], yh0b[pr][:, :]))
                        q.vector.append(lambda e, pr=pr, et=et: e.scalar_tensor_tensor(
                            out=yt[:, :], in0=uet[pr][:, :], scalar=sb_D[:, et:et + 1], in1=yt[:, :],
                            op0=ALU.mult, op1=ALU.add))
                        if et >= 1:
                            q.vector.append(W(sSt, yg_spill[et - 1]))
                        q.vector.append(lambda e, pr=pr: e.tensor_mul(
                            yg[:, :], yt[:, :], get_[pr][:, :]).then_inc(sV.h, 1))
                        vygd = sV.inc()
                        q.sync.append(W(sV, vygd))
                        q.sync.append(lambda e, et=et: e.dma_start(
                            out=d_yg[et * 128:(et + 1) * 128, :], in_=yg[:, :]).then_inc(sSt.h, 16))
                        sSt.n += 16
                        yg_spill[et] = sSt.n
                q.sync.append(W(sSt, sSt.n))
                emit_block(nc, q)

        if "yg" in taps:
            q = Q()
            q.sync.append(lambda e: e.dma_start(out=d_taps["yg"], in_=d_yg).then_inc(sSt.h, 16))
            sSt.n += 16
            q.sync.append(W(sSt, sSt.n))
            emit_block(nc, q)

        if _skip("ph4"):
            return _stop()
        q = Q()
        with ExitStack() as ph:
            wob = ph.enter_context(nc.sbuf_tensor("t_wob", [128, NET * 512], BF16))
            yst = [ph.enter_context(nc.sbuf_tensor(f"t_yst{i}", [128, NET * 128], BF16)) for i in range(2)]
            aev = [ph.enter_context(nc.sbuf_tensor(f"t_aev{i}", [128, 512], F32)) for i in range(2)]
            ops_ = [ph.enter_context(nc.psum_tensor(f"t_ops{i}", [128, 512], F32)) for i in range(2)]

            ev_done = {}
            mm_done = {}
            for htl in range(4):
                if htl > 0:
                    q.gpsimd.append(W(sP, sP.n))
                q.gpsimd.append(lambda e, htl=htl: e.dma_start(
                    out=wob[:, :].rearrange("p (t h) -> p t h", h=512),
                    in_=d_wout[:, htl * 512:(htl + 1) * 512].rearrange("(t p) h -> p t h", p=128)).then_inc(sGd.h, 16))
                sGd.n += 16
                vw = sGd.n
                q.gpsimd.append(W(sGd, vw))
                for lt in range(NLT):
                    pr = lt % 2
                    gl = htl * NLT + lt
                    if gl >= 2:
                        q.sync.append(W(sP, mm_done[gl - 2]))
                    q.sync.append(lambda e, lt=lt, pr=pr: e.dma_start(
                        out=yst[pr][:, :].rearrange("p (t m) -> p t m", m=128),
                        in_=d_yg[:, lt * 128:(lt + 1) * 128].rearrange("(t p) m -> p t m", p=128)).then_inc(sL.h, 16))
                    sL.n += 16
                    vy = sL.n
                    q.tensor.append(W(sGd, vw))
                    q.tensor.append(W(sL, vy))
                    if gl >= 2:
                        q.tensor.append(W(sA, ev_done[gl - 2]))
                    for kt in range(NET):
                        fin = kt == NET - 1
                        q.tensor.append(lambda e, pr=pr, kt=kt, fin=fin: (
                            e.matmul(ops_[pr][:, :], yst[pr][:, kt * 128:(kt + 1) * 128],
                                     wob[:, kt * 512:(kt + 1) * 512],
                                     start=(kt == 0), stop=(kt == NET - 1)).then_inc(sP.h, 1) if fin else
                            e.matmul(ops_[pr][:, :], yst[pr][:, kt * 128:(kt + 1) * 128],
                                     wob[:, kt * 512:(kt + 1) * 512],
                                     start=(kt == 0), stop=(kt == NET - 1))))
                    vp = sP.inc()
                    mm_done[gl] = vp
                    q.scalar.append(W(sP, vp))
                    if gl >= 2:
                        q.scalar.append(W(sSt, sSt.n))
                    q.scalar.append(lambda e, pr=pr: e.activation(
                        out=aev[pr][:, :], in_=ops_[pr][:, :], func=AF.Copy).then_inc(sA.h, 1))
                    ev_done[gl] = sA.inc()
                    q.sync.append(W(sA, ev_done[gl]))
                    q.sync.append(lambda e, lt=lt, htl=htl, pr=pr: e.dma_start(
                        out=d_attn[lt * 128:(lt + 1) * 128, htl * 512:(htl + 1) * 512],
                        in_=aev[pr][:, :]).then_inc(sSt.h, 16))
                    sSt.n += 16
            q.sync.append(W(sSt, sSt.n))
            if "attn" in taps:
                q.sync.append(lambda e: e.dma_start(out=d_taps["attn"], in_=d_attn).then_inc(sSt.h, 16))
                sSt.n += 16
                q.sync.append(W(sSt, sSt.n))
            emit_block(nc, q)

        if _skip("ph5"):
            return _stop()
        es3 = ExitStack(); open_stacks.append(es3)
        sb_xTp = es3.enter_context(nc.sbuf_tensor("p_xTp", [128, 16, L], BF16))
        es2 = ExitStack(); open_stacks.append(es2)
        sb_x1ln = es2.enter_context(nc.sbuf_tensor("p_x1ln", [128, NLT, H], BF16))
        q = Q()
        with ExitStack() as ph:
            at = [ph.enter_context(nc.sbuf_tensor("t_at0", [128, H], F32))]
            xt = [ph.enter_context(nc.sbuf_tensor("t_xt50", [128, H], F32))]
            x1t = [ph.enter_context(nc.sbuf_tensor(f"t_x1t{i}", [128, H], F32)) for i in range(2)]
            scr = ph.enter_context(nc.sbuf_tensor("t_scr5", [128, H], F32))
            c1 = ph.enter_context(nc.sbuf_tensor("t_c15", [128, 8], F32))

            x1ln_done = {}
            x1_done = {}
            for lt in range(NLT):
                pr = lt % 2
                if lt >= 2:
                    q.sync.append(W(sV, x1ln_done[lt - 2]))
                if lt >= 1:
                    q.sync.append(W(sV, x1_done[lt - 1]))
                q.sync.append(lambda e, lt=lt: e.dma_start(
                    out=at[0][:, :], in_=d_attn[lt * 128:(lt + 1) * 128, :]).then_inc(sL.h, 16))
                sL.n += 16
                q.sync.append(lambda e, lt=lt: e.dma_start(
                    out=xt[0][:, :], in_=d_x[lt * 128:(lt + 1) * 128, :]).then_inc(sL.h, 16))
                sL.n += 16
                vl = sL.n

                q.scalar.append(W(sL, vl))
                va = ln_stats(q, sA, sV, (lambda: at[0][:, :]), scr, c1, 1e-5)
                q.vector.append(W(sA, va))
                q.vector.append(lambda e: e.tensor_scalar(
                    out=scr[:, :], in0=at[0][:, :], scalar1=c1[:, 2:3], scalar2=c1[:, 7:8],
                    op0=ALU.subtract, op1=ALU.mult))
                q.vector.append(lambda e, pr=pr: e.tensor_add(
                    x1t[pr][:, :], xt[0][:, :], scr[:, :]).then_inc(sV.h, 1))
                vx1 = sV.inc()
                x1_done[lt] = vx1
                q.sync.append(W(sV, vx1))
                q.sync.append(lambda e, lt=lt, pr=pr: e.dma_start(
                    out=d_x1[lt * 128:(lt + 1) * 128, :], in_=x1t[pr][:, :]).then_inc(sSt.h, 16))
                sSt.n += 16
                q.scalar.append(W(sV, vx1))
                va2 = ln_stats(q, sA, sV, (lambda pr=pr: x1t[pr][:, :]), scr, c1, 1e-5)
                q.vector.append(W(sA, va2))
                q.vector.append(lambda e, pr=pr, lt=lt: e.tensor_scalar(
                    out=sb_x1ln[:, lt, :], in0=x1t[pr][:, :], scalar1=c1[:, 2:3], scalar2=c1[:, 7:8],
                    op0=ALU.subtract, op1=ALU.mult).then_inc(sV.h, 1))
                x1ln_done[lt] = sV.inc()
            q.sync.append(W(sV, sV.n))
            if "x1" in taps:
                q.sync.append(lambda e: e.dma_start(out=d_taps["x1"], in_=d_x1).then_inc(sSt.h, 16))
                sSt.n += 16
                q.sync.append(W(sSt, sSt.n))
            emit_block(nc, q)

        if _skip("ph6a"):
            return _stop()
        q = Q()
        with ExitStack() as ph:
            pt = [ph.enter_context(nc.psum_tensor(f"t_pt6{i}", [128, 1024], BF16)) for i in range(2)]
            ebase = sV2.n
            gidx = 0
            for kk in range(8):
                for jb in range(2):
                    pid = kk * 2 + jb
                    for g in range(4):
                        gpr = gidx % 2
                        if gidx >= 2:
                            q.tensor.append(W(sV2, ebase + gidx - 1))
                        for li in range(4):
                            lt = g * 4 + li
                            fin = li == 3
                            q.tensor.append(lambda e, gpr=gpr, li=li, lt=lt, kk=kk, jb=jb, fin=fin: (
                                e.transpose(pt[gpr][:, li * 128:(li + 1) * 128],
                                            sb_x1ln[:, lt, kk + 1024 * jb: kk + 1024 * jb + 8 * 127 + 1: 8],
                                            sb_id[:, :]).then_inc(sP.h, 1) if fin else
                                e.transpose(pt[gpr][:, li * 128:(li + 1) * 128],
                                            sb_x1ln[:, lt, kk + 1024 * jb: kk + 1024 * jb + 8 * 127 + 1: 8],
                                            sb_id[:, :])))
                        vp = sP.inc()
                        q.vector.append(W(sP, vp))
                        q.vector.append(lambda e, gpr=gpr, pid=pid, g=g: e.tensor_copy(
                            out=sb_xTp[:, pid, g * 512:(g + 1) * 512],
                            in_=pt[gpr][:, 0:512]).then_inc(sV2.h, 1))
                        sV2.inc()
                        gidx += 1
            q.vector.append(W(sV2, sV2.n))
            if "xtp" in taps:
                q.sync.append(W(sV2, sV2.n))
                q.sync.append(lambda e: e.dma_start(
                    out=d_taps["xtp"], in_=sb_xTp[:, :, :].rearrange("p a b -> p (a b)")).then_inc(sSt.h, 16))
                sSt.n += 16
                q.sync.append(W(sSt, sSt.n))
            emit_block(nc, q)
        es2.close(); open_stacks.remove(es2)

        if _skip("ph6b"):
            return _stop()
        q = Q()
        with ExitStack() as ph:
            lgt = [ph.enter_context(nc.sbuf_tensor(f"t_lgt{i}", [128, 16 * 128], BF16)) for i in range(2)]
            lut = [ph.enter_context(nc.sbuf_tensor(f"t_lut{i}", [128, 16 * 128], BF16)) for i in range(2)]
            sg = [ph.enter_context(nc.sbuf_tensor(f"t_sg{i}", [128, 512], BF16)) for i in range(2)]
            su = [ph.enter_context(nc.sbuf_tensor(f"t_su{i}", [128, 512], BF16)) for i in range(2)]
            hrow = ph.enter_context(nc.sbuf_tensor("t_hrow", [128, L], BF16))
            gps = [ph.enter_context(nc.psum_tensor(f"t_gps{i}", [128, 512], F32)) for i in range(2)]
            ups = [ph.enter_context(nc.psum_tensor(f"t_ups{i}", [128, 512], F32)) for i in range(2)]

            ev_done = {}
            mul_done = {}
            spill_done = {}
            mmct_done = {}
            q.sync.append(W(sCC, 1))
            for ct in range(44):
                pr = ct % 2
                if ct >= 2:
                    q.sync.append(W(sP, mmct_done[ct - 2]))
                q.sync.append(lambda e, ct=ct, pr=pr: e.dma_start(
                    out=lgt[pr][:, :].rearrange("p (pid c) -> p pid c", c=128),
                    in_=d_plgu[:, :, ct * 128:(ct + 1) * 128].rearrange("pid p c -> p pid c")).then_inc(sL.h, 16))
                sL.n += 16
                q.sync.append(lambda e, ct=ct, pr=pr: e.dma_start(
                    out=lut[pr][:, :].rearrange("p (pid c) -> p pid c", c=128),
                    in_=d_plgu[:, :, IFF + ct * 128: IFF + (ct + 1) * 128].rearrange("pid p c -> p pid c")).then_inc(sL.h, 16))
                sL.n += 16
                vl = sL.n
                q.tensor.append(W(sL, vl))
                for lt in range(4):
                    lpr = lt % 2
                    gq = ct * 4 + lt
                    if gq >= 2:
                        q.tensor.append(W(sA, ev_done[gq - 2]))
                    for pid in range(16):
                        q.tensor.append(lambda e, pr=pr, lpr=lpr, pid=pid, lt=lt: e.matmul(
                            gps[lpr][:, :], lgt[pr][:, pid * 128:(pid + 1) * 128],
                            sb_xTp[:, pid, lt * 512:(lt + 1) * 512],
                            start=(pid == 0), stop=(pid == 15)))
                    for pid in range(16):
                        fin = pid == 15
                        q.tensor.append(lambda e, pr=pr, lpr=lpr, pid=pid, lt=lt, fin=fin: (
                            e.matmul(ups[lpr][:, :], lut[pr][:, pid * 128:(pid + 1) * 128],
                                     sb_xTp[:, pid, lt * 512:(lt + 1) * 512],
                                     start=(pid == 0), stop=(pid == 15)).then_inc(sP.h, 1) if fin else
                            e.matmul(ups[lpr][:, :], lut[pr][:, pid * 128:(pid + 1) * 128],
                                     sb_xTp[:, pid, lt * 512:(lt + 1) * 512],
                                     start=(pid == 0), stop=(pid == 15))))
                    vp = sP.inc()
                    q.scalar.append(W(sP, vp))
                    if gq >= 2:
                        q.scalar.append(W(sV, mul_done[gq - 2]))
                    q.scalar.append(lambda e, lpr=lpr: e.activation(
                        out=sg[lpr][:, :], in_=gps[lpr][:, :], func=AF.Silu))
                    q.scalar.append(lambda e, lpr=lpr: e.activation(
                        out=su[lpr][:, :], in_=ups[lpr][:, :], func=AF.Copy).then_inc(sA.h, 1))
                    ev_done[gq] = sA.inc()
                    q.vector.append(W(sA, ev_done[gq]))
                    if ct >= 1 and lt == 0:
                        q.vector.append(W(sSt, spill_done[ct - 1]))
                    q.vector.append(lambda e, lt=lt, lpr=lpr: e.tensor_mul(
                        hrow[:, lt * 512:(lt + 1) * 512], sg[lpr][:, :], su[lpr][:, :]).then_inc(sV.h, 1))
                    mul_done[gq] = sV.inc()
                mmct_done[ct] = sP.n
                q.sync.append(W(sV, mul_done[ct * 4 + 3]))
                q.sync.append(lambda e, ct=ct: e.dma_start(
                    out=d_hact[ct * 128:(ct + 1) * 128, :], in_=hrow[:, :]).then_inc(sSt.h, 16))
                sSt.n += 16
                spill_done[ct] = sSt.n
            q.sync.append(W(sSt, sSt.n))
            if "hact" in taps:
                q.sync.append(lambda e: e.dma_start(out=d_taps["hact"], in_=d_hact).then_inc(sSt.h, 16))
                sSt.n += 16
                q.sync.append(W(sSt, sSt.n))
            emit_block(nc, q)
        es3.close(); open_stacks.remove(es3)

        if _skip("ph6c"):
            return _stop()
        q = Q()
        with ExitStack() as ph:
            wdn = [ph.enter_context(nc.sbuf_tensor(f"t_wdn{i}", [128, 44 * 512], BF16)) for i in range(2)]
            hst8 = ph.enter_context(nc.sbuf_tensor("t_hst8", [128, 8, 44 * 128], BF16))
            fev = [ph.enter_context(nc.sbuf_tensor(f"t_fev{i}", [128, 512], F32)) for i in range(2)]
            dps6 = [ph.enter_context(nc.psum_tensor(f"t_dps6{i}", [128, 512], F32)) for i in range(2)]

            ev_done = {}
            mm_done = {}
            gl = 0
            q.sync.append(W(sCC, 2))
            q.sync.append(W(sSt, sSt.n))
            for hf in range(2):
                if hf > 0:
                    q.sync.append(W(sP, mm_done[gl - 1]))
                for lt8 in range(8):
                    q.sync.append(lambda e, lt8=lt8, hf=hf: e.dma_start(
                        out=hst8[:, lt8, :].rearrange("p (t m) -> p t m", m=128),
                        in_=d_hact[:, (hf * 8 + lt8) * 128:(hf * 8 + lt8 + 1) * 128].rearrange(
                            "(t p) m -> p t m", p=128)).then_inc(sL.h, 16))
                    sL.n += 16
                vh = sL.n
                for htl in range(4):
                    g6 = hf * 4 + htl
                    prw = g6 % 2
                    if g6 >= 2:
                        q.gpsimd.append(W(sP, mm_done[g6 * 8 - 9]))
                    q.gpsimd.append(lambda e, htl=htl, prw=prw: e.dma_start(
                        out=wdn[prw][:, :].rearrange("p (t h) -> p t h", h=512),
                        in_=d_pldn[:, htl * 512:(htl + 1) * 512].rearrange(
                            "(t p) h -> p t h", p=128)).then_inc(sGd.h, 16))
                    sGd.n += 16
                    vw = sGd.n
                    for lt8 in range(8):
                        lt = hf * 8 + lt8
                        pr = gl % 2
                        q.tensor.append(W(sL, vh))
                        q.tensor.append(W(sGd, vw))
                        if gl >= 2:
                            q.tensor.append(W(sA, ev_done[gl - 2]))
                        for kt in range(44):
                            fin = kt == 43
                            q.tensor.append(lambda e, pr=pr, prw=prw, lt8=lt8, kt=kt, fin=fin: (
                                e.matmul(dps6[pr][:, :], hst8[:, lt8, kt * 128:(kt + 1) * 128],
                                         wdn[prw][:, kt * 512:(kt + 1) * 512],
                                         start=(kt == 0), stop=(kt == 43)).then_inc(sP.h, 1) if fin else
                                e.matmul(dps6[pr][:, :], hst8[:, lt8, kt * 128:(kt + 1) * 128],
                                         wdn[prw][:, kt * 512:(kt + 1) * 512],
                                         start=(kt == 0), stop=(kt == 43))))
                        vp = sP.inc()
                        mm_done[gl] = vp
                        q.scalar.append(W(sP, vp))
                        if gl >= 2:
                            q.scalar.append(W(sSt, sSt.n))
                        q.scalar.append(lambda e, pr=pr: e.activation(
                            out=fev[pr][:, :], in_=dps6[pr][:, :], func=AF.Copy).then_inc(sA.h, 1))
                        ev_done[gl] = sA.inc()
                        q.sync.append(W(sA, ev_done[gl]))
                        q.sync.append(lambda e, lt=lt, htl=htl, pr=pr: e.dma_start(
                            out=d_ffn[lt * 128:(lt + 1) * 128, htl * 512:(htl + 1) * 512],
                            in_=fev[pr][:, :]).then_inc(sSt.h, 16))
                        sSt.n += 16
                        gl += 1
            q.sync.append(W(sSt, sSt.n))
            if "ffn" in taps:
                q.sync.append(lambda e: e.dma_start(out=d_taps["ffn"], in_=d_ffn).then_inc(sSt.h, 16))
                sSt.n += 16
                q.sync.append(W(sSt, sSt.n))
            emit_block(nc, q)

        if _skip("ph7"):
            return _stop()
        q = Q()
        with ExitStack() as ph:
            ft = [ph.enter_context(nc.sbuf_tensor(f"t_ft{i}", [128, H], F32)) for i in range(2)]
            x1t = [ph.enter_context(nc.sbuf_tensor(f"t_x1t7{i}", [128, H], F32)) for i in range(2)]
            vt = ph.enter_context(nc.sbuf_tensor("t_vt", [128, H], F32))
            ot = [ph.enter_context(nc.sbuf_tensor(f"t_ot{i}", [128, H], F32)) for i in range(2)]
            scr = ph.enter_context(nc.sbuf_tensor("t_scr7", [128, H], F32))
            c1 = ph.enter_context(nc.sbuf_tensor("t_c17", [128, 8], F32))

            out_done = {}
            for lt in range(NLT):
                pr = lt % 2
                if lt >= 2:
                    q.sync.append(W(sV, out_done[lt - 2]))
                q.sync.append(lambda e, lt=lt, pr=pr: e.dma_start(
                    out=ft[pr][:, :], in_=d_ffn[lt * 128:(lt + 1) * 128, :]).then_inc(sL.h, 16))
                sL.n += 16
                q.sync.append(lambda e, lt=lt, pr=pr: e.dma_start(
                    out=x1t[pr][:, :], in_=d_x1[lt * 128:(lt + 1) * 128, :]).then_inc(sL.h, 16))
                sL.n += 16
                vl = sL.n
                q.scalar.append(W(sL, vl))
                q.scalar.append(lambda e, pr=pr: e.activation(
                    out=scr[:, :], in_=ft[pr][:, :], func=AF.Square, accum_out=c1[:, 0:1]).then_inc(sA.h, 1))
                va = sA.inc()
                q.vector.append(W(sA, va))
                q.vector.append(lambda e: e.tensor_scalar(
                    out=c1[:, 1:2], in0=c1[:, 0:1], scalar1=1.0 / H, scalar2=1e-6,
                    op0=ALU.mult, op1=ALU.add).then_inc(sV.h, 1))
                vv = sV.inc()
                q.scalar.append(W(sV, vv))
                q.scalar.append(lambda e: e.activation(out=c1[:, 2:3], in_=c1[:, 1:2], func=AF.Ln).then_inc(sA.h, 1))
                vln = sA.inc()
                q.scalar.append(W(sA, vln))
                q.scalar.append(lambda e: e.activation(out=c1[:, 3:4], in_=c1[:, 2:3], func=AF.Exp,
                                                       scale=-0.5).then_inc(sA.h, 1))
                va2 = sA.inc()
                q.vector.append(W(sA, va2))
                q.vector.append(lambda e, pr=pr: e.tensor_scalar_mul(
                    vt[:, :], ft[pr][:, :], c1[:, 3:4]).then_inc(sV.h, 1))
                vv2 = sV.inc()
                q.scalar.append(W(sV, vv2))
                va3 = ln_stats(q, sA, sV, (lambda: vt[:, :]), scr, c1, 1e-5)
                q.vector.append(W(sA, va3))
                q.vector.append(lambda e: e.tensor_scalar(
                    out=scr[:, :], in0=vt[:, :], scalar1=c1[:, 2:3], scalar2=c1[:, 7:8],
                    op0=ALU.subtract, op1=ALU.mult))
                q.vector.append(lambda e, pr=pr: e.tensor_add(
                    ot[pr][:, :], x1t[pr][:, :], scr[:, :]).then_inc(sV.h, 1))
                out_done[lt] = sV.inc()
                q.sync.append(W(sV, out_done[lt]))
                q.sync.append(lambda e, lt=lt, pr=pr: e.dma_start(
                    out=d_out[lt * 128:(lt + 1) * 128, :], in_=ot[pr][:, :]).then_inc(sSt.h, 16))
                sSt.n += 16
            q.sync.append(W(sSt, sSt.n))
            emit_block(nc, q)

    return nc


def _prep_inputs(inputs):
    x = np.ascontiguousarray(np.asarray(inputs["L0_attn_input_states"], np.float32))
    common = dict(
        w_in=np.ascontiguousarray(np.asarray(inputs["L0_attn_W_in"], np.float32)),
        conv_k=np.ascontiguousarray(np.asarray(inputs["L0_attn_conv_kernel"], np.float32).reshape(E, 4)),
        conv_b=np.ascontiguousarray(np.asarray(inputs["L0_attn_conv_bias"], np.float32).reshape(NET, 128).T),
        w_x=np.ascontiguousarray(np.asarray(inputs["L0_attn_W_x"], np.float32)),
        w_dt=np.ascontiguousarray(np.asarray(inputs["L0_attn_W_dt"], np.float32)),
        b_dt=np.ascontiguousarray(np.asarray(inputs["L0_attn_b_dt"], np.float32).reshape(NET, 128).T),
        w_out=np.ascontiguousarray(np.asarray(inputs["L0_attn_W_out"], np.float32)),
        a_log=np.ascontiguousarray(np.asarray(inputs["L0_attn_A_log"], np.float32)),
        dvec=np.ascontiguousarray(np.asarray(inputs["L0_attn_D"], np.float32).reshape(NET, 128).T),

        s_gu=np.ascontiguousarray(np.asarray(inputs["L0_ffn_gate_up_scales"], np.float16)[:, _SGU_PERM]),

        s_dn=np.ascontiguousarray(np.asarray(inputs["L0_ffn_down_scales"], np.float16)),
        ident=np.eye(128, dtype=np.float32),
    )
    pgu_t = np.ascontiguousarray(np.asarray(inputs["L0_ffn_gate_up_packed"], np.int32).T)
    pdn_t = np.ascontiguousarray(np.asarray(inputs["L0_ffn_down_packed"], np.int32).T)
    pgu16 = pgu_t.view(np.int16)
    pdn16 = pdn_t.view(np.int16)
    in_maps = []
    for b in range(8):
        m = dict(common)
        m["x"] = np.ascontiguousarray(x[b])
        h16 = b // 4
        kpos = b % 4
        m["pgu_h"] = np.ascontiguousarray(pgu16[:, h16::2][:, _SGU_PERM])
        m["pdn_h"] = np.ascontiguousarray(pdn16[:, h16::2])
        sh = np.empty((128, 2), np.int32)
        sh[:, 0] = 4 * kpos
        sh[:, 1] = 12 - 4 * kpos
        m["dqsh"] = sh
        in_maps.append(m)
    return in_maps


def kernel(**inputs):
    in_maps = _prep_inputs(inputs)
    nc = build_program(taps=DEBUG_TAPS)
    res = run_bass_kernel_spmd(nc, in_maps, core_ids=list(range(8)))
    out = np.stack([np.asarray(r["out"], np.float32) for r in res.results], axis=0)
    if DEBUG_TAPS:
        kernel.last_results = res
    return out


# version 09
# speedup vs baseline: 1.2179x; 1.0723x over previous
import numpy as np
import ml_dtypes
from contextlib import ExitStack

import concourse.bass as bass
import concourse.mybir as mybir
from concourse.bass_utils import run_bass_kernel_spmd

F32 = mybir.dt.float32
BF16 = mybir.dt.bfloat16
F16 = mybir.dt.float16
I16 = mybir.dt.int16
I32 = mybir.dt.int32
AF = mybir.ActivationFunctionType
ALU = mybir.AluOpType

L = 2048
H = 2048
E = 4096
NST = 16
TR = 128
IFF = 5632
OGU = 11264
NET = 32
NKT = 16
NLT = 16

DEBUG_TAPS = ()

_c = np.arange(OGU if False else 11264)
_half = _c // 5632
_r = _c % 5632
_SGU_PERM = _half * 5632 + 8 * (_r % 704) + (_r // 704)


class Sem:
    def __init__(self, stack, nc, name):
        self.h = stack.enter_context(nc.semaphore(name))
        self.n = 0

    def inc(self, k=1):
        self.n += k
        return self.n


class Q:
    def __init__(self):
        self.sync = []
        self.tensor = []
        self.vector = []
        self.scalar = []
        self.gpsimd = []


def emit_block(nc, q):
    with nc.Block() as block:
        if q.sync:
            block.sync(lambda e, ops=tuple(q.sync): [f(e) for f in ops])
        if q.tensor:
            block.tensor(lambda e, ops=tuple(q.tensor): [f(e) for f in ops])
        if q.vector:
            block.vector(lambda e, ops=tuple(q.vector): [f(e) for f in ops])
        if q.scalar:
            block.scalar(lambda e, ops=tuple(q.scalar): [f(e) for f in ops])
        if q.gpsimd:
            block.gpsimd(lambda e, ops=tuple(q.gpsimd): [f(e) for f in ops])


def W(sem, v):
    assert v <= sem.n, f"wait beyond planned count on {sem.h}: {v} > {sem.n}"
    return lambda e, h=sem.h, v=v: e.wait_ge(h, v)


def ln_stats(q, sA, sV, src_f, scr, c1, eps):
    q.scalar.append(lambda e: e.activation(out=scr[:, :], in_=src_f(), func=AF.Identity,
                                           accum_out=c1[:, 0:1]))
    q.scalar.append(lambda e: e.activation(out=scr[:, :], in_=src_f(), func=AF.Square,
                                           accum_out=c1[:, 1:2]).then_inc(sA.h, 1))
    va = sA.inc()
    q.vector.append(W(sA, va))
    q.vector.append(lambda e: e.tensor_scalar_mul(c1[:, 2:3], c1[:, 0:1], 1.0 / H))
    q.vector.append(lambda e: e.tensor_scalar_mul(c1[:, 3:4], c1[:, 1:2], 1.0 / H).then_inc(sV.h, 1))
    g1 = sV.inc()
    q.vector.append(W(sV, g1))
    q.vector.append(lambda e: e.tensor_mul(c1[:, 4:5], c1[:, 2:3], c1[:, 2:3]).then_inc(sV.h, 1))
    g2 = sV.inc()
    q.vector.append(W(sV, g2))
    q.vector.append(lambda e: e.tensor_scalar(
        out=c1[:, 5:6], in0=c1[:, 3:4], scalar1=c1[:, 4:5], scalar2=float(eps),
        op0=ALU.subtract, op1=ALU.add).then_inc(sV.h, 1))
    vv = sV.inc()
    q.scalar.append(W(sV, vv))
    q.scalar.append(lambda e: e.activation(out=c1[:, 6:7], in_=c1[:, 5:6], func=AF.Ln).then_inc(sA.h, 1))
    ga = sA.inc()
    q.scalar.append(W(sA, ga))
    q.scalar.append(lambda e: e.activation(out=c1[:, 7:8], in_=c1[:, 6:7], func=AF.Exp,
                                           scale=-0.5).then_inc(sA.h, 1))
    return sA.inc()


def dequant_ops(q, sV, nib, ta, tb, tc, ts_, wpl, sfull_ap, src, kpos, rows, width):
    r = rows
    q.vector.append(lambda e: e.tensor_scalar(
        out=nib[0:r, 0:width], in0=src, scalar1=int(4 * kpos), scalar2=int(7),
        op0=ALU.logical_shift_right, op1=ALU.bitwise_and))
    q.vector.append(lambda e: e.tensor_scalar(
        out=ta[0:r, 0:width], in0=nib[0:r, 0:width], scalar1=int(2), scalar2=int(0x3F00),
        op0=ALU.min, op1=ALU.mult))
    q.vector.append(lambda e: e.tensor_scalar(
        out=tb[0:r, 0:width], in0=nib[0:r, 0:width], scalar1=int(0x80), scalar2=int(0x3E80),
        op0=ALU.mult, op1=ALU.add))
    q.vector.append(lambda e: e.tensor_scalar(
        out=tc[0:r, 0:width], in0=nib[0:r, 0:width], scalar1=int(0x40), scalar2=int(0x3F00),
        op0=ALU.mult, op1=ALU.add))
    q.vector.append(lambda e: e.tensor_tensor(
        out=ta[0:r, 0:width], in0=ta[0:r, 0:width], in1=tb[0:r, 0:width], op=ALU.min))
    q.vector.append(lambda e: e.tensor_tensor(
        out=ta[0:r, 0:width], in0=ta[0:r, 0:width], in1=tc[0:r, 0:width], op=ALU.min))
    q.vector.append(lambda e: e.tensor_scalar(
        out=ts_[0:r, 0:width], in0=src, scalar1=int(12 - 4 * kpos), scalar2=int(-0x8000),
        op0=ALU.logical_shift_left, op1=ALU.bitwise_and))
    q.vector.append(lambda e: e.tensor_tensor(
        out=ta[0:r, 0:width], in0=ta[0:r, 0:width], in1=ts_[0:r, 0:width], op=ALU.bitwise_or))
    q.vector.append(lambda e: e.tensor_mul(
        wpl[0:r, 0:width], ta[0:r, 0:width].bitcast(BF16), sfull_ap).then_inc(sV.h, 1))
    return sV.inc()




def make_deq_gu_feeder(nc, q, ph, sems, d_pgu, d_sgu_bf, d_plgu):
    sL, sGd, sSt, sV = sems
    pk32 = ph.enter_context(nc.sbuf_tensor("t_pk32", [128, OGU], I32))
    nib = ph.enter_context(nc.sbuf_tensor("t_nib", [128, 704], I16))
    ta = ph.enter_context(nc.sbuf_tensor("t_ta6", [128, 704], I16))
    tb = ph.enter_context(nc.sbuf_tensor("t_tb6", [128, 704], I16))
    tc = ph.enter_context(nc.sbuf_tensor("t_tc6", [128, 704], I16))
    ts_ = ph.enter_context(nc.sbuf_tensor("t_ts6", [128, 704], I16))
    sfull = ph.enter_context(nc.sbuf_tensor("t_sf6", [128, 704], BF16))
    wpl = ph.enter_context(nc.sbuf_tensor("t_wpl", [128, 704], BF16))
    st = {"i": 0, "prev_spill": 0, "loaded_jb": -1}
    order = [(jb, half, kp, kk) for jb in range(2) for half in range(2)
             for kp in range(8) for kk in range(8)]

    def feed(n):
        for _ in range(n):
            if st["i"] >= len(order):
                return
            jb, half, kp, kk = order[st["i"]]
            st["i"] += 1
            if jb != st["loaded_jb"]:
                if jb > 0:
                    q.sync.append(W(sSt, sSt.n))
                q.sync.append(lambda e, jb=jb: e.dma_start(
                    out=pk32[:, :], in_=d_pgu[jb * 128:(jb + 1) * 128, :]).then_inc(sL.h, 16))
                sL.n += 16
                st["vl"] = sL.n
                st["loaded_jb"] = jb
            if kk == 0:
                if st["i"] > 1:
                    q.gpsimd.append(W(sV, sV.n))
                for g in range(8):
                    q.gpsimd.append(lambda e, jb=jb, half=half, kp=kp, g=g: e.dma_start(
                        out=sfull[g * 16:(g + 1) * 16, :],
                        in_=bass.AP(d_sgu_bf.tensor, (jb * 8 + g) * OGU + half * IFF + kp * 704,
                                    [[0, 16], [1, 704]])).then_inc(sGd.h, 16))
                    sGd.n += 16
                st["vsg"] = sGd.n
                q.gpsimd.append(W(sGd, st["vsg"]))
            h16 = kk // 4
            kpos = kk % 4
            pk16 = pk32[:, :].bitcast(I16)
            base = 2 * (half * IFF + kp) + h16
            src = pk16[:, base: base + 16 * 703 + 1: 16]
            q.vector.append(W(sL, st["vl"]))
            q.vector.append(W(sGd, st["vsg"]))
            if st["prev_spill"]:
                q.vector.append(W(sSt, st["prev_spill"]))
            dequant_ops(q, sV, nib, ta, tb, tc, ts_, wpl, sfull[:, :], src, kpos, 128, 704)
            vw = sV.n
            pid = kk * 2 + jb
            q.sync.append(W(sV, vw))
            q.sync.append(lambda e, pid=pid, half=half, kp=kp: e.dma_start(
                out=d_plgu[pid, :, half * IFF + kp * 704: half * IFF + (kp + 1) * 704],
                in_=wpl[:, :]).then_inc(sSt.h, 16))
            sSt.n += 16
            st["prev_spill"] = sSt.n

    def remaining():
        return len(order) - st["i"]
    feed.remaining = remaining
    return feed


def make_deq_dn_feeder(nc, q, ph, sems, d_pdn, d_sdn_bf, d_pldn):
    sL, sGd, sSt, sV = sems
    pk32 = ph.enter_context(nc.sbuf_tensor("t_pk32d", [128, 6 * H], I32))
    nib = ph.enter_context(nc.sbuf_tensor("t_nibd", [128, H], I16))
    ta = ph.enter_context(nc.sbuf_tensor("t_tad", [128, H], I16))
    tb = ph.enter_context(nc.sbuf_tensor("t_tbd", [128, H], I16))
    tc = ph.enter_context(nc.sbuf_tensor("t_tcd", [128, H], I16))
    ts_ = ph.enter_context(nc.sbuf_tensor("t_tsd", [128, H], I16))
    sfull = ph.enter_context(nc.sbuf_tensor("t_sfd", [128, 6 * H], BF16))
    wpl = ph.enter_context(nc.sbuf_tensor("t_wpld", [128, H], BF16))
    st = {"i": 0, "prev_spill": 0, "init": False}
    order = [(kp, pt_) for kp in range(8) for pt_ in range(6)]

    def feed(n):
        if not st["init"]:
            st["init"] = True
            q.sync.append(lambda e: e.dma_start(
                out=pk32[:, 0:5 * H].rearrange("p (t h) -> p t h", t=5),
                in_=d_pdn[0:640, :].rearrange("(t p) h -> p t h", p=128)).then_inc(sL.h, 16))
            sL.n += 16
            q.sync.append(lambda e: e.dma_start(
                out=pk32[0:64, 5 * H:6 * H], in_=d_pdn[640:704, :]).then_inc(sL.h, 16))
            sL.n += 16
            st["vl"] = sL.n
            for pt_ in range(6):
                ng = 8 if pt_ < 5 else 4
                for g in range(ng):
                    q.gpsimd.append(lambda e, pt_=pt_, g=g: e.dma_start(
                        out=sfull[g * 16:(g + 1) * 16, pt_ * H:(pt_ + 1) * H],
                        in_=bass.AP(d_sdn_bf.tensor, (pt_ * 8 + g) * H, [[0, 16], [1, H]])).then_inc(sGd.h, 16))
                    sGd.n += 16
            q.gpsimd.append(W(sGd, sGd.n))
            st["vg"] = sGd.n
        for _ in range(n):
            if st["i"] >= len(order):
                return
            kp, pt_ = order[st["i"]]
            st["i"] += 1
            rows = 128 if pt_ < 5 else 64
            pk16 = pk32[:, :].bitcast(I16)
            h16 = kp // 4
            kpos = kp % 4
            src = pk16[0:rows, 2 * pt_ * H + h16: 2 * pt_ * H + h16 + 2 * (H - 1) + 1: 2]
            q.vector.append(W(sL, st["vl"]))
            q.vector.append(W(sGd, st["vg"]))
            if st["prev_spill"]:
                q.vector.append(W(sSt, st["prev_spill"]))
            dequant_ops(q, sV, nib, ta, tb, tc, ts_, wpl,
                        sfull[0:rows, pt_ * H:(pt_ + 1) * H], src, kpos, rows, H)
            vw = sV.n
            q.sync.append(W(sV, vw))
            q.sync.append(lambda e, kp=kp, pt_=pt_, rows=rows: e.dma_start(
                out=d_pldn[kp * 704 + pt_ * 128: kp * 704 + pt_ * 128 + rows, :],
                in_=wpl[0:rows, :]).then_inc(sSt.h, 16))
            sSt.n += 16
            st["prev_spill"] = sSt.n

    def remaining():
        return len(order) - st["i"]
    feed.remaining = remaining
    return feed


PHASE_ORDER = ["ph0", "ph0b", "ph1", "ph2", "ph3", "ph4",
               "ph5", "ph6a", "ph6b", "ph6c", "ph7"]


def dequant_ops_col(q, sV, nib, ta, tb, tc, ts_, wpl, sfull_ap, src, shc, rows, width):
    r = rows
    q.vector.append(lambda e: e.tensor_single_scalar(
        out=nib[0:r, 0:width], in_=src, scalar=shc[0:r, 0:1],
        op=ALU.logical_shift_right))
    q.vector.append(lambda e: e.tensor_single_scalar(
        out=nib[0:r, 0:width], in_=nib[0:r, 0:width], scalar=int(7),
        op=ALU.bitwise_and))
    q.vector.append(lambda e: e.tensor_scalar(
        out=ta[0:r, 0:width], in0=nib[0:r, 0:width], scalar1=int(2), scalar2=int(0x3F00),
        op0=ALU.min, op1=ALU.mult))
    q.vector.append(lambda e: e.tensor_scalar(
        out=tb[0:r, 0:width], in0=nib[0:r, 0:width], scalar1=int(0x80), scalar2=int(0x3E80),
        op0=ALU.mult, op1=ALU.add))
    q.vector.append(lambda e: e.tensor_scalar(
        out=tc[0:r, 0:width], in0=nib[0:r, 0:width], scalar1=int(0x40), scalar2=int(0x3F00),
        op0=ALU.mult, op1=ALU.add))
    q.vector.append(lambda e: e.tensor_tensor(
        out=ta[0:r, 0:width], in0=ta[0:r, 0:width], in1=tb[0:r, 0:width], op=ALU.min))
    q.vector.append(lambda e: e.tensor_tensor(
        out=ta[0:r, 0:width], in0=ta[0:r, 0:width], in1=tc[0:r, 0:width], op=ALU.min))
    q.vector.append(lambda e: e.tensor_single_scalar(
        out=ts_[0:r, 0:width], in_=src, scalar=shc[0:r, 1:2],
        op=ALU.logical_shift_left))
    q.vector.append(lambda e: e.tensor_single_scalar(
        out=ts_[0:r, 0:width], in_=ts_[0:r, 0:width], scalar=int(-0x8000),
        op=ALU.bitwise_and))
    q.vector.append(lambda e: e.tensor_tensor(
        out=ta[0:r, 0:width], in0=ta[0:r, 0:width], in1=ts_[0:r, 0:width], op=ALU.bitwise_or))
    q.vector.append(lambda e: e.tensor_mul(
        wpl[0:r, 0:width], ta[0:r, 0:width].bitcast(BF16), sfull_ap).then_inc(sV.h, 1))
    return sV.inc()


def build_program(taps=(), upto="ph7"):
    _lim = PHASE_ORDER.index(upto)

    def _skip(name):
        return PHASE_ORDER.index(name) > _lim

    nc = bass.Bass("TRN2", target_bir_lowering=False, detect_race_conditions=False)
    open_stacks = []

    def _stop():
        for st_ in reversed(open_stacks):
            st_.close()
        return nc

    d_x = nc.dram_tensor("x", [L, H], F32, kind="ExternalInput").ap()
    d_win = nc.dram_tensor("w_in", [H, 2 * E], F32, kind="ExternalInput").ap()
    d_ck = nc.dram_tensor("conv_k", [E, 4], F32, kind="ExternalInput").ap()
    d_cb = nc.dram_tensor("conv_b", [128, NET], F32, kind="ExternalInput").ap()
    d_wx = nc.dram_tensor("w_x", [E, 160], F32, kind="ExternalInput").ap()
    d_wdt = nc.dram_tensor("w_dt", [TR, E], F32, kind="ExternalInput").ap()
    d_bdt = nc.dram_tensor("b_dt", [128, NET], F32, kind="ExternalInput").ap()
    d_wout = nc.dram_tensor("w_out", [E, H], F32, kind="ExternalInput").ap()
    d_alog = nc.dram_tensor("a_log", [E, NST], F32, kind="ExternalInput").ap()
    d_D = nc.dram_tensor("dvec", [128, NET], F32, kind="ExternalInput").ap()
    d_pguh = nc.dram_tensor("pgu_h", [256, OGU], I16, kind="ExternalInput").ap()
    d_sgu = nc.dram_tensor("s_gu", [16, OGU], F16, kind="ExternalInput").ap()
    d_pdnh = nc.dram_tensor("pdn_h", [704, H], I16, kind="ExternalInput").ap()
    d_dqsh = nc.dram_tensor("dqsh", [128, 2], I32, kind="ExternalInput").ap()
    d_sdn = nc.dram_tensor("s_dn", [44, H], F16, kind="ExternalInput").ap()
    d_id = nc.dram_tensor("ident", [128, 128], F32, kind="ExternalInput").ap()

    d_u = nc.dram_tensor("sc_u", [E, L], BF16, kind="Internal").ap()
    d_gate = nc.dram_tensor("sc_gate", [E, L], BF16, kind="Internal").ap()
    d_bcbf = nc.dram_tensor("sc_bc", [32, L], BF16, kind="Internal").ap()
    d_yh0 = nc.dram_tensor("sc_yh0", [E, L], BF16, kind="Internal").ap()
    d_yg = nc.dram_tensor("sc_yg", [E, L], BF16, kind="Internal").ap()
    d_attn = nc.dram_tensor("sc_attn", [L, H], F32, kind="Internal").ap()
    d_x1 = nc.dram_tensor("sc_x1", [L, H], F32, kind="Internal").ap()
    d_sgu_bf = nc.dram_tensor("sc_sgu", [16, OGU], BF16, kind="Internal").ap()
    d_sdn_bf = nc.dram_tensor("sc_sdn", [44, H], BF16, kind="Internal").ap()
    d_plgu = nc.dram_tensor("sc_plgu", [16, 128, OGU], BF16, kind="Internal").ap()
    d_plgu_loc = nc.dram_tensor("sc_plgu_loc", [256, OGU], BF16, kind="Internal").ap()
    d_pldn_loc = nc.dram_tensor("sc_pldn_loc", [704, H], BF16, kind="Internal").ap()
    d_hact = nc.dram_tensor("sc_hact", [IFF, L], BF16, kind="Internal").ap()
    d_pldn = nc.dram_tensor("sc_pldn", [IFF, H], BF16, kind="Internal").ap()
    d_ffn = nc.dram_tensor("sc_ffn", [L, H], F32, kind="Internal").ap()

    d_out = nc.dram_tensor("out", [L, H], F32, kind="ExternalOutput").ap()
    d_taps = {}
    for t in taps:
        shp = {"xlnT": [128, NKT * L], "u": [E, L], "gate": [E, L], "dtT": [TR, L],
               "bc": [32, L], "yg": [E, L], "attn": [L, H], "x1": [L, H],
               "hact": [IFF, L], "ffn": [L, H], "plgu": [16 * 128, OGU],
               "xtp": [128, 16 * L]}[t]
        dt = F32 if t in ("attn", "x1", "ffn") else BF16
        d_taps[t] = nc.dram_tensor("tap_" + t, shp, dt, kind="ExternalOutput").ap()

    stack = ExitStack()
    with stack:
        sL = Sem(stack, nc, "sL")
        sGd = Sem(stack, nc, "sGd")
        sSt = Sem(stack, nc, "sSt")
        sP = Sem(stack, nc, "sP")
        sA = Sem(stack, nc, "sA")
        sV = Sem(stack, nc, "sV")
        sA2 = Sem(stack, nc, "sA2")
        sV2 = Sem(stack, nc, "sV2")
        sCC = Sem(stack, nc, "sCC")

        sb_id = stack.enter_context(nc.sbuf_tensor("c_id", [128, 128], BF16))
        sb_aneg = stack.enter_context(nc.sbuf_tensor("c_aneg", [128, NET * NST], F32))
        sb_D = stack.enter_context(nc.sbuf_tensor("c_D", [128, NET], F32))
        sb_cb = stack.enter_context(nc.sbuf_tensor("c_cb", [128, NET], F32))
        sb_bdt = stack.enter_context(nc.sbuf_tensor("c_bdt", [128, NET], F32))
        sb_ck = stack.enter_context(nc.sbuf_tensor("c_ck", [128, NET * 4], F32))
        sb_wdt = stack.enter_context(nc.sbuf_tensor("c_wdt", [128, E], BF16))
        sb_wx = stack.enter_context(nc.sbuf_tensor("c_wx", [128, NET * 160], BF16))
        sb_dtT = stack.enter_context(nc.sbuf_tensor("p_dtT", [128, L], BF16))

        q = Q()
        ph0 = ExitStack()
        sb_alog = ph0.enter_context(nc.sbuf_tensor("t_alog", [128, NET * NST], F32))

        def _ph0_sync(e):
            e.dma_start(out=sb_alog[:, :].rearrange("p (t n) -> p t n", n=NST), in_=d_alog.rearrange("(t p) n -> p t n", p=128)).then_inc(sL.h, 16)
            e.dma_start(out=sb_D[:, :], in_=d_D).then_inc(sL.h, 16)
            e.dma_start(out=sb_cb[:, :], in_=d_cb).then_inc(sL.h, 16)
            e.dma_start(out=sb_bdt[:, :], in_=d_bdt).then_inc(sL.h, 16)
            e.dma_start(out=sb_ck[:, :].rearrange("p (t k) -> p t k", k=4), in_=d_ck.rearrange("(t p) k -> p t k", p=128)).then_inc(sL.h, 16)
        sL.n += 5 * 16
        q.sync.append(_ph0_sync)

        def _ph0_gp(e):
            e.dma_start(out=sb_id[:, :], in_=d_id).then_inc(sGd.h, 16)
            e.dma_start(out=sb_wdt[:, :], in_=d_wdt).then_inc(sGd.h, 16)
            e.dma_start(out=sb_wx[:, :].rearrange("p (t d) -> p t d", d=160), in_=d_wx.rearrange("(t p) d -> p t d", p=128)).then_inc(sGd.h, 16)
            e.dma_start(out=d_sgu_bf, in_=d_sgu).then_inc(sGd.h, 16)
            e.dma_start(out=d_sdn_bf, in_=d_sdn).then_inc(sGd.h, 16)
            e.wait_ge(sGd.h, 80)
        sGd.n += 5 * 16
        q.gpsimd.append(_ph0_gp)

        q.scalar.append(W(sL, sL.n))
        q.scalar.append(lambda e: e.activation(out=sb_aneg[:, :], in_=sb_alog[:, :],
                                               func=AF.Exp).then_inc(sA.h, 1))
        va = sA.inc()
        q.vector.append(W(sA, va))
        q.vector.append(lambda e: e.tensor_scalar_mul(sb_aneg[:, :], sb_aneg[:, :], -1.0).then_inc(sV.h, 1))
        sV.inc()
        emit_block(nc, q)
        ph0.close()


        q = Q()
        with ExitStack() as ph:
            pkg = ph.enter_context(nc.sbuf_tensor("t_pkg", [128, OGU], I16))
            pkd = ph.enter_context(nc.sbuf_tensor("t_pkd", [128, 6 * H], I16))
            shc = ph.enter_context(nc.sbuf_tensor("t_shc", [128, 2], I32))
            nib = ph.enter_context(nc.sbuf_tensor("t_nibq", [128, H], I16))
            ta = ph.enter_context(nc.sbuf_tensor("t_taq", [128, H], I16))
            tb = ph.enter_context(nc.sbuf_tensor("t_tbq", [128, H], I16))
            tc = ph.enter_context(nc.sbuf_tensor("t_tcq", [128, H], I16))
            ts_ = ph.enter_context(nc.sbuf_tensor("t_tsq", [128, H], I16))
            sfull = ph.enter_context(nc.sbuf_tensor("t_sfq", [128, 6 * H], BF16))
            wpl = ph.enter_context(nc.sbuf_tensor("t_wplq", [128, H], BF16))

            q.sync.append(lambda e: e.dma_start(out=shc[:, :], in_=d_dqsh).then_inc(sL.h, 16))
            sL.n += 16
            prev_spill = 0
            for jb in range(2):
                if jb > 0:
                    q.sync.append(W(sSt, sSt.n))
                q.sync.append(lambda e, jb=jb: e.dma_start(
                    out=pkg[:, :], in_=d_pguh[jb * 128:(jb + 1) * 128, :]).then_inc(sL.h, 16))
                sL.n += 16
                vl = sL.n
                for half in range(2):
                    for kp in range(8):
                        if not (jb == 0 and half == 0 and kp == 0):
                            q.gpsimd.append(W(sV, sV.n))
                        for g in range(8):
                            q.gpsimd.append(lambda e, jb=jb, half=half, kp=kp, g=g: e.dma_start(
                                out=sfull[g * 16:(g + 1) * 16, 0:704],
                                in_=bass.AP(d_sgu_bf.tensor, (jb * 8 + g) * OGU + half * IFF + kp * 704,
                                            [[0, 16], [1, 704]])).then_inc(sGd.h, 16))
                            sGd.n += 16
                        vsg = sGd.n
                        q.gpsimd.append(W(sGd, vsg))
                        src = pkg[:, half * IFF + kp * 704: half * IFF + (kp + 1) * 704]
                        q.vector.append(W(sL, vl))
                        q.vector.append(W(sGd, vsg))
                        if prev_spill:
                            q.vector.append(W(sSt, prev_spill))
                        dequant_ops_col(q, sV, nib, ta, tb, tc, ts_, wpl,
                                        sfull[:, 0:704], src, shc, 128, 704)
                        vw = sV.n
                        q.sync.append(W(sV, vw))
                        q.sync.append(lambda e, jb=jb, half=half, kp=kp: e.dma_start(
                            out=d_plgu_loc[jb * 128:(jb + 1) * 128,
                                           half * IFF + kp * 704: half * IFF + (kp + 1) * 704],
                            in_=wpl[:, 0:704]).then_inc(sSt.h, 16))
                        sSt.n += 16
                        prev_spill = sSt.n
            q.sync.append(lambda e: e.dma_start(
                out=pkd[:, 0:5 * H].rearrange("p (t h) -> p t h", t=5),
                in_=d_pdnh[0:640, :].rearrange("(t p) h -> p t h", p=128)).then_inc(sL.h, 16))
            sL.n += 16
            q.sync.append(lambda e: e.dma_start(
                out=pkd[0:64, 5 * H:6 * H], in_=d_pdnh[640:704, :]).then_inc(sL.h, 16))
            sL.n += 16
            vld = sL.n
            for pt_ in range(6):
                ng = 8 if pt_ < 5 else 4
                q.gpsimd.append(W(sV, sV.n))
                for g in range(ng):
                    q.gpsimd.append(lambda e, pt_=pt_, g=g: e.dma_start(
                        out=sfull[g * 16:(g + 1) * 16, pt_ * H:(pt_ + 1) * H],
                        in_=bass.AP(d_sdn_bf.tensor, (pt_ * 8 + g) * H, [[0, 16], [1, H]])).then_inc(sGd.h, 16))
                    sGd.n += 16
                vsg = sGd.n
                q.gpsimd.append(W(sGd, vsg))
                rows = 128 if pt_ < 5 else 64
                src = pkd[0:rows, pt_ * H:(pt_ + 1) * H]
                q.vector.append(W(sL, vld))
                q.vector.append(W(sGd, vsg))
                if prev_spill:
                    q.vector.append(W(sSt, prev_spill))
                dequant_ops_col(q, sV, nib, ta, tb, tc, ts_, wpl,
                                sfull[0:rows, pt_ * H:(pt_ + 1) * H], src, shc, rows, H)
                vw = sV.n
                q.sync.append(W(sV, vw))
                q.sync.append(lambda e, pt_=pt_, rows=rows: e.dma_start(
                    out=d_pldn_loc[pt_ * 128: pt_ * 128 + rows, :],
                    in_=wpl[0:rows, :]).then_inc(sSt.h, 16))
                sSt.n += 16
                prev_spill = sSt.n
            q.gpsimd.append(W(sSt, sSt.n))
            q.gpsimd.append(lambda e: e.collective_compute(
                "AllGather", ALU.bypass, replica_groups=[list(range(8))],
                ins=[d_plgu_loc.opt()], outs=[d_plgu.rearrange("pid p c -> (pid p) c").opt()]).then_inc(sCC.h))
            sCC.n += 1
            q.gpsimd.append(lambda e: e.collective_compute(
                "AllGather", ALU.bypass, replica_groups=[list(range(8))],
                ins=[d_pldn_loc.opt()], outs=[d_pldn.opt()]).then_inc(sCC.h))
            sCC.n += 1
            q.gpsimd.append(W(sCC, 2))
            q.sync.append(W(sSt, sSt.n))
            if "plgu" in taps:
                q.sync.append(W(sCC, 1))
                q.sync.append(lambda e: e.dma_start(out=d_taps["plgu"], in_=d_plgu.rearrange("pid p c -> (pid p) c")).then_inc(sSt.h, 16))
                sSt.n += 16
                q.sync.append(W(sSt, sSt.n))
            emit_block(nc, q)

        if _skip("ph0b"):
            return _stop()
        es1 = ExitStack(); open_stacks.append(es1)
        sb_xlnT = es1.enter_context(nc.sbuf_tensor("p_xlnT", [128, NKT, L], BF16))
        q = Q()
        with ExitStack() as ph:
            xt = [ph.enter_context(nc.sbuf_tensor(f"t_x{i}", [128, H], F32)) for i in range(2)]
            xln = [ph.enter_context(nc.sbuf_tensor(f"t_xl{i}", [128, H], BF16)) for i in range(2)]
            scr = ph.enter_context(nc.sbuf_tensor("t_scr", [128, H], F32))
            c1 = ph.enter_context(nc.sbuf_tensor("t_c1", [128, 8], F32))
            pt = [ph.enter_context(nc.psum_tensor(f"t_pt{i}", [128, 1024], BF16)) for i in range(2)]

            norm_done = {}
            evac_done = {}
            gi = 0
            for lt in range(NLT):
                pr = lt % 2
                if lt >= 2:
                    q.sync.append(W(sV, norm_done[lt - 2]))
                q.sync.append(lambda e, lt=lt, pr=pr: e.dma_start(
                    out=xt[pr][:, :], in_=d_x[lt * 128:(lt + 1) * 128, :]).then_inc(sL.h, 16))
                sL.n += 16
                vl = sL.n

                q.scalar.append(W(sL, vl))
                va = ln_stats(q, sA, sV, (lambda pr=pr: xt[pr][:, :]), scr, c1, 1e-5)
                q.vector.append(W(sA, va))
                if lt >= 2:
                    q.vector.append(W(sP, sP.n))
                q.vector.append(lambda e, pr=pr: e.tensor_scalar(
                    out=xln[pr][:, :], in0=xt[pr][:, :], scalar1=c1[:, 2:3], scalar2=c1[:, 7:8],
                    op0=ALU.subtract, op1=ALU.mult).then_inc(sV.h, 1))
                norm_done[lt] = sV.inc()

                import os as _os
                if _os.environ.get("K_PH0B_CUT") == "1":
                    continue
                _noevac = _os.environ.get("K_PH0B_NOEVAC") == "1"
                for g in range(4):
                    gpr = gi % 2
                    q.tensor.append(W(sV, norm_done[lt]))
                    if gi >= 2:
                        q.tensor.append(W(sV2, evac_done[gi - 2]))
                    for hb in range(4):
                        ht = g * 4 + hb
                        q.tensor.append(lambda e, pr=pr, gpr=gpr, ht=ht, hb=hb: (
                            e.transpose(pt[gpr][:, hb * 128:(hb + 1) * 128],
                                        xln[pr][:, ht * 128:(ht + 1) * 128],
                                        sb_id[:, :]).then_inc(sP.h, 1) if hb == 3 else
                            e.transpose(pt[gpr][:, hb * 128:(hb + 1) * 128],
                                        xln[pr][:, ht * 128:(ht + 1) * 128], sb_id[:, :])))
                    vp = sP.inc()
                    q.vector.append(W(sP, vp))
                    if _noevac:
                        q.vector.append(lambda e: e.engine_nop().then_inc(sV2.h, 1))
                    else:
                        for hb in range(4):
                            ht = g * 4 + hb
                            last = hb == 3
                            q.vector.append(lambda e, gpr=gpr, hb=hb, ht=ht, lt=lt, last=last: (
                                e.tensor_copy(out=sb_xlnT[:, ht, lt * 128:(lt + 1) * 128],
                                              in_=pt[gpr][:, hb * 128:(hb + 1) * 128]).then_inc(sV2.h, 1)
                                if last else
                                e.tensor_copy(out=sb_xlnT[:, ht, lt * 128:(lt + 1) * 128],
                                              in_=pt[gpr][:, hb * 128:(hb + 1) * 128])))
                    evac_done[gi] = sV2.inc()
                    gi += 1
            if "xlnT" in taps:
                q.sync.append(W(sV2, sV2.n))
                q.sync.append(lambda e: e.dma_start(
                    out=d_taps["xlnT"], in_=sb_xlnT[:, :, :].rearrange("p a b -> p (a b)")).then_inc(sSt.h, 16))
                sSt.n += 16
                q.sync.append(W(sSt, sSt.n))
            emit_block(nc, q)

        if _skip("ph1"):
            return _stop()
        q = Q()
        with ExitStack() as ph:
            winb = [ph.enter_context(nc.sbuf_tensor(f"t_win{i}", [128, NKT * 128], BF16)) for i in range(2)]
            hpad = ph.enter_context(nc.sbuf_tensor("t_hpad", [128, L + 8], BF16))
            diag = ph.enter_context(nc.sbuf_tensor("t_diag", [128, 4 * 128], BF16))
            ubuf = ph.enter_context(nc.sbuf_tensor("t_ubuf", [128, L], BF16))
            gbuf = ph.enter_context(nc.sbuf_tensor("t_gbuf", [128, L], BF16))
            pps = ph.enter_context(nc.psum_tensor("t_pps", [128, L], F32))
            cps = ph.enter_context(nc.psum_tensor("t_cps", [128, L], F32))

            q.vector.append(lambda e: e.memset(hpad[:, 0:8], 0.0).then_inc(sV.h, 1))
            sV.inc()

            proj_done = {}
            for et in range(64):
                pr = et % 2
                hidden = et < NET
                if et >= 2:
                    q.gpsimd.append(W(sP, proj_done[et - 2]))
                q.gpsimd.append(lambda e, et=et, pr=pr: e.dma_start(
                    out=winb[pr][:, :].rearrange("p (t e) -> p t e", e=128),
                    in_=d_win[:, et * 128:(et + 1) * 128].rearrange("(t p) e -> p t e", p=128)).then_inc(sGd.h, 16))
                sGd.n += 16
                vg = sGd.n
                q.gpsimd.append(W(sGd, vg))

                q.tensor.append(W(sGd, vg))
                if et >= 1:
                    q.tensor.append(W(sA, sA.n))
                for lt in range(4):
                    for kt in range(NKT):
                        fin = (lt == 3 and kt == NKT - 1)
                        q.tensor.append(lambda e, pr=pr, lt=lt, kt=kt, fin=fin: (
                            e.matmul(pps[:, lt * 512:(lt + 1) * 512],
                                     winb[pr][:, kt * 128:(kt + 1) * 128],
                                     sb_xlnT[:, kt, lt * 512:(lt + 1) * 512],
                                     start=(kt == 0), stop=(kt == NKT - 1)
                                     ).then_inc(sP.h, 1) if fin else
                            e.matmul(pps[:, lt * 512:(lt + 1) * 512],
                                     winb[pr][:, kt * 128:(kt + 1) * 128],
                                     sb_xlnT[:, kt, lt * 512:(lt + 1) * 512],
                                     start=(kt == 0), stop=(kt == NKT - 1))))
                proj_done[et] = sP.inc()

                if hidden:
                    q.scalar.append(W(sP, proj_done[et]))
                    q.scalar.append(lambda e: e.activation(
                        out=hpad[:, 8:8 + L], in_=pps[:, :], func=AF.Copy).then_inc(sA.h, 1))
                    vh = sA.inc()
                    q.vector.append(W(sP, sP.n))
                    q.vector.append(lambda e, et=et: [e.tensor_scalar_mul(
                        diag[:, k * 128:(k + 1) * 128], sb_id[:, :],
                        sb_ck[:, et * 4 + k:et * 4 + k + 1]) for k in range(4)][-1].then_inc(sV.h, 1))
                    vd = sV.inc()
                    q.tensor.append(W(sA, vh))
                    q.tensor.append(W(sV, vd))
                    for lt in range(4):
                        for k in range(4):
                            fin = (lt == 3 and k == 3)
                            q.tensor.append(lambda e, lt=lt, k=k, fin=fin: (
                                e.matmul(cps[:, lt * 512:(lt + 1) * 512],
                                         diag[:, k * 128:(k + 1) * 128],
                                         hpad[:, 5 + k + lt * 512: 5 + k + lt * 512 + 512],
                                         start=(k == 0), stop=(k == 3)).then_inc(sP.h, 1) if fin else
                                e.matmul(cps[:, lt * 512:(lt + 1) * 512],
                                         diag[:, k * 128:(k + 1) * 128],
                                         hpad[:, 5 + k + lt * 512: 5 + k + lt * 512 + 512],
                                         start=(k == 0), stop=(k == 3))))
                    vconv = sP.inc()
                    q.scalar.append(W(sP, vconv))
                    if et >= 1:
                        q.scalar.append(W(sSt, sSt.n))
                    q.scalar.append(lambda e, et=et: e.activation(
                        out=ubuf[:, :], in_=cps[:, :], func=AF.Silu,
                        bias=sb_cb[:, et:et + 1]).then_inc(sA.h, 1))
                    vu = sA.inc()
                    q.sync.append(W(sA, vu))
                    q.sync.append(lambda e, et=et: e.dma_start(
                        out=d_u[et * 128:(et + 1) * 128, :], in_=ubuf[:, :]).then_inc(sSt.h, 16))
                    sSt.n += 16
                else:
                    ge = et - NET
                    q.scalar.append(W(sP, proj_done[et]))
                    if et >= NET + 1:
                        q.scalar.append(W(sSt, sSt.n))
                    q.scalar.append(lambda e: e.activation(
                        out=gbuf[:, :], in_=pps[:, :], func=AF.Silu).then_inc(sA.h, 1))
                    vgate = sA.inc()
                    q.sync.append(W(sA, vgate))
                    q.sync.append(lambda e, ge=ge: e.dma_start(
                        out=d_gate[ge * 128:(ge + 1) * 128, :], in_=gbuf[:, :]).then_inc(sSt.h, 16))
                    sSt.n += 16
            q.sync.append(W(sSt, sSt.n))
            emit_block(nc, q)
        es1.close(); open_stacks.remove(es1)

        if "u" in taps or "gate" in taps:
            q = Q()
            if "u" in taps:
                q.sync.append(lambda e: e.dma_start(out=d_taps["u"], in_=d_u).then_inc(sSt.h, 16))
                sSt.n += 16
            if "gate" in taps:
                q.sync.append(lambda e: e.dma_start(out=d_taps["gate"], in_=d_gate).then_inc(sSt.h, 16))
                sSt.n += 16
            q.sync.append(W(sSt, sSt.n))
            emit_block(nc, q)

        if _skip("ph2"):
            return _stop()
        q = Q()
        with ExitStack() as ph:
            ust = [ph.enter_context(nc.sbuf_tensor(f"t_ust{i}", [128, NET * 512], BF16)) for i in range(2)]
            bcf = ph.enter_context(nc.sbuf_tensor("t_bcf", [32, L], BF16))
            dps = [ph.enter_context(nc.psum_tensor(f"t_dps{i}", [128, 512], F32)) for i in range(2)]
            bps = [ph.enter_context(nc.psum_tensor(f"t_bps{i}", [32, 512], F32)) for i in range(2)]

            evdt = {}
            evbc = {}
            mm2_done = {}
            for lt in range(4):
                pr = lt % 2
                if lt >= 2:
                    q.sync.append(W(sP, mm2_done[lt - 2]))
                q.sync.append(lambda e, lt=lt, pr=pr: e.dma_start(
                    out=ust[pr][:, :].rearrange("p (t m) -> p t m", m=512),
                    in_=d_u[:, lt * 512:(lt + 1) * 512].rearrange("(t p) m -> p t m", p=128)).then_inc(sL.h, 16))
                sL.n += 16
                vl = sL.n
                q.tensor.append(W(sL, vl))
                if lt >= 2:
                    q.tensor.append(W(sA, evdt[lt - 2]))
                    q.tensor.append(W(sV, evbc[lt - 2]))
                for kt in range(NET):
                    q.tensor.append(lambda e, pr=pr, kt=kt: e.matmul(
                        dps[pr][:, :], sb_wx[:, kt * 160:kt * 160 + 128],
                        ust[pr][:, kt * 512:(kt + 1) * 512],
                        start=(kt == 0), stop=(kt == NET - 1)))
                for kt in range(NET):
                    fin = kt == NET - 1
                    q.tensor.append(lambda e, pr=pr, kt=kt, fin=fin: (
                        e.matmul(bps[pr][:, :], sb_wx[:, kt * 160 + 128:kt * 160 + 160],
                                 ust[pr][:, kt * 512:(kt + 1) * 512],
                                 start=(kt == 0), stop=(kt == NET - 1)).then_inc(sP.h, 1) if fin else
                        e.matmul(bps[pr][:, :], sb_wx[:, kt * 160 + 128:kt * 160 + 160],
                                 ust[pr][:, kt * 512:(kt + 1) * 512],
                                 start=(kt == 0), stop=(kt == NET - 1))))
                vp = sP.inc()
                mm2_done[lt] = vp
                q.scalar.append(W(sP, vp))
                q.scalar.append(lambda e, lt=lt, pr=pr: e.activation(
                    out=sb_dtT[:, lt * 512:(lt + 1) * 512], in_=dps[pr][:, :],
                    func=AF.Copy).then_inc(sA.h, 1))
                evdt[lt] = sA.inc()
                q.vector.append(W(sP, vp))
                q.vector.append(lambda e, lt=lt, pr=pr: e.tensor_copy(
                    out=bcf[:, lt * 512:(lt + 1) * 512], in_=bps[pr][:, :]).then_inc(sV.h, 1))
                evbc[lt] = sV.inc()
            q.sync.append(W(sV, sV.n))
            q.sync.append(lambda e: e.dma_start(out=d_bcbf, in_=bcf[:, :]).then_inc(sSt.h, 16))
            sSt.n += 16
            if "dtT" in taps:
                q.sync.append(W(sA, sA.n))
                q.sync.append(lambda e: e.dma_start(out=d_taps["dtT"], in_=sb_dtT[:, :]).then_inc(sSt.h, 16))
                sSt.n += 16
            if "bc" in taps:
                q.sync.append(lambda e: e.dma_start(out=d_taps["bc"], in_=bcf[:, :]).then_inc(sSt.h, 16))
                sSt.n += 16
            q.sync.append(W(sSt, sSt.n))
            emit_block(nc, q)

        if _skip("ph3"):
            return _stop()
        for half in range(2):
            q = Q()
            with ExitStack() as ph:
                bcb = ph.enter_context(nc.sbuf_tensor(f"t_bcb{half}", [128, 16, L], BF16))
                t1 = ph.enter_context(nc.sbuf_tensor(f"t_t1{half}", [128, L], F32))
                t2 = ph.enter_context(nc.sbuf_tensor(f"t_t2{half}", [128, L], F32))
                dts = ph.enter_context(nc.sbuf_tensor(f"t_dts{half}", [128, L], F32))
                du = ph.enter_context(nc.sbuf_tensor(f"t_du{half}", [128, L], BF16))
                uet = [ph.enter_context(nc.sbuf_tensor(f"t_uet{half}_{i}", [128, L], BF16)) for i in range(2)]
                dab = [ph.enter_context(nc.sbuf_tensor(f"t_dab{half}_{i}", [128, L], F32)) for i in range(2)]
                dbu = ph.enter_context(nc.sbuf_tensor(f"t_dbu{half}", [128, L], BF16))
                hh = ph.enter_context(nc.sbuf_tensor(f"t_hh{half}", [128, L], BF16))
                zb = [ph.enter_context(nc.sbuf_tensor(f"t_z{half}_{i}", [128, L], BF16)) for i in range(4)]
                yt = ph.enter_context(nc.sbuf_tensor(f"t_yt{half}", [128, L], F32))
                yg = ph.enter_context(nc.sbuf_tensor(f"t_yg{half}", [128, L], BF16))
                if half == 1:
                    get_ = [ph.enter_context(nc.sbuf_tensor(f"t_get{half}_{i}", [128, L], BF16)) for i in range(2)]
                    yh0b = [ph.enter_context(nc.sbuf_tensor(f"t_yh0{half}_{i}", [128, L], BF16)) for i in range(2)]
                sps = [ph.enter_context(nc.psum_tensor(f"t_sps{half}_{i}", [128, L], F32)) for i in range(2)]

                for i in range(8):
                    q.sync.append(lambda e, i=i, half=half: e.dma_start(
                        out=bcb[:, i, :],
                        in_=d_bcbf[half * 8 + i: half * 8 + i + 1, :].to_broadcast((128, L))).then_inc(sL.h, 16))
                    sL.n += 16
                    q.sync.append(lambda e, i=i, half=half: e.dma_start(
                        out=bcb[:, 8 + i, :],
                        in_=d_bcbf[16 + half * 8 + i: 16 + half * 8 + i + 1, :].to_broadcast((128, L))).then_inc(sL.h, 16))
                    sL.n += 16
                vbc = sL.n

                scan_base = sV2.n
                sidx = 0
                psum_free = {}
                yg_spill = {}
                for et in range(NET):
                    pr = et % 2
                    if et >= 2:
                        q.sync.append(W(sV, sV.n))
                    q.sync.append(lambda e, et=et, pr=pr: e.dma_start(
                        out=uet[pr][:, :], in_=d_u[et * 128:(et + 1) * 128, :]).then_inc(sL.h, 16))
                    sL.n += 16
                    if half == 1:
                        q.sync.append(lambda e, et=et, pr=pr: e.dma_start(
                            out=get_[pr][:, :], in_=d_gate[et * 128:(et + 1) * 128, :]).then_inc(sL.h, 16))
                        sL.n += 16
                        q.sync.append(lambda e, et=et, pr=pr: e.dma_start(
                            out=yh0b[pr][:, :], in_=d_yh0[et * 128:(et + 1) * 128, :]).then_inc(sL.h, 16))
                        sL.n += 16
                    vload = sL.n

                    if et >= 2:
                        q.tensor.append(W(sA, psum_free[et - 2]))
                    for lt in range(4):
                        fin = lt == 3
                        q.tensor.append(lambda e, pr=pr, et=et, lt=lt, fin=fin: (
                            e.matmul(sps[pr][:, lt * 512:(lt + 1) * 512],
                                     sb_wdt[:, et * 128:(et + 1) * 128],
                                     sb_dtT[:, lt * 512:(lt + 1) * 512],
                                     start=True, stop=True).then_inc(sP.h, 1) if fin else
                            e.matmul(sps[pr][:, lt * 512:(lt + 1) * 512],
                                     sb_wdt[:, et * 128:(et + 1) * 128],
                                     sb_dtT[:, lt * 512:(lt + 1) * 512],
                                     start=True, stop=True)))
                    vdts = sP.inc()

                    q.scalar.append(W(sP, vdts))
                    if et >= 1:
                        q.scalar.append(W(sV, sV.n))
                    q.scalar.append(lambda e, pr=pr, et=et: e.activation(
                        out=t1[:, :], in_=sps[pr][:, :], func=AF.Abs, bias=sb_bdt[:, et:et + 1]))
                    q.scalar.append(lambda e, pr=pr, et=et: e.activation(
                        out=t2[:, :], in_=sps[pr][:, :], func=AF.Relu,
                        bias=sb_bdt[:, et:et + 1]).then_inc(sA.h, 1))
                    psum_free[et] = sA.inc()
                    q.scalar.append(lambda e: e.activation(out=t1[:, :], in_=t1[:, :], func=AF.Exp, scale=-1.0))
                    q.scalar.append(lambda e: e.activation(out=t1[:, :], in_=t1[:, :], func=AF.Ln,
                                                           bias=1.0).then_inc(sA.h, 1))
                    vsp = sA.inc()

                    q.vector.append(W(sA, vsp))
                    q.vector.append(W(sL, vload))
                    q.vector.append(lambda e: e.tensor_add(dts[:, :], t1[:, :], t2[:, :]).then_inc(sV.h, 1))
                    vdtsd = sV.inc()
                    q.vector.append(lambda e, pr=pr: e.tensor_mul(du[:, :], dts[:, :], uet[pr][:, :]))

                    for nn in range(8):
                        n = half * 8 + nn
                        npr = sidx % 2
                        q.scalar.append(W(sV, vdtsd))
                        if sidx >= 2:
                            q.scalar.append(W(sV2, scan_base + sidx - 1))
                        q.scalar.append(lambda e, npr=npr, et=et, n=n: e.activation(
                            out=dab[npr][:, :], in_=dts[:, :], func=AF.Exp,
                            scale=sb_aneg[:, et * NST + n:et * NST + n + 1]).then_inc(sA2.h, 1))
                        vda = sA2.inc()
                        q.vector.append(W(sA2, vda))
                        if et == 0 and nn == 0:
                            q.vector.append(W(sL, vbc))
                        q.vector.append(lambda e, nn=nn: e.tensor_mul(dbu[:, :], du[:, :], bcb[:, nn, :]))
                        q.vector.append(lambda e, npr=npr: e.tensor_tensor_scan(
                            out=hh[:, :], data0=dab[npr][:, :], data1=dbu[:, :],
                            initial=0.0, op0=ALU.mult, op1=ALU.add).then_inc(sV2.h, 1))
                        sV2.inc()
                        sidx += 1
                        q.vector.append(lambda e, nn=nn: e.tensor_mul(zb[nn % 4][:, :], hh[:, :], bcb[:, 8 + nn, :]))
                        if nn % 2 == 1:
                            q.vector.append(lambda e, nn=nn: e.tensor_add(
                                zb[nn - 1 if nn < 4 else nn - 5][:, :],
                                zb[(nn - 1) % 4][:, :], zb[nn % 4][:, :]))
                        if nn == 3:
                            q.vector.append(lambda e: e.tensor_add(yt[:, :], zb[0][:, :], zb[2][:, :]))
                        if nn == 7:
                            q.vector.append(lambda e: e.tensor_add(zb[0][:, :], zb[0][:, :], zb[2][:, :]))
                            q.vector.append(lambda e: e.tensor_add(yt[:, :], yt[:, :], zb[0][:, :]))

                    if half == 0:
                        if et >= 1:
                            q.vector.append(W(sSt, yg_spill[et - 1]))
                        q.vector.append(lambda e: e.tensor_copy(out=yg[:, :], in_=yt[:, :]).then_inc(sV.h, 1))
                        vyh = sV.inc()
                        q.sync.append(W(sV, vyh))
                        q.sync.append(lambda e, et=et: e.dma_start(
                            out=d_yh0[et * 128:(et + 1) * 128, :], in_=yg[:, :]).then_inc(sSt.h, 16))
                        sSt.n += 16
                        yg_spill[et] = sSt.n
                    else:
                        q.vector.append(lambda e, pr=pr: e.tensor_add(yt[:, :], yt[:, :], yh0b[pr][:, :]))
                        q.vector.append(lambda e, pr=pr, et=et: e.scalar_tensor_tensor(
                            out=yt[:, :], in0=uet[pr][:, :], scalar=sb_D[:, et:et + 1], in1=yt[:, :],
                            op0=ALU.mult, op1=ALU.add))
                        if et >= 1:
                            q.vector.append(W(sSt, yg_spill[et - 1]))
                        q.vector.append(lambda e, pr=pr: e.tensor_mul(
                            yg[:, :], yt[:, :], get_[pr][:, :]).then_inc(sV.h, 1))
                        vygd = sV.inc()
                        q.sync.append(W(sV, vygd))
                        q.sync.append(lambda e, et=et: e.dma_start(
                            out=d_yg[et * 128:(et + 1) * 128, :], in_=yg[:, :]).then_inc(sSt.h, 16))
                        sSt.n += 16
                        yg_spill[et] = sSt.n
                q.sync.append(W(sSt, sSt.n))
                emit_block(nc, q)

        if "yg" in taps:
            q = Q()
            q.sync.append(lambda e: e.dma_start(out=d_taps["yg"], in_=d_yg).then_inc(sSt.h, 16))
            sSt.n += 16
            q.sync.append(W(sSt, sSt.n))
            emit_block(nc, q)

        if _skip("ph4"):
            return _stop()
        q = Q()
        with ExitStack() as ph:
            wob = [ph.enter_context(nc.sbuf_tensor(f"t_wob{i}", [128, NET * 512], BF16)) for i in range(2)]
            yst8 = ph.enter_context(nc.sbuf_tensor("t_yst8", [128, 8, NET * 128], BF16))
            aev = [ph.enter_context(nc.sbuf_tensor(f"t_aev{i}", [128, 512], F32)) for i in range(2)]
            ops_ = [ph.enter_context(nc.psum_tensor(f"t_ops{i}", [128, 512], F32)) for i in range(2)]

            ev_done = {}
            mm_done = {}
            gl = 0
            q.sync.append(W(sSt, sSt.n))
            for hf in range(2):
                if hf > 0:
                    q.sync.append(W(sP, mm_done[gl - 1]))
                for lt8 in range(8):
                    q.sync.append(lambda e, lt8=lt8, hf=hf: e.dma_start(
                        out=yst8[:, lt8, :].rearrange("p (t m) -> p t m", m=128),
                        in_=d_yg[:, (hf * 8 + lt8) * 128:(hf * 8 + lt8 + 1) * 128].rearrange(
                            "(t p) m -> p t m", p=128)).then_inc(sL.h, 16))
                    sL.n += 16
                vy = sL.n
                for htl in range(4):
                    g6 = hf * 4 + htl
                    prw = g6 % 2
                    if g6 >= 2:
                        q.gpsimd.append(W(sP, mm_done[g6 * 8 - 9]))
                    q.gpsimd.append(lambda e, htl=htl, prw=prw: e.dma_start(
                        out=wob[prw][:, :].rearrange("p (t h) -> p t h", h=512),
                        in_=d_wout[:, htl * 512:(htl + 1) * 512].rearrange(
                            "(t p) h -> p t h", p=128)).then_inc(sGd.h, 16))
                    sGd.n += 16
                    vw = sGd.n
                    for lt8 in range(8):
                        lt = hf * 8 + lt8
                        pr = gl % 2
                        q.tensor.append(W(sL, vy))
                        q.tensor.append(W(sGd, vw))
                        if gl >= 2:
                            q.tensor.append(W(sA, ev_done[gl - 2]))
                        for kt in range(NET):
                            fin = kt == NET - 1
                            q.tensor.append(lambda e, pr=pr, prw=prw, lt8=lt8, kt=kt, fin=fin: (
                                e.matmul(ops_[pr][:, :], yst8[:, lt8, kt * 128:(kt + 1) * 128],
                                         wob[prw][:, kt * 512:(kt + 1) * 512],
                                         start=(kt == 0), stop=(kt == NET - 1)).then_inc(sP.h, 1) if fin else
                                e.matmul(ops_[pr][:, :], yst8[:, lt8, kt * 128:(kt + 1) * 128],
                                         wob[prw][:, kt * 512:(kt + 1) * 512],
                                         start=(kt == 0), stop=(kt == NET - 1))))
                        vp = sP.inc()
                        mm_done[gl] = vp
                        q.scalar.append(W(sP, vp))
                        if gl >= 2:
                            q.scalar.append(W(sSt, sSt.n))
                        q.scalar.append(lambda e, pr=pr: e.activation(
                            out=aev[pr][:, :], in_=ops_[pr][:, :], func=AF.Copy).then_inc(sA.h, 1))
                        ev_done[gl] = sA.inc()
                        q.sync.append(W(sA, ev_done[gl]))
                        q.sync.append(lambda e, lt=lt, htl=htl, pr=pr: e.dma_start(
                            out=d_attn[lt * 128:(lt + 1) * 128, htl * 512:(htl + 1) * 512],
                            in_=aev[pr][:, :]).then_inc(sSt.h, 16))
                        sSt.n += 16
                        gl += 1
            q.sync.append(W(sSt, sSt.n))
            if "attn" in taps:
                q.sync.append(lambda e: e.dma_start(out=d_taps["attn"], in_=d_attn).then_inc(sSt.h, 16))
                sSt.n += 16
                q.sync.append(W(sSt, sSt.n))
            emit_block(nc, q)

        if _skip("ph5"):
            return _stop()
        es3 = ExitStack(); open_stacks.append(es3)
        sb_xTp = es3.enter_context(nc.sbuf_tensor("p_xTp", [128, 16, L], BF16))
        es2 = ExitStack(); open_stacks.append(es2)
        sb_x1ln = es2.enter_context(nc.sbuf_tensor("p_x1ln", [128, NLT, H], BF16))
        q = Q()
        with ExitStack() as ph:
            at = [ph.enter_context(nc.sbuf_tensor("t_at0", [128, H], F32))]
            xt = [ph.enter_context(nc.sbuf_tensor("t_xt50", [128, H], F32))]
            x1t = [ph.enter_context(nc.sbuf_tensor(f"t_x1t{i}", [128, H], F32)) for i in range(2)]
            scr = ph.enter_context(nc.sbuf_tensor("t_scr5", [128, H], F32))
            c1 = ph.enter_context(nc.sbuf_tensor("t_c15", [128, 8], F32))

            x1ln_done = {}
            x1_done = {}
            for lt in range(NLT):
                pr = lt % 2
                if lt >= 2:
                    q.sync.append(W(sV, x1ln_done[lt - 2]))
                if lt >= 1:
                    q.sync.append(W(sV, x1_done[lt - 1]))
                q.sync.append(lambda e, lt=lt: e.dma_start(
                    out=at[0][:, :], in_=d_attn[lt * 128:(lt + 1) * 128, :]).then_inc(sL.h, 16))
                sL.n += 16
                q.sync.append(lambda e, lt=lt: e.dma_start(
                    out=xt[0][:, :], in_=d_x[lt * 128:(lt + 1) * 128, :]).then_inc(sL.h, 16))
                sL.n += 16
                vl = sL.n

                q.scalar.append(W(sL, vl))
                va = ln_stats(q, sA, sV, (lambda: at[0][:, :]), scr, c1, 1e-5)
                q.vector.append(W(sA, va))
                q.vector.append(lambda e: e.tensor_scalar(
                    out=scr[:, :], in0=at[0][:, :], scalar1=c1[:, 2:3], scalar2=c1[:, 7:8],
                    op0=ALU.subtract, op1=ALU.mult))
                q.vector.append(lambda e, pr=pr: e.tensor_add(
                    x1t[pr][:, :], xt[0][:, :], scr[:, :]).then_inc(sV.h, 1))
                vx1 = sV.inc()
                x1_done[lt] = vx1
                q.sync.append(W(sV, vx1))
                q.sync.append(lambda e, lt=lt, pr=pr: e.dma_start(
                    out=d_x1[lt * 128:(lt + 1) * 128, :], in_=x1t[pr][:, :]).then_inc(sSt.h, 16))
                sSt.n += 16
                q.scalar.append(W(sV, vx1))
                va2 = ln_stats(q, sA, sV, (lambda pr=pr: x1t[pr][:, :]), scr, c1, 1e-5)
                q.vector.append(W(sA, va2))
                q.vector.append(lambda e, pr=pr, lt=lt: e.tensor_scalar(
                    out=sb_x1ln[:, lt, :], in0=x1t[pr][:, :], scalar1=c1[:, 2:3], scalar2=c1[:, 7:8],
                    op0=ALU.subtract, op1=ALU.mult).then_inc(sV.h, 1))
                x1ln_done[lt] = sV.inc()
            q.sync.append(W(sV, sV.n))
            if "x1" in taps:
                q.sync.append(lambda e: e.dma_start(out=d_taps["x1"], in_=d_x1).then_inc(sSt.h, 16))
                sSt.n += 16
                q.sync.append(W(sSt, sSt.n))
            emit_block(nc, q)

        if _skip("ph6a"):
            return _stop()
        q = Q()
        with ExitStack() as ph:
            pt = [ph.enter_context(nc.psum_tensor(f"t_pt6{i}", [128, 1024], BF16)) for i in range(2)]
            ebase = sV2.n
            gidx = 0
            for kk in range(8):
                for jb in range(2):
                    pid = kk * 2 + jb
                    for g in range(4):
                        gpr = gidx % 2
                        if gidx >= 2:
                            q.tensor.append(W(sV2, ebase + gidx - 1))
                        for li in range(4):
                            lt = g * 4 + li
                            fin = li == 3
                            q.tensor.append(lambda e, gpr=gpr, li=li, lt=lt, kk=kk, jb=jb, fin=fin: (
                                e.transpose(pt[gpr][:, li * 128:(li + 1) * 128],
                                            sb_x1ln[:, lt, kk + 1024 * jb: kk + 1024 * jb + 8 * 127 + 1: 8],
                                            sb_id[:, :]).then_inc(sP.h, 1) if fin else
                                e.transpose(pt[gpr][:, li * 128:(li + 1) * 128],
                                            sb_x1ln[:, lt, kk + 1024 * jb: kk + 1024 * jb + 8 * 127 + 1: 8],
                                            sb_id[:, :])))
                        vp = sP.inc()
                        q.vector.append(W(sP, vp))
                        q.vector.append(lambda e, gpr=gpr, pid=pid, g=g: e.tensor_copy(
                            out=sb_xTp[:, pid, g * 512:(g + 1) * 512],
                            in_=pt[gpr][:, 0:512]).then_inc(sV2.h, 1))
                        sV2.inc()
                        gidx += 1
            q.vector.append(W(sV2, sV2.n))
            if "xtp" in taps:
                q.sync.append(W(sV2, sV2.n))
                q.sync.append(lambda e: e.dma_start(
                    out=d_taps["xtp"], in_=sb_xTp[:, :, :].rearrange("p a b -> p (a b)")).then_inc(sSt.h, 16))
                sSt.n += 16
                q.sync.append(W(sSt, sSt.n))
            emit_block(nc, q)
        es2.close(); open_stacks.remove(es2)

        if _skip("ph6b"):
            return _stop()
        q = Q()
        with ExitStack() as ph:
            lgt = [ph.enter_context(nc.sbuf_tensor(f"t_lgt{i}", [128, 16 * 128], BF16)) for i in range(2)]
            lut = [ph.enter_context(nc.sbuf_tensor(f"t_lut{i}", [128, 16 * 128], BF16)) for i in range(2)]
            sg = [ph.enter_context(nc.sbuf_tensor(f"t_sg{i}", [128, 512], BF16)) for i in range(2)]
            su = [ph.enter_context(nc.sbuf_tensor(f"t_su{i}", [128, 512], BF16)) for i in range(2)]
            hrow = ph.enter_context(nc.sbuf_tensor("t_hrow", [128, L], BF16))
            gps = [ph.enter_context(nc.psum_tensor(f"t_gps{i}", [128, 512], F32)) for i in range(2)]
            ups = [ph.enter_context(nc.psum_tensor(f"t_ups{i}", [128, 512], F32)) for i in range(2)]

            ev_done = {}
            mul_done = {}
            spill_done = {}
            mmct_done = {}
            q.sync.append(W(sCC, 1))
            for ct in range(44):
                pr = ct % 2
                if ct >= 2:
                    q.sync.append(W(sP, mmct_done[ct - 2]))
                q.sync.append(lambda e, ct=ct, pr=pr: e.dma_start(
                    out=lgt[pr][:, :].rearrange("p (pid c) -> p pid c", c=128),
                    in_=d_plgu[:, :, ct * 128:(ct + 1) * 128].rearrange("pid p c -> p pid c")).then_inc(sL.h, 16))
                sL.n += 16
                q.sync.append(lambda e, ct=ct, pr=pr: e.dma_start(
                    out=lut[pr][:, :].rearrange("p (pid c) -> p pid c", c=128),
                    in_=d_plgu[:, :, IFF + ct * 128: IFF + (ct + 1) * 128].rearrange("pid p c -> p pid c")).then_inc(sL.h, 16))
                sL.n += 16
                vl = sL.n
                q.tensor.append(W(sL, vl))
                for lt in range(4):
                    lpr = lt % 2
                    gq = ct * 4 + lt
                    if gq >= 2:
                        q.tensor.append(W(sA, ev_done[gq - 2]))
                    for pid in range(16):
                        q.tensor.append(lambda e, pr=pr, lpr=lpr, pid=pid, lt=lt: e.matmul(
                            gps[lpr][:, :], lgt[pr][:, pid * 128:(pid + 1) * 128],
                            sb_xTp[:, pid, lt * 512:(lt + 1) * 512],
                            start=(pid == 0), stop=(pid == 15)))
                    for pid in range(16):
                        fin = pid == 15
                        q.tensor.append(lambda e, pr=pr, lpr=lpr, pid=pid, lt=lt, fin=fin: (
                            e.matmul(ups[lpr][:, :], lut[pr][:, pid * 128:(pid + 1) * 128],
                                     sb_xTp[:, pid, lt * 512:(lt + 1) * 512],
                                     start=(pid == 0), stop=(pid == 15)).then_inc(sP.h, 1) if fin else
                            e.matmul(ups[lpr][:, :], lut[pr][:, pid * 128:(pid + 1) * 128],
                                     sb_xTp[:, pid, lt * 512:(lt + 1) * 512],
                                     start=(pid == 0), stop=(pid == 15))))
                    vp = sP.inc()
                    q.scalar.append(W(sP, vp))
                    if gq >= 2:
                        q.scalar.append(W(sV, mul_done[gq - 2]))
                    q.scalar.append(lambda e, lpr=lpr: e.activation(
                        out=sg[lpr][:, :], in_=gps[lpr][:, :], func=AF.Silu))
                    q.scalar.append(lambda e, lpr=lpr: e.activation(
                        out=su[lpr][:, :], in_=ups[lpr][:, :], func=AF.Copy).then_inc(sA.h, 1))
                    ev_done[gq] = sA.inc()
                    q.vector.append(W(sA, ev_done[gq]))
                    if ct >= 1 and lt == 0:
                        q.vector.append(W(sSt, spill_done[ct - 1]))
                    q.vector.append(lambda e, lt=lt, lpr=lpr: e.tensor_mul(
                        hrow[:, lt * 512:(lt + 1) * 512], sg[lpr][:, :], su[lpr][:, :]).then_inc(sV.h, 1))
                    mul_done[gq] = sV.inc()
                mmct_done[ct] = sP.n
                q.sync.append(W(sV, mul_done[ct * 4 + 3]))
                q.sync.append(lambda e, ct=ct: e.dma_start(
                    out=d_hact[ct * 128:(ct + 1) * 128, :], in_=hrow[:, :]).then_inc(sSt.h, 16))
                sSt.n += 16
                spill_done[ct] = sSt.n
            q.sync.append(W(sSt, sSt.n))
            if "hact" in taps:
                q.sync.append(lambda e: e.dma_start(out=d_taps["hact"], in_=d_hact).then_inc(sSt.h, 16))
                sSt.n += 16
                q.sync.append(W(sSt, sSt.n))
            emit_block(nc, q)
        es3.close(); open_stacks.remove(es3)

        if _skip("ph6c"):
            return _stop()
        q = Q()
        with ExitStack() as ph:
            wdn = [ph.enter_context(nc.sbuf_tensor(f"t_wdn{i}", [128, 44 * 512], BF16)) for i in range(2)]
            hst8 = ph.enter_context(nc.sbuf_tensor("t_hst8", [128, 8, 44 * 128], BF16))
            fev = [ph.enter_context(nc.sbuf_tensor(f"t_fev{i}", [128, 512], F32)) for i in range(2)]
            dps6 = [ph.enter_context(nc.psum_tensor(f"t_dps6{i}", [128, 512], F32)) for i in range(2)]

            ev_done = {}
            mm_done = {}
            gl = 0
            q.sync.append(W(sCC, 2))
            q.sync.append(W(sSt, sSt.n))
            for hf in range(2):
                if hf > 0:
                    q.sync.append(W(sP, mm_done[gl - 1]))
                for lt8 in range(8):
                    q.sync.append(lambda e, lt8=lt8, hf=hf: e.dma_start(
                        out=hst8[:, lt8, :].rearrange("p (t m) -> p t m", m=128),
                        in_=d_hact[:, (hf * 8 + lt8) * 128:(hf * 8 + lt8 + 1) * 128].rearrange(
                            "(t p) m -> p t m", p=128)).then_inc(sL.h, 16))
                    sL.n += 16
                vh = sL.n
                for htl in range(4):
                    g6 = hf * 4 + htl
                    prw = g6 % 2
                    if g6 >= 2:
                        q.gpsimd.append(W(sP, mm_done[g6 * 8 - 9]))
                    q.gpsimd.append(lambda e, htl=htl, prw=prw: e.dma_start(
                        out=wdn[prw][:, :].rearrange("p (t h) -> p t h", h=512),
                        in_=d_pldn[:, htl * 512:(htl + 1) * 512].rearrange(
                            "(t p) h -> p t h", p=128)).then_inc(sGd.h, 16))
                    sGd.n += 16
                    vw = sGd.n
                    for lt8 in range(8):
                        lt = hf * 8 + lt8
                        pr = gl % 2
                        q.tensor.append(W(sL, vh))
                        q.tensor.append(W(sGd, vw))
                        if gl >= 2:
                            q.tensor.append(W(sA, ev_done[gl - 2]))
                        for kt in range(44):
                            fin = kt == 43
                            q.tensor.append(lambda e, pr=pr, prw=prw, lt8=lt8, kt=kt, fin=fin: (
                                e.matmul(dps6[pr][:, :], hst8[:, lt8, kt * 128:(kt + 1) * 128],
                                         wdn[prw][:, kt * 512:(kt + 1) * 512],
                                         start=(kt == 0), stop=(kt == 43)).then_inc(sP.h, 1) if fin else
                                e.matmul(dps6[pr][:, :], hst8[:, lt8, kt * 128:(kt + 1) * 128],
                                         wdn[prw][:, kt * 512:(kt + 1) * 512],
                                         start=(kt == 0), stop=(kt == 43))))
                        vp = sP.inc()
                        mm_done[gl] = vp
                        q.scalar.append(W(sP, vp))
                        if gl >= 2:
                            q.scalar.append(W(sSt, sSt.n))
                        q.scalar.append(lambda e, pr=pr: e.activation(
                            out=fev[pr][:, :], in_=dps6[pr][:, :], func=AF.Copy).then_inc(sA.h, 1))
                        ev_done[gl] = sA.inc()
                        q.sync.append(W(sA, ev_done[gl]))
                        q.sync.append(lambda e, lt=lt, htl=htl, pr=pr: e.dma_start(
                            out=d_ffn[lt * 128:(lt + 1) * 128, htl * 512:(htl + 1) * 512],
                            in_=fev[pr][:, :]).then_inc(sSt.h, 16))
                        sSt.n += 16
                        gl += 1
            q.sync.append(W(sSt, sSt.n))
            if "ffn" in taps:
                q.sync.append(lambda e: e.dma_start(out=d_taps["ffn"], in_=d_ffn).then_inc(sSt.h, 16))
                sSt.n += 16
                q.sync.append(W(sSt, sSt.n))
            emit_block(nc, q)

        if _skip("ph7"):
            return _stop()
        q = Q()
        with ExitStack() as ph:
            ft = [ph.enter_context(nc.sbuf_tensor(f"t_ft{i}", [128, H], F32)) for i in range(2)]
            x1t = [ph.enter_context(nc.sbuf_tensor(f"t_x1t7{i}", [128, H], F32)) for i in range(2)]
            vt = ph.enter_context(nc.sbuf_tensor("t_vt", [128, H], F32))
            ot = [ph.enter_context(nc.sbuf_tensor(f"t_ot{i}", [128, H], F32)) for i in range(2)]
            scr = ph.enter_context(nc.sbuf_tensor("t_scr7", [128, H], F32))
            c1 = ph.enter_context(nc.sbuf_tensor("t_c17", [128, 8], F32))

            out_done = {}
            for lt in range(NLT):
                pr = lt % 2
                if lt >= 2:
                    q.sync.append(W(sV, out_done[lt - 2]))
                q.sync.append(lambda e, lt=lt, pr=pr: e.dma_start(
                    out=ft[pr][:, :], in_=d_ffn[lt * 128:(lt + 1) * 128, :]).then_inc(sL.h, 16))
                sL.n += 16
                q.sync.append(lambda e, lt=lt, pr=pr: e.dma_start(
                    out=x1t[pr][:, :], in_=d_x1[lt * 128:(lt + 1) * 128, :]).then_inc(sL.h, 16))
                sL.n += 16
                vl = sL.n
                q.scalar.append(W(sL, vl))
                q.scalar.append(lambda e, pr=pr: e.activation(
                    out=scr[:, :], in_=ft[pr][:, :], func=AF.Square, accum_out=c1[:, 0:1]).then_inc(sA.h, 1))
                va = sA.inc()
                q.vector.append(W(sA, va))
                q.vector.append(lambda e: e.tensor_scalar(
                    out=c1[:, 1:2], in0=c1[:, 0:1], scalar1=1.0 / H, scalar2=1e-6,
                    op0=ALU.mult, op1=ALU.add).then_inc(sV.h, 1))
                vv = sV.inc()
                q.scalar.append(W(sV, vv))
                q.scalar.append(lambda e: e.activation(out=c1[:, 2:3], in_=c1[:, 1:2], func=AF.Ln).then_inc(sA.h, 1))
                vln = sA.inc()
                q.scalar.append(W(sA, vln))
                q.scalar.append(lambda e: e.activation(out=c1[:, 3:4], in_=c1[:, 2:3], func=AF.Exp,
                                                       scale=-0.5).then_inc(sA.h, 1))
                va2 = sA.inc()
                q.vector.append(W(sA, va2))
                q.vector.append(lambda e, pr=pr: e.tensor_scalar_mul(
                    vt[:, :], ft[pr][:, :], c1[:, 3:4]).then_inc(sV.h, 1))
                vv2 = sV.inc()
                q.scalar.append(W(sV, vv2))
                va3 = ln_stats(q, sA, sV, (lambda: vt[:, :]), scr, c1, 1e-5)
                q.vector.append(W(sA, va3))
                q.vector.append(lambda e: e.tensor_scalar(
                    out=scr[:, :], in0=vt[:, :], scalar1=c1[:, 2:3], scalar2=c1[:, 7:8],
                    op0=ALU.subtract, op1=ALU.mult))
                q.vector.append(lambda e, pr=pr: e.tensor_add(
                    ot[pr][:, :], x1t[pr][:, :], scr[:, :]).then_inc(sV.h, 1))
                out_done[lt] = sV.inc()
                q.sync.append(W(sV, out_done[lt]))
                q.sync.append(lambda e, lt=lt, pr=pr: e.dma_start(
                    out=d_out[lt * 128:(lt + 1) * 128, :], in_=ot[pr][:, :]).then_inc(sSt.h, 16))
                sSt.n += 16
            q.sync.append(W(sSt, sSt.n))
            emit_block(nc, q)

    return nc


def _prep_inputs(inputs):
    x = np.ascontiguousarray(np.asarray(inputs["L0_attn_input_states"], np.float32))
    common = dict(
        w_in=np.ascontiguousarray(np.asarray(inputs["L0_attn_W_in"], np.float32)),
        conv_k=np.ascontiguousarray(np.asarray(inputs["L0_attn_conv_kernel"], np.float32).reshape(E, 4)),
        conv_b=np.ascontiguousarray(np.asarray(inputs["L0_attn_conv_bias"], np.float32).reshape(NET, 128).T),
        w_x=np.ascontiguousarray(np.asarray(inputs["L0_attn_W_x"], np.float32)),
        w_dt=np.ascontiguousarray(np.asarray(inputs["L0_attn_W_dt"], np.float32)),
        b_dt=np.ascontiguousarray(np.asarray(inputs["L0_attn_b_dt"], np.float32).reshape(NET, 128).T),
        w_out=np.ascontiguousarray(np.asarray(inputs["L0_attn_W_out"], np.float32)),
        a_log=np.ascontiguousarray(np.asarray(inputs["L0_attn_A_log"], np.float32)),
        dvec=np.ascontiguousarray(np.asarray(inputs["L0_attn_D"], np.float32).reshape(NET, 128).T),

        s_gu=np.ascontiguousarray(np.asarray(inputs["L0_ffn_gate_up_scales"], np.float16)[:, _SGU_PERM]),

        s_dn=np.ascontiguousarray(np.asarray(inputs["L0_ffn_down_scales"], np.float16)),
        ident=np.eye(128, dtype=np.float32),
    )
    pgu_t = np.ascontiguousarray(np.asarray(inputs["L0_ffn_gate_up_packed"], np.int32).T)
    pdn_t = np.ascontiguousarray(np.asarray(inputs["L0_ffn_down_packed"], np.int32).T)
    pgu16 = pgu_t.view(np.int16)
    pdn16 = pdn_t.view(np.int16)
    in_maps = []
    for b in range(8):
        m = dict(common)
        m["x"] = np.ascontiguousarray(x[b])
        h16 = b // 4
        kpos = b % 4
        m["pgu_h"] = np.ascontiguousarray(pgu16[:, h16::2][:, _SGU_PERM])
        m["pdn_h"] = np.ascontiguousarray(pdn16[:, h16::2])
        sh = np.empty((128, 2), np.int32)
        sh[:, 0] = 4 * kpos
        sh[:, 1] = 12 - 4 * kpos
        m["dqsh"] = sh
        in_maps.append(m)
    return in_maps


def kernel(**inputs):
    in_maps = _prep_inputs(inputs)
    nc = build_program(taps=DEBUG_TAPS)
    res = run_bass_kernel_spmd(nc, in_maps, core_ids=list(range(8)))
    out = np.stack([np.asarray(r["out"], np.float32) for r in res.results], axis=0)
    if DEBUG_TAPS:
        kernel.last_results = res
    return out
